# version 01
# baseline (speedup 1.0000x reference)
import numpy as np
import ml_dtypes
import concourse.bass as bass
import concourse.mybir as mybir
from concourse.bass_utils import run_bass_kernel_spmd
from contextlib import ExitStack

F32 = mybir.dt.float32
BF16 = mybir.dt.bfloat16
F32R = mybir.dt.float32r
I32 = mybir.dt.int32
AF = mybir.ActivationFunctionType
ALU = mybir.AluOpType
AX = mybir.AxisListType
BF = ml_dtypes.bfloat16

D = 2048
DT = D // 128
NCORES = 8
SEQ = 4096
NB = 2
H16 = 16
HD = 128
EPS_LN = 1e-5
EPS_L1 = 1e-6


def _f32(x):
    return np.ascontiguousarray(x, dtype=np.float32)


def img_lhsT(w_in_out, mg_cols=256):
    din, dout = w_in_out.shape
    assert din == D
    n_mg = dout // mg_cols
    v = w_in_out.reshape(DT, 128, n_mg, mg_cols)
    return _f32(v.transpose(2, 0, 1, 3))


def img_rhs(w_in_out, dn_cols=512):
    din, dout = w_in_out.shape
    n_dn = dout // dn_cols
    v = w_in_out.reshape(DT, 128, n_dn, dn_cols)
    return _f32(v.transpose(2, 0, 1, 3))


def img_col(vec):
    return _f32(np.asarray(vec, np.float32).reshape(-1, 128).T)


class Sems:

    def __init__(self, nc, ctx, names):
        self.h = {}
        self.v = {}
        for n in names:
            self.h[n] = ctx.enter_context(nc.semaphore(n))
            self.v[n] = 0

    def inc(self, name, instr, k=1):
        instr.then_inc(self.h[name], k)
        self.v[name] += k
        return self.v[name]

    def val(self, name):
        return self.v[name]

    def __getitem__(self, name):
        return self.h[name]



def build_launch_A():
    TOK = 1024
    NCH = TOK // 512
    nc = bass.Bass("TRN2", target_bir_lowering=False)
    x0T = nc.dram_tensor("x0T", [D, TOK], F32, kind="ExternalInput").ap()
    wq = nc.dram_tensor("wq", [8, DT, 128, 256], F32, kind="ExternalInput").ap()
    wk = nc.dram_tensor("wk", [8, DT, 128, 256], F32, kind="ExternalInput").ap()
    wv = nc.dram_tensor("wv", [4, DT, 128, 512], F32, kind="ExternalInput").ap()
    bq = nc.dram_tensor("bq", [128, DT], F32, kind="ExternalInput").ap()
    bk = nc.dram_tensor("bk", [128, DT], F32, kind="ExternalInput").ap()
    bvb = nc.dram_tensor("bvb", [128, D], F32, kind="ExternalInput").ap()
    qT_o = nc.dram_tensor("qT_o", [D, TOK], BF16, kind="ExternalOutput").ap()
    kT_o = nc.dram_tensor("kT_o", [D, TOK], BF16, kind="ExternalOutput").ap()
    v_o = nc.dram_tensor("v_o", [TOK, D], BF16, kind="ExternalOutput").ap()
    xn_o = nc.dram_tensor("xn_o", [D, TOK], BF16, kind="ExternalOutput").ap()

    ones_col_c = nc.inline_tensor(np.ones((128, 1), BF), "ones_col_c").ap()
    ones_row_c = nc.inline_tensor(np.ones((1, 128), np.float32), "ones_row_c").ap()

    ctx = ExitStack()
    sb = lambda name, shp, dt: ctx.enter_context(nc.sbuf_tensor(name, shp, dt))
    psum = lambda name, shp, dt=F32: ctx.enter_context(nc.psum_tensor(name, shp, dt))

    ones_col = sb("ones_col", [128, 1], BF16)
    ones_row = sb("ones_row", [1, 128], F32)
    xn = sb("xn", [128, DT * TOK], BF16)
    r_f = sb("r_f", [128, TOK], F32)
    mr_f = sb("mr_f", [128, TOK], F32)
    rowbuf = sb("rowbuf", [1, 6 * TOK], F32)
    rows = {n: rowbuf[:, i * TOK:(i + 1) * TOK]
            for i, n in enumerate(("m_row", "e2_row", "m2_row", "var_row", "r_row", "mr_row"))}
    rows["std_row"] = rows["m2_row"]
    eps_sb = sb("eps_sb", [128, 2], F32)
    bq_sb = sb("bq_sb", [128, DT], F32)
    bk_sb = sb("bk_sb", [128, DT], F32)
    bvb_sb = sb("bvb_sb", [128, D], F32)
    ctx_ln = ExitStack()
    sbl = lambda name, shp, dt: ctx_ln.enter_context(nc.sbuf_tensor(name, shp, dt))
    x_bf = sbl("x_bf", [128, DT * TOK], BF16)
    sqb = [sbl(f"sq{i}", [128, TOK], BF16) for i in range(2)]
    tmpx = [sbl(f"tmpx{i}", [128, TOK], F32) for i in range(2)]

    ctx_ps = ExitStack()
    psl = lambda name, shp: ctx_ps.enter_context(nc.psum_tensor(name, shp, F32))
    ps_sx = [psl(f"ps_sx{i}", [1, 512]) for i in range(2)]
    ps_sq = [psl(f"ps_sq{i}", [1, 512]) for i in range(2)]
    ps_r = psl("ps_r", [128, TOK])
    ps_mr = psl("ps_mr", [128, TOK])

    S = Sems(nc, ctx, ["ld", "wl0", "wl1", "ts", "vs", "ss", "st"])
    g, t, v, s, sy = nc.gpsimd, nc.tensor, nc.vector, nc.scalar, nc.sync

    with nc.Block():
        g.memset(eps_sb[:, 0:1], EPS_LN)
        g.memset(eps_sb[:, 1:2], EPS_L1)
        S.inc("ld", g.dma_start(out=ones_col[:], in_=ones_col_c), 16)
        S.inc("ld", g.dma_start(out=ones_row[:], in_=ones_row_c), 16)
        S.inc("ld", g.dma_start(out=x_bf[:].rearrange("p (t n) -> p t n", t=DT),
                                in_=x0T.rearrange("(t p) n -> p t n", p=128)), 16)
        S.inc("ld", g.dma_start(out=bq_sb[:], in_=bq), 16)
        S.inc("ld", g.dma_start(out=bk_sb[:], in_=bk), 16)
        S.inc("ld", g.dma_start(out=bvb_sb[:], in_=bvb), 16)
        lds0 = S.val("ld")

        t.wait_ge(S["ld"], lds0)
        for nch in range(NCH):
            for ct in range(DT):
                mm = t.matmul(ps_sx[nch][:], ones_col[:],
                              x_bf[:, ct * TOK + nch * 512: ct * TOK + (nch + 1) * 512],
                              start=(ct == 0), stop=(ct == DT - 1))
        S.inc("ts", mm)
        sq_done = {}
        s.wait_ge(S["ld"], lds0)
        for ct in range(DT):
            if ct >= 2:
                s.wait_ge(S["ts"], sq_done[ct - 2])
            ins = s.activation(sqb[ct % 2][:], x_bf[:, ct * TOK:(ct + 1) * TOK], AF.Square)
            ssv = S.inc("ss", ins)
            t.wait_ge(S["ss"], ssv)
            for nch in range(NCH):
                mm = t.matmul(ps_sq[nch][:], ones_col[:],
                              sqb[ct % 2][:, nch * 512:(nch + 1) * 512],
                              start=(ct == 0), stop=(ct == DT - 1))
            sq_done[ct] = S.inc("ts", mm)
        ts_stats = S.val("ts")

        v.wait_ge(S["ts"], ts_stats)
        for nch in range(NCH):
            sl = slice(nch * 512, (nch + 1) * 512)
            v.tensor_scalar_mul(rows["m_row"][:, sl], ps_sx[nch][:], 1.0 / D)
            ins = v.tensor_scalar_mul(rows["e2_row"][:, sl], ps_sq[nch][:], 1.0 / D)
        v.wait_ge(S["vs"], S.inc("vs", ins))
        ins = v.tensor_mul(rows["m2_row"][:], rows["m_row"][:], rows["m_row"][:])
        v.wait_ge(S["vs"], S.inc("vs", ins))
        ins = v.tensor_sub(rows["var_row"][:], rows["e2_row"][:], rows["m2_row"][:])
        S.inc("vs", ins)
        s.wait_ge(S["vs"], S.val("vs"))
        ins = s.activation(rows["std_row"][:], rows["var_row"][:], AF.Sqrt, bias=eps_sb[0:1, 0:1])
        S.inc("ss", ins)
        v.wait_ge(S["ss"], S.val("ss"))
        ins = v.reciprocal(rows["r_row"][:], rows["std_row"][:])
        v.wait_ge(S["vs"], S.inc("vs", ins))
        ins = v.tensor_mul(rows["mr_row"][:], rows["m_row"][:], rows["r_row"][:])
        S.inc("vs", ins)

        t.wait_ge(S["vs"], S.val("vs"))
        for nch in range(NCH):
            sl = slice(nch * 512, (nch + 1) * 512)
            t.matmul(ps_r[:, sl], ones_row[:], rows["r_row"][:, sl], start=True, stop=True)
            mm = t.matmul(ps_mr[:, sl], ones_row[:], rows["mr_row"][:, sl], start=True, stop=True)
        S.inc("ts", mm)

        v.wait_ge(S["ts"], S.val("ts"))
        v.tensor_copy(r_f[:], ps_r[:])
        ins = v.tensor_copy(mr_f[:], ps_mr[:])
        v.wait_ge(S["vs"], S.inc("vs", ins))
        for ct in range(DT):
            ins = v.tensor_mul(tmpx[ct % 2][:], x_bf[:, ct * TOK:(ct + 1) * TOK], r_f[:])
            v.wait_ge(S["vs"], S.inc("vs", ins))
            ins = v.tensor_sub(xn[:, ct * TOK:(ct + 1) * TOK], tmpx[ct % 2][:], mr_f[:])
            S.inc("vs", ins)
        vs_xn = S.val("vs")

        sy.wait_ge(S["vs"], vs_xn)
        ins = sy.dma_start(out=xn_o.rearrange("(t p) n -> p t n", p=128),
                           in_=xn[:].rearrange("p (t n) -> p t n", t=DT))
        S.inc("st", ins, 16)

        ctx_ln.close()
        ctx_ps.close()
        pb = [psum(f"pb{i}", [128, 512]) for i in range(4)]
        wbuf = [sb(f"wbuf{i}", [128, DT * 256], BF16) for i in range(2)]
        wvbuf = [sb(f"wvbuf{i}", [128, DT * 512], BF16) for i in range(2)]
        out_sb = sb("out_sb", [128, DT * TOK], BF16)
        v_sb = sb("v_sb", [128, 8 * D], BF16)

        qk_tile_mm = {}
        qk_tile_ev = {}
        wl_after = {}
        g.wait_ge(S["vs"], vs_xn)
        n_tile = 0
        for wi, wimg in enumerate((wq, wk)):
            for mg in range(8):
                li = wi * 8 + mg
                if li >= 2:
                    g.wait_ge(S["ts"], qk_tile_mm[(li - 2) * 4 + 3])
                wsem = "wl0" if li % 2 == 0 else "wl1"
                ins = g.dma_start(out=wbuf[li % 2][:].rearrange("p (t n) -> p t n", t=DT),
                                  in_=wimg[mg].rearrange("t p n -> p t n"))
                wl_after[li] = S.inc(wsem, ins, 16)
                t.wait_ge(S[wsem], wl_after[li])
                for m2 in range(2):
                    mt = mg * 2 + m2
                    for nch in range(NCH):
                        bank = n_tile % 4
                        if n_tile >= 4:
                            t.wait_ge(S["ss"], qk_tile_ev[n_tile - 4])
                        for ct in range(DT):
                            mm = t.matmul(pb[bank][:],
                                          wbuf[li % 2][:, ct * 256 + m2 * 128: ct * 256 + (m2 + 1) * 128],
                                          xn[:, ct * TOK + nch * 512: ct * TOK + (nch + 1) * 512],
                                          start=(ct == 0), stop=(ct == DT - 1))
                        qk_tile_mm[n_tile] = S.inc("ts", mm)
                        s.wait_ge(S["ts"], qk_tile_mm[n_tile])
                        if wi == 1 and mg == 0 and m2 == 0 and nch == 0:
                            s.wait_ge(S["st"], 32)
                        bsb = bq_sb if wi == 0 else bk_sb
                        ins = s.activation(out_sb[:, mt * TOK + nch * 512: mt * TOK + (nch + 1) * 512],
                                           pb[bank][:], AF.Identity, bias=bsb[:, mt:mt + 1])
                        qk_tile_ev[n_tile] = S.inc("ss", ins)
                        n_tile += 1
            sy.wait_ge(S["ss"], qk_tile_ev[n_tile - 1])
            ins = sy.dma_start(out=(qT_o if wi == 0 else kT_o).rearrange("(t p) n -> p t n", p=128),
                               in_=out_sb[:].rearrange("p (t n) -> p t n", t=DT))
            S.inc("st", ins, 16)

        v_tile_mm = {}
        v_tile_ev = {}
        n_vt = 0
        for dn in range(4):
            if dn >= 2:
                g.wait_ge(S["ts"], v_tile_mm[(dn - 2) * 8 + 7])
            wsem = "wl0" if dn % 2 == 0 else "wl1"
            ins = g.dma_start(out=wvbuf[dn % 2][:].rearrange("p (t n) -> p t n", t=DT),
                              in_=wv[dn].rearrange("t p n -> p t n"))
            wlv = S.inc(wsem, ins, 16)
            t.wait_ge(S[wsem], wlv)
            for tt in range(8):
                bank = n_vt % 4
                if n_vt == 0:
                    t.wait_ge(S["ss"], qk_tile_ev[63])
                elif n_vt >= 4:
                    t.wait_ge(S["vs"], v_tile_ev[n_vt - 4])
                for ct in range(DT):
                    mm = t.matmul(pb[bank][:],
                                  xn[:, ct * TOK + tt * 128: ct * TOK + (tt + 1) * 128],
                                  wvbuf[dn % 2][:, ct * 512:(ct + 1) * 512],
                                  start=(ct == 0), stop=(ct == DT - 1))
                v_tile_mm[n_vt] = S.inc("ts", mm)
                v.wait_ge(S["ts"], v_tile_mm[n_vt])
                ins = v.tensor_add(v_sb[:, tt * D + dn * 512: tt * D + (dn + 1) * 512],
                                   pb[bank][:], bvb_sb[:, dn * 512:(dn + 1) * 512])
                v_tile_ev[n_vt] = S.inc("vs", ins)
                n_vt += 1
        sy.wait_ge(S["vs"], v_tile_ev[n_vt - 1])
        ins = sy.dma_start(out=v_o.rearrange("(t p) n -> p t n", p=128),
                           in_=v_sb[:].rearrange("p (t n) -> p t n", t=8))
        S.inc("st", ins, 16)
        sy.wait_ge(S["st"], S.val("st"))

    ctx.close()
    return nc


def prep_A(inputs):
    X = _f32(inputs["L0_attn_X"])
    xT = _f32(X.reshape(NCORES * 1024, D).T)
    wq_img = img_lhsT(_f32(inputs["L0_attn_Wq"]).T)
    wk_img = img_lhsT(_f32(inputs["L0_attn_Wk"]).T)
    wv_img = img_rhs(_f32(inputs["L0_attn_Wv"]).T)
    bq_img = img_col(inputs["L0_attn_bq"])
    bk_img = img_col(inputs["L0_attn_bk"])
    bvb = _f32(np.broadcast_to(np.asarray(inputs["L0_attn_bv"], np.float32), (128, D)))
    return [{"x0T": _f32(xT[:, c * 1024:(c + 1) * 1024]),
             "wq": wq_img, "wk": wk_img, "wv": wv_img,
             "bq": bq_img, "bk": bk_img, "bvb": bvb} for c in range(NCORES)]





def build_launch_B(debug=False):
    TOK = 1024
    NCH = TOK // 512
    nc = bass.Bass("TRN2", target_bir_lowering=False)
    qT_i = nc.dram_tensor("qT_i", [D, TOK], BF16, kind="ExternalInput").ap()
    kTb = nc.dram_tensor("kTb", [H16, 128, SEQ], BF16, kind="ExternalInput").ap()
    vb = nc.dram_tensor("vb", [H16, 32, 128, 128], BF16, kind="ExternalInput").ap()
    wo = nc.dram_tensor("wo", [8, DT, 128, 256], F32, kind="ExternalInput").ap()
    bo = nc.dram_tensor("bo", [128, DT], F32, kind="ExternalInput").ap()
    xn_i = nc.dram_tensor("xn_i", [D, TOK], BF16, kind="ExternalInput").ap()
    x0T = nc.dram_tensor("x0T", [D, TOK], F32, kind="ExternalInput").ap()
    gw = nc.dram_tensor("gw", [DT, 128, 8], F32, kind="ExternalInput").ap()
    wg = nc.dram_tensor("wg", [8, 11, DT, 128, 128], F32, kind="ExternalInput").ap()
    wu = nc.dram_tensor("wu", [8, 11, DT, 128, 128], F32, kind="ExternalInput").ap()
    wd = nc.dram_tensor("wd", [8, 8, 11, 128, 256], F32, kind="ExternalInput").ap()
    shg = nc.dram_tensor("shg", [128, DT], F32, kind="ExternalInput").ap()
    shwg = nc.dram_tensor("shwg", [22, DT, 128, 128], F32, kind="ExternalInput").ap()
    shwu = nc.dram_tensor("shwu", [22, DT, 128, 128], F32, kind="ExternalInput").ap()
    shwd = nc.dram_tensor("shwd", [8, 22, 128, 256], F32, kind="ExternalInput").ap()
    sT_o = nc.dram_tensor("sT_o", [D, TOK], BF16, kind="ExternalOutput").ap()
    x1T_o = nc.dram_tensor("x1T_o", [D, TOK], F32, kind="ExternalOutput").ap()
    if debug:
        dbg_logit = nc.dram_tensor("dbg_logit", [128, 64], F32, kind="ExternalOutput").ap()
        dbg_comb = nc.dram_tensor("dbg_comb", [128, 64], F32, kind="ExternalOutput").ap()
        dbg_ce = nc.dram_tensor("dbg_ce", [128, TOK], F32, kind="ExternalOutput").ap()
        dbg_macc = nc.dram_tensor("dbg_macc", [D, TOK], F32, kind="ExternalOutput").ap()

    ones_col_c = nc.inline_tensor(np.ones((128, 1), BF), "ones_col_c").ap()
    ones_row_c = nc.inline_tensor(np.ones((1, 128), np.float32), "ones_row_c").ap()

    ctx = ExitStack()
    sb = lambda name, shp, dt: ctx.enter_context(nc.sbuf_tensor(name, shp, dt))
    psum = lambda name, shp, dt=F32: ctx.enter_context(nc.psum_tensor(name, shp, dt))

    ones_col = sb("ones_col", [128, 1], BF16)
    ones_row = sb("ones_row", [1, 128], F32)
    xn = sb("xn", [128, DT * TOK], BF16)
    out_sb = sb("out_sb", [128, DT * TOK], BF16)
    ctx_att = ExitStack()
    sba = lambda name, shp, dt: ctx_att.enter_context(nc.sbuf_tensor(name, shp, dt))
    psa = lambda name, shp: ctx_att.enter_context(nc.psum_tensor(name, shp, F32))
    qT = sba("qT", [128, DT * TOK], BF16)
    oT = sba("oT", [128, DT * TOK], BF16)
    kbuf = [sba(f"kbuf{i}", [128, SEQ], BF16) for i in range(2)]
    vbuf = [sba(f"vbuf{i}", [128, SEQ], BF16) for i in range(2)]
    att = sba("att", [128, TOK], BF16)
    den_row = sba("den_row", [1, TOK], F32)
    rden_row = sba("rden_row", [1, TOK], F32)
    bc_f = sba("bc_f", [128, TOK], F32)
    bo_sb = sba("bo_sb", [128, DT], F32)
    wbuf = [sba(f"wbuf{i}", [128, DT * 256], BF16) for i in range(2)]

    ps_s = psa("ps_s", [128, TOK])
    ps_o = psa("ps_o", [128, TOK])
    ps_den = [psa(f"ps_den{i}", [1, 512]) for i in range(2)]
    ps_bc = psa("ps_bc", [128, TOK])

    S = Sems(nc, ctx, ["ld", "kl0", "kl1", "wl0", "wl1", "ts", "vs", "ss", "st", "sx"])
    g, t, v, s, sy = nc.gpsimd, nc.tensor, nc.vector, nc.scalar, nc.sync

    with nc.Block():
        S.inc("ld", g.dma_start(out=ones_col[:], in_=ones_col_c), 16)
        S.inc("ld", g.dma_start(out=ones_row[:], in_=ones_row_c), 16)
        S.inc("ld", g.dma_start(out=qT[:].rearrange("p (t n) -> p t n", t=DT),
                                in_=qT_i.rearrange("(t p) n -> p t n", p=128)), 16)
        S.inc("ld", g.dma_start(out=bo_sb[:], in_=bo), 16)
        S.inc("ld", g.dma_start(out=xn[:].rearrange("p (t n) -> p t n", t=DT),
                                in_=xn_i.rearrange("(t p) n -> p t n", p=128)), 16)
        lds0 = S.val("ld")

        kv_after = {}
        S.inc("kl0", sy.dma_start(out=kbuf[0][:], in_=kTb[0]), 16)
        ins = sy.dma_start(out=vbuf[0][:].rearrange("p (k n) -> p k n", k=32),
                           in_=vb[0].rearrange("k p n -> p k n"))
        kv_after[0] = S.inc("kl0", ins, 16)

        exp_done = {}
        avden_done = {}
        norm_done = {}
        for h in range(H16):
            if h + 1 < H16:
                if h >= 1:
                    sy.wait_ge(S["ts"], avden_done[(h - 1, 31)])
                ksem = "kl0" if (h + 1) % 2 == 0 else "kl1"
                S.inc(ksem, sy.dma_start(out=kbuf[(h + 1) % 2][:], in_=kTb[h + 1]), 16)
                ins = sy.dma_start(out=vbuf[(h + 1) % 2][:].rearrange("p (k n) -> p k n", k=32),
                                   in_=vb[h + 1].rearrange("k p n -> p k n"))
                kv_after[h + 1] = S.inc(ksem, ins, 16)
            t.wait_ge(S["kl0" if h % 2 == 0 else "kl1"], kv_after[h])
            if h == 0:
                t.wait_ge(S["ld"], lds0)
            for kjt in range(32):
                if (h, kjt) != (0, 0):
                    prev = (h, kjt - 1) if kjt else (h - 1, 31)
                    t.wait_ge(S["ss"], exp_done[prev])
                for nch in range(NCH):
                    mm = t.matmul(ps_s[:, nch * 512:(nch + 1) * 512],
                                  kbuf[h % 2][:, kjt * 128:(kjt + 1) * 128],
                                  qT[:, h * TOK + nch * 512: h * TOK + (nch + 1) * 512],
                                  start=True, stop=True)
                sc_v = S.inc("ts", mm)
                s.wait_ge(S["ts"], sc_v)
                if (h, kjt) != (0, 0):
                    prev = (h, kjt - 1) if kjt else (h - 1, 31)
                    s.wait_ge(S["ts"], avden_done[prev])
                ins = s.activation(att[:], ps_s[:], AF.Exp, scale=float(1.0 / np.sqrt(HD)))
                exp_done[(h, kjt)] = S.inc("ss", ins)
                t.wait_ge(S["ss"], exp_done[(h, kjt)])
                if kjt == 0 and h > 0:
                    t.wait_ge(S["vs"], norm_done[h - 1])
                for nch in range(NCH):
                    t.matmul(ps_o[:, nch * 512:(nch + 1) * 512],
                             vbuf[h % 2][:, kjt * 128:(kjt + 1) * 128],
                             att[:, nch * 512:(nch + 1) * 512],
                             start=(kjt == 0), stop=(kjt == 31))
                    mm = t.matmul(ps_den[nch][:], ones_col[:],
                                  att[:, nch * 512:(nch + 1) * 512],
                                  start=(kjt == 0), stop=(kjt == 31))
                avden_done[(h, kjt)] = S.inc("ts", mm)
            v.wait_ge(S["ts"], avden_done[(h, 31)])
            for nch in range(NCH):
                ins = v.tensor_copy(den_row[:, nch * 512:(nch + 1) * 512], ps_den[nch][:])
            v.wait_ge(S["vs"], S.inc("vs", ins))
            ins = v.reciprocal(rden_row[:], den_row[:])
            S.inc("vs", ins)
            t.wait_ge(S["vs"], S.val("vs"))
            for nch in range(NCH):
                mm = t.matmul(ps_bc[:, nch * 512:(nch + 1) * 512], ones_row[:],
                              rden_row[:, nch * 512:(nch + 1) * 512], start=True, stop=True)
            bc_v = S.inc("ts", mm)
            v.wait_ge(S["ts"], bc_v)
            ins = v.tensor_copy(bc_f[:], ps_bc[:])
            v.wait_ge(S["vs"], S.inc("vs", ins))
            ins = v.tensor_mul(oT[:, h * TOK:(h + 1) * TOK], ps_o[:], bc_f[:])
            norm_done[h] = S.inc("vs", ins)

        n_tile = 0
        wo_mm = {}
        wo_ev = {}
        wl_after = {}
        pb = [ps_s[:, 0:512], ps_s[:, 512:1024], ps_o[:, 0:512], ps_o[:, 512:1024]]
        g.wait_ge(S["vs"], norm_done[H16 - 1])
        for mg in range(8):
            if mg >= 2:
                g.wait_ge(S["ts"], wo_mm[(mg - 2) * 4 + 3])
            wsem = "wl0" if mg % 2 == 0 else "wl1"
            ins = g.dma_start(out=wbuf[mg % 2][:].rearrange("p (t n) -> p t n", t=DT),
                              in_=wo[mg].rearrange("t p n -> p t n"))
            wl_after[mg] = S.inc(wsem, ins, 16)
            t.wait_ge(S[wsem], wl_after[mg])
            if mg == 0:
                t.wait_ge(S["vs"], norm_done[H16 - 1])
            for m2 in range(2):
                mt = mg * 2 + m2
                for nch in range(NCH):
                    bank = n_tile % 4
                    if n_tile >= 4:
                        t.wait_ge(S["ss"], wo_ev[n_tile - 4])
                    for ct in range(DT):
                        mm = t.matmul(pb[bank],
                                      wbuf[mg % 2][:, ct * 256 + m2 * 128: ct * 256 + (m2 + 1) * 128],
                                      oT[:, ct * TOK + nch * 512: ct * TOK + (nch + 1) * 512],
                                      start=(ct == 0), stop=(ct == DT - 1))
                    wo_mm[n_tile] = S.inc("ts", mm)
                    s.wait_ge(S["ts"], wo_mm[n_tile])
                    ins = s.activation(out_sb[:, mt * TOK + nch * 512: mt * TOK + (nch + 1) * 512],
                                       pb[bank], AF.Identity, bias=bo_sb[:, mt:mt + 1])
                    wo_ev[n_tile] = S.inc("ss", ins)
                    n_tile += 1
        sy.wait_ge(S["ss"], wo_ev[n_tile - 1])
        ins = sy.dma_start(out=sT_o.rearrange("(t p) n -> p t n", p=128),
                           in_=out_sb[:].rearrange("p (t n) -> p t n", t=DT))
        S.inc("st", ins, 16)

    ctx_att.close()
    I128_c = nc.inline_tensor(np.eye(128, dtype=np.float32), "I128_c").ap()
    ctx_moe = ExitStack()
    sbm = lambda name, shp, dt: ctx_moe.enter_context(nc.sbuf_tensor(name, shp, dt))
    psm = lambda name, shp: ctx_moe.enter_context(nc.psum_tensor(name, shp, F32))
    I128 = sbm("I128", [128, 128], F32)
    gw_sb = sbm("gw_sb", [128, DT * 8], BF16)
    logit = sbm("logit", [128, 8 * 8], F32)
    comb = sbm("comb", [128, 8 * 8], F32)
    msc = sbm("msc", [128, 6], F32)
    mask = sbm("mask", [128, 8], F32)
    zex = sbm("zex", [128, 8], F32)
    macc = sbm("macc", [128, DT * TOK], F32)
    ce_row = sbm("ce_row", [1, TOK], F32)
    ce_bc = sbm("ce_bc", [128, TOK], F32)
    shg_sb = sbm("shg_sb", [128, DT], BF16)
    sg_row = sbm("sg_row", [1, TOK], F32)
    ctx_ex = ExitStack()
    sbe = lambda name, shp, dt: ctx_ex.enter_context(nc.sbuf_tensor(name, shp, dt))
    sg_t = sbe("sg_t", [128, 512], BF16)
    u_t = sbe("u_t", [128, 512], BF16)
    wga = [sbe(f"wga{i}", [128, DT * 128], BF16) for i in range(2)]
    wua = [sbe(f"wua{i}", [128, DT * 128], BF16) for i in range(2)]
    wda = [sbe(f"wda{i}", [128, 11 * 256], BF16) for i in range(2)]
    pb4 = [psm(f"pbm{i}", [128, 512]) for i in range(4)]
    ps_row = [psm(f"ps_row{i}", [1, 512]) for i in range(2)]
    ps_bc2 = psm("ps_bc2", [128, TOK])

    with nc.Block():
        S.inc("ld", g.dma_start(out=I128[:], in_=I128_c), 16)
        S.inc("ld", g.dma_start(out=gw_sb[:].rearrange("p (t n) -> p t n", t=DT),
                                in_=gw.rearrange("t p n -> p t n")), 16)
        S.inc("ld", g.dma_start(out=shg_sb[:], in_=shg), 16)
        lds1 = S.val("ld")

        t.wait_ge(S["ld"], lds1)
        lcopy = {}
        for tt in range(8):
            if tt >= 2:
                t.wait_ge(S["vs"], lcopy[tt - 2])
            for ct in range(DT):
                mm = t.matmul(pb4[tt % 2][:, 0:8],
                              xn[:, ct * TOK + tt * 128: ct * TOK + (tt + 1) * 128],
                              gw_sb[:, ct * 8:(ct + 1) * 8],
                              start=(ct == 0), stop=(ct == DT - 1))
            lv = S.inc("ts", mm)
            v.wait_ge(S["ts"], lv)
            ins = v.tensor_copy(logit[:, tt * 8:(tt + 1) * 8], pb4[tt % 2][:, 0:8])
            lcopy[tt] = S.inc("vs", ins)
        vs_logit = S.val("vs")
        def vsync(ins):
            v.wait_ge(S["vs"], S.inc("vs", ins))
        for tt in range(8):
            lsl = logit[:, tt * 8:(tt + 1) * 8]
            vsync(v.reduce_max(msc[:, 0:1], lsl, axis=AX.X))
            ins = v.tensor_scalar_mul(msc[:, 1:2], msc[:, 0:1], -1.0)
            vv = S.inc("vs", ins)
            v.wait_ge(S["vs"], vv)
            s.wait_ge(S["vs"], vv)
            ins = s.activation(mask[:], lsl, AF.Identity, bias=msc[:, 1:2])
            sv = S.inc("ss", ins)
            v.wait_ge(S["ss"], sv)
            vsync(v.tensor_scalar(mask[:], mask[:], 0.0, None, ALU.is_ge))
            vsync(v.tensor_scalar_mul(mask[:], mask[:], -1e30))
            vsync(v.tensor_add(mask[:], mask[:], lsl))
            vsync(v.reduce_max(msc[:, 2:3], mask[:], axis=AX.X))
            ins = v.tensor_scalar_mul(msc[:, 3:4], msc[:, 2:3], -1.0)
            vv = S.inc("vs", ins)
            v.wait_ge(S["vs"], vv)
            s.wait_ge(S["vs"], vv)
            ins = s.activation(zex[:], lsl, AF.Exp, bias=msc[:, 1:2])
            v.wait_ge(S["ss"], S.inc("ss", ins))
            ins = s.activation(mask[:], lsl, AF.Identity, bias=msc[:, 3:4])
            sv = S.inc("ss", ins)
            v.wait_ge(S["ss"], sv)
            vsync(v.tensor_scalar(mask[:], mask[:], 0.0, None, ALU.is_ge))
            vsync(v.tensor_mul(zex[:], zex[:], mask[:]))
            vsync(v.reduce_sum(msc[:, 4:5], zex[:], axis=AX.X))
            ins = v.reciprocal(msc[:, 5:6], msc[:, 4:5])
            vv = S.inc("vs", ins)
            v.wait_ge(S["vs"], vv)
            s.wait_ge(S["vs"], vv)
            ins = s.mul(comb[:, tt * 8:(tt + 1) * 8], zex[:], msc[:, 5:6])
            S.inc("ss", ins)
        ss_comb = S.val("ss")

        if debug:
            sy.wait_ge(S["ss"], ss_comb)
            S.inc("st", sy.dma_start(out=dbg_logit, in_=logit[:]), 16)
            S.inc("st", sy.dma_start(out=dbg_comb, in_=comb[:]), 16)

        t.wait_ge(S["ss"], ss_comb)
        for nch in range(NCH):
            for ct in range(DT):
                mm = t.matmul(ps_row[nch][:], shg_sb[:, ct:ct + 1],
                              xn[:, ct * TOK + nch * 512: ct * TOK + (nch + 1) * 512],
                              start=(ct == 0), stop=(ct == DT - 1))
        tv = S.inc("ts", mm)
        s.wait_ge(S["ts"], tv)
        for nch in range(NCH):
            ins = s.activation(sg_row[:, nch * 512:(nch + 1) * 512], ps_row[nch][:], AF.Sigmoid)
        sv = S.inc("ss", ins)
        t.wait_ge(S["ss"], sv)
        for nch in range(NCH):
            mm = t.matmul(ps_bc2[:, nch * 512:(nch + 1) * 512], ones_row[:],
                          sg_row[:, nch * 512:(nch + 1) * 512], start=True, stop=True)
        tv = S.inc("ts", mm)
        v.wait_ge(S["ts"], tv)
        ins = v.tensor_copy(ce_bc[:], ps_bc2[:])
        vs_sg = S.inc("vs", ins)
        v.wait_ge(S["vs"], vs_sg)

        he = ctx_ex.enter_context(nc.sbuf_tensor("he", [128, 11 * TOK], BF16))
        ev_prev = None
        for ui in range(10):
            is_sh = ui < 2
            e = ui - 2
            if ui == 0:
                pass
            elif ui == 1:
                pass
            else:
                t.wait_ge(S["ss"], ss_comb)
                t.wait_ge(S["vs"], S.val("vs"))
                for tt in range(8):
                    t.matmul(ps_row[0][0:1, tt * 64: tt * 64 + 64],
                             comb[:, tt * 8 + e: tt * 8 + e + 1], I128[:, 0:64],
                             start=True, stop=True)
                    mm = t.matmul(ps_row[1][0:1, tt * 64: tt * 64 + 64],
                                  comb[:, tt * 8 + e: tt * 8 + e + 1], I128[:, 64:128],
                                  start=True, stop=True)
                tv = S.inc("ts", mm)
                v.wait_ge(S["ts"], tv)
                for tt in range(8):
                    v.tensor_copy(ce_row[:, tt * 128: tt * 128 + 64],
                                  ps_row[0][0:1, tt * 64: tt * 64 + 64])
                    ins = v.tensor_copy(ce_row[:, tt * 128 + 64: (tt + 1) * 128],
                                        ps_row[1][0:1, tt * 64: tt * 64 + 64])
                vv = S.inc("vs", ins)
                t.wait_ge(S["vs"], vv)
                for nch in range(NCH):
                    mm = t.matmul(ps_bc2[:, nch * 512:(nch + 1) * 512], ones_row[:],
                                  ce_row[:, nch * 512:(nch + 1) * 512], start=True, stop=True)
                tv = S.inc("ts", mm)
                v.wait_ge(S["ts"], tv)
                ins = v.tensor_copy(ce_bc[:], ps_bc2[:])
                v.wait_ge(S["vs"], S.inc("vs", ins))
            if debug and ui == 2:
                sy.wait_ge(S["vs"], S.val("vs"))
                S.inc("st", sy.dma_start(out=dbg_ce, in_=ce_bc[:]), 16)
            gu_done = {}
            for hc in range(11):
                g.wait_ge(S["ts"], S.val("ts"))
                if is_sh:
                    gsrc = shwg[ui * 11 + hc]
                    usrc = shwu[ui * 11 + hc]
                else:
                    gsrc = wg[e, hc]
                    usrc = wu[e, hc]
                wsem = "wl0" if hc % 2 == 0 else "wl1"
                S.inc(wsem, g.dma_start(out=wga[hc % 2][:].rearrange("p (t n) -> p t n", t=DT),
                                        in_=gsrc.rearrange("t p n -> p t n")), 16)
                S.inc(wsem, g.dma_start(out=wua[hc % 2][:].rearrange("p (t n) -> p t n", t=DT),
                                        in_=usrc.rearrange("t p n -> p t n")), 16)
                wlv = S.val(wsem)
                t.wait_ge(S[wsem], wlv)
                if ui == 0 and hc == 0:
                    t.wait_ge(S["vs"], vs_sg)
                for nch in range(NCH):
                    bg, bu = nch, 2 + nch
                    if ("evg", hc - 1, nch) in gu_done:
                        t.wait_ge(S["ss"], gu_done[("evg", hc - 1, nch)])
                        t.wait_ge(S["vs"], gu_done[("evu", hc - 1, nch)])
                    elif hc == 0 and ui > 0:
                        t.wait_ge(S["vs"], S.val("vs"))
                    for ct in range(DT):
                        mm = t.matmul(pb4[bg][:], wga[hc % 2][:, ct * 128:(ct + 1) * 128],
                                      xn[:, ct * TOK + nch * 512: ct * TOK + (nch + 1) * 512],
                                      start=(ct == 0), stop=(ct == DT - 1))
                    S.inc("ts", mm)
                    for ct in range(DT):
                        mm = t.matmul(pb4[bu][:], wua[hc % 2][:, ct * 128:(ct + 1) * 128],
                                      xn[:, ct * TOK + nch * 512: ct * TOK + (nch + 1) * 512],
                                      start=(ct == 0), stop=(ct == DT - 1))
                    mmv = S.inc("ts", mm)
                    s.wait_ge(S["ts"], mmv)
                    if ev_prev is not None:
                        s.wait_ge(S["vs"], ev_prev)
                    ins = s.activation(sg_t[:], pb4[bg][:], AF.Silu)
                    gu_done[("evg", hc, nch)] = S.inc("ss", ins)
                    v.wait_ge(S["ss"], gu_done[("evg", hc, nch)])
                    ins = v.tensor_mul(u_t[:], pb4[bu][:], ce_bc[:, nch * 512:(nch + 1) * 512])
                    v.wait_ge(S["vs"], S.inc("vs", ins))
                    ins = v.tensor_mul(he[:, hc * TOK + nch * 512: hc * TOK + (nch + 1) * 512],
                                       sg_t[:], u_t[:])
                    gu_done[("evu", hc, nch)] = S.inc("vs", ins)
                    ev_prev = gu_done[("evu", hc, nch)]
            vs_he = S.val("vs")
            nmt2 = 0
            dn_mm = {}
            dn_ev = {}
            for mg in range(8):
                if mg >= 2:
                    g.wait_ge(S["ts"], dn_mm[(mg - 2) * 4 + 3])
                if is_sh:
                    dsrc = shwd[mg, ui * 11:(ui + 1) * 11]
                else:
                    dsrc = wd[e, mg]
                wsem = "wl0" if mg % 2 == 0 else "wl1"
                S.inc(wsem, g.dma_start(out=wda[mg % 2][:].rearrange("p (t n) -> p t n", t=11),
                                        in_=dsrc.rearrange("t p n -> p t n")), 16)
                wlv = S.val(wsem)
                t.wait_ge(S[wsem], wlv)
                if mg == 0:
                    t.wait_ge(S["vs"], vs_he)
                for m2 in range(2):
                    mt = mg * 2 + m2
                    for nch in range(NCH):
                        bank = nmt2 % 4
                        if nmt2 >= 4:
                            t.wait_ge(S["vs"], dn_ev[nmt2 - 4])
                        for hc in range(11):
                            mm = t.matmul(pb4[bank][:],
                                          wda[mg % 2][:, hc * 256 + m2 * 128: hc * 256 + (m2 + 1) * 128],
                                          he[:, hc * TOK + nch * 512: hc * TOK + (nch + 1) * 512],
                                          start=(hc == 0), stop=(hc == 10))
                        dn_mm[nmt2] = S.inc("ts", mm)
                        v.wait_ge(S["ts"], dn_mm[nmt2])
                        sl = slice(mt * TOK + nch * 512, mt * TOK + (nch + 1) * 512)
                        if ui == 0:
                            ins = v.tensor_copy(macc[:, sl], pb4[bank][:])
                        else:
                            ins = v.tensor_add(macc[:, sl], macc[:, sl], pb4[bank][:])
                        dn_ev[nmt2] = S.inc("vs", ins)
                        v.wait_ge(S["vs"], dn_ev[nmt2])
                        nmt2 += 1

        if debug:
            sy.wait_ge(S["vs"], S.val("vs"))
            S.inc("st", sy.dma_start(out=dbg_macc.rearrange("(t p) n -> p t n", p=128),
                                     in_=macc[:].rearrange("p (t n) -> p t n", t=DT)), 16)
        ctx_ex.close()
        x0b = [ctx_moe.enter_context(nc.sbuf_tensor(f"x0b{i}", [128, TOK], F32)) for i in range(2)]
        gt = ctx_moe.enter_context(nc.sbuf_tensor("gt", [128, TOK], BF16))
        x1t = [ctx_moe.enter_context(nc.sbuf_tensor(f"x1t{i}", [128, TOK], F32)) for i in range(2)]
        vs_all = S.val("vs")
        gt2 = ctx_moe.enter_context(nc.sbuf_tensor("gt2", [128, TOK], F32))
        x0_ld = {}
        add_done = {}
        st_x1 = {}
        g.wait_ge(S["vs"], vs_all)
        x0_ld[0] = S.inc("kl0", g.dma_start(out=x0b[0][:], in_=x0T[0:128, :]), 16)
        x0_ld[1] = S.inc("kl1", g.dma_start(out=x0b[1][:], in_=x0T[128:256, :]), 16)
        s.wait_ge(S["vs"], vs_all)
        for mt in range(DT):
            if mt - 1 in add_done:
                s.wait_ge(S["vs"], add_done[mt - 1])
            ins = s.activation(gt[:], macc[:, mt * TOK:(mt + 1) * TOK], AF.Silu)
            gv = S.inc("ss", ins)
            v.wait_ge(S["ss"], gv)
            v.wait_ge(S["kl0" if mt % 2 == 0 else "kl1"], x0_ld[mt])
            ins = v.tensor_mul(gt2[:], out_sb[:, mt * TOK:(mt + 1) * TOK], gt[:])
            v.wait_ge(S["vs"], S.inc("vs", ins))
            if mt >= 2:
                v.wait_ge(S["st" if (mt - 2) % 2 == 0 else "sx"], st_x1[mt - 2])
            ins = v.tensor_add(x1t[mt % 2][:], x0b[mt % 2][:], gt2[:])
            vv = S.inc("vs", ins)
            add_done[mt] = vv
            v.wait_ge(S["vs"], vv)
            sy.wait_ge(S["vs"], vv)
            ins = sy.dma_start(out=x1T_o[mt * 128:(mt + 1) * 128, :], in_=x1t[mt % 2][:])
            st_x1[mt] = S.inc("st" if mt % 2 == 0 else "sx", ins, 16)
            if mt + 2 < DT:
                g.wait_ge(S["vs"], vv)
                x0_ld[mt + 2] = S.inc("kl0" if mt % 2 == 0 else "kl1", g.dma_start(
                    out=x0b[mt % 2][:], in_=x0T[(mt + 2) * 128:(mt + 3) * 128, :]), 16)
        sy.wait_ge(S["st"], S.val("st"))
        sy.wait_ge(S["sx"], S.val("sx"))

    ctx_moe.close()
    ctx.close()
    return nc


def prep_B(inputs, a_results):
    wo_img = img_lhsT(_f32(inputs["L0_attn_Wo"]).T)
    bo_img = img_col(inputs["L0_attn_bo"])
    kTb_list, vb_list = [], []
    for b in range(NB):
        ks = np.concatenate([np.asarray(a_results[b * 4 + i]["kT_o"]) for i in range(4)], axis=1)
        kTb_list.append(np.ascontiguousarray(ks.reshape(H16, 128, SEQ)))
        vs_nat = np.concatenate([np.asarray(a_results[b * 4 + i]["v_o"]) for i in range(4)], axis=0)
        vimg = vs_nat.reshape(32, 128, H16, 128).transpose(2, 0, 1, 3)
        vb_list.append(np.ascontiguousarray(vimg))
    X = _f32(inputs["L0_attn_X"])
    xT = _f32(X.reshape(NCORES * 1024, D).T)
    gw_img = _f32(inputs["L0_ffn_gate_w"]).T.reshape(DT, 128, 8)
    wg_img = np.stack([_f32(inputs["L0_ffn_w_gate"][e]).reshape(DT, 128, 11, 128).transpose(2, 0, 1, 3)
                       for e in range(8)])
    wu_img = np.stack([_f32(inputs["L0_ffn_w_up"][e]).reshape(DT, 128, 11, 128).transpose(2, 0, 1, 3)
                       for e in range(8)])
    wd_img = np.stack([_f32(inputs["L0_ffn_w_down"][e]).reshape(11, 128, 8, 256).transpose(2, 0, 1, 3)
                       for e in range(8)])
    shg_img = img_col(inputs["L0_ffn_shgate_w"])
    shwg_img = _f32(inputs["L0_ffn_shw_gate"]).reshape(DT, 128, 22, 128).transpose(2, 0, 1, 3)
    shwu_img = _f32(inputs["L0_ffn_shw_up"]).reshape(DT, 128, 22, 128).transpose(2, 0, 1, 3)
    shwd_img = _f32(inputs["L0_ffn_shw_down"]).reshape(22, 128, 8, 256).transpose(2, 0, 1, 3)
    in_maps = []
    for c in range(NCORES):
        in_maps.append({
            "qT_i": np.asarray(a_results[c]["qT_o"]),
            "kTb": kTb_list[c // 4], "vb": vb_list[c // 4],
            "wo": wo_img, "bo": bo_img,
            "xn_i": np.asarray(a_results[c]["xn_o"]),
            "x0T": _f32(xT[:, c * 1024:(c + 1) * 1024]),
            "gw": _f32(gw_img), "wg": _f32(wg_img), "wu": _f32(wu_img), "wd": _f32(wd_img),
            "shg": shg_img, "shwg": _f32(shwg_img), "shwu": _f32(shwu_img), "shwd": _f32(shwd_img),
        })
    return in_maps





def build_launch_C():
    TOK = 1024
    NCH = TOK // 512
    nc = bass.Bass("TRN2", target_bir_lowering=False)
    x1T = nc.dram_tensor("x1T", [D, TOK], F32, kind="ExternalInput").ap()
    wq = nc.dram_tensor("wq", [8, DT, 128, 256], F32, kind="ExternalInput").ap()
    wk = nc.dram_tensor("wk", [8, DT, 128, 256], F32, kind="ExternalInput").ap()
    wv = nc.dram_tensor("wv", [4, DT, 128, 512], F32, kind="ExternalInput").ap()
    wb = nc.dram_tensor("wb", [DT, 128, 16], F32, kind="ExternalInput").ap()
    wg1 = nc.dram_tensor("wg1", [DT, 128, 128], F32, kind="ExternalInput").ap()
    wg2 = nc.dram_tensor("wg2", [128, D], F32, kind="ExternalInput").ap()
    qT_o = nc.dram_tensor("qT_o", [D, TOK], BF16, kind="ExternalOutput").ap()
    kT_o = nc.dram_tensor("kT_o", [D, TOK], BF16, kind="ExternalOutput").ap()
    kn_o = nc.dram_tensor("kn_o", [TOK, D], BF16, kind="ExternalOutput").ap()
    v_o = nc.dram_tensor("v_o", [TOK, D], BF16, kind="ExternalOutput").ap()
    beta_o = nc.dram_tensor("beta_o", [TOK, 16], F32, kind="ExternalOutput").ap()
    gate_o = nc.dram_tensor("gate_o", [D, TOK], BF16, kind="ExternalOutput").ap()
    xn1_o = nc.dram_tensor("xn1_o", [D, TOK], BF16, kind="ExternalOutput").ap()

    ones_col_c = nc.inline_tensor(np.ones((128, 1), BF), "ones_col_c").ap()
    ones_row_c = nc.inline_tensor(np.ones((1, 128), np.float32), "ones_row_c").ap()
    ident_c = nc.inline_tensor(np.eye(128, dtype=BF), "ident_c").ap()

    ctx = ExitStack()
    sb = lambda name, shp, dt: ctx.enter_context(nc.sbuf_tensor(name, shp, dt))
    psum = lambda name, shp, dt=F32: ctx.enter_context(nc.psum_tensor(name, shp, dt))

    ones_col = sb("ones_col", [128, 1], BF16)
    ones_colf = sb("ones_colf", [128, 1], F32)
    ones_row = sb("ones_row", [1, 128], F32)
    ident = sb("ident", [128, 128], BF16)
    eps_sb = sb("eps_sb", [128, 2], F32)
    xn = sb("xn", [128, DT * TOK], BF16)
    r_f = sb("r_f", [128, TOK], F32)
    mr_f = sb("mr_f", [128, TOK], F32)
    rowbuf = sb("rowbuf", [1, 6 * TOK], F32)
    rows = {n: rowbuf[:, i * TOK:(i + 1) * TOK]
            for i, n in enumerate(("m_row", "e2_row", "m2_row", "var_row", "r_row", "mr_row"))}
    rows["std_row"] = rows["m2_row"]
    out_sb = sb("out_sb", [128, DT * TOK], BF16)
    knT = sb("knT", [128, DT * TOK], BF16)
    wb_sb = sb("wb_sb", [128, DT * 16], BF16)
    wg1_sb = sb("wg1_sb", [128, DT * 128], BF16)
    wg2_sb = sb("wg2_sb", [128, D], BF16)
    t1_sb = sb("t1_sb", [128, TOK], BF16)
    beta_sb = sb("beta_sb", [128, 8 * 16], F32)

    ctx_ln = ExitStack()
    x_bf = ctx_ln.enter_context(nc.sbuf_tensor("x_bf", [128, DT * TOK], BF16))
    sqb = [ctx_ln.enter_context(nc.sbuf_tensor(f"sq{i}", [128, TOK], BF16)) for i in range(2)]
    tmpx = [ctx_ln.enter_context(nc.sbuf_tensor(f"tmpx{i}", [128, TOK], F32)) for i in range(2)]

    ctx_ps = ExitStack()
    psl = lambda name, shp: ctx_ps.enter_context(nc.psum_tensor(name, shp, F32))
    ps_sx = [psl(f"ps_sx{i}", [1, 512]) for i in range(2)]
    ps_sq = [psl(f"ps_sq{i}", [1, 512]) for i in range(2)]
    ps_r = psl("ps_r", [128, TOK])
    ps_mr = psl("ps_mr", [128, TOK])

    S = Sems(nc, ctx, ["ld", "wl0", "wl1", "ts", "vs", "ss", "st"])
    g, t, v, s, sy = nc.gpsimd, nc.tensor, nc.vector, nc.scalar, nc.sync

    def vsync(ins):
        v.wait_ge(S["vs"], S.inc("vs", ins))

    with nc.Block():
        g.memset(eps_sb[:, 0:1], EPS_LN)
        g.memset(eps_sb[:, 1:2], EPS_L1)
        g.memset(ones_colf[:], 1.0)
        S.inc("ld", g.dma_start(out=ones_col[:], in_=ones_col_c), 16)
        S.inc("ld", g.dma_start(out=ones_row[:], in_=ones_row_c), 16)
        S.inc("ld", g.dma_start(out=ident[:], in_=ident_c), 16)
        S.inc("ld", g.dma_start(out=x_bf[:].rearrange("p (t n) -> p t n", t=DT),
                                in_=x1T.rearrange("(t p) n -> p t n", p=128)), 16)
        S.inc("ld", g.dma_start(out=wb_sb[:].rearrange("p (t n) -> p t n", t=DT),
                                in_=wb.rearrange("t p n -> p t n")), 16)
        S.inc("ld", g.dma_start(out=wg1_sb[:].rearrange("p (t n) -> p t n", t=DT),
                                in_=wg1.rearrange("t p n -> p t n")), 16)
        S.inc("ld", g.dma_start(out=wg2_sb[:], in_=wg2), 16)
        lds0 = S.val("ld")

        t.wait_ge(S["ld"], lds0)
        for nch in range(NCH):
            for ct in range(DT):
                mm = t.matmul(ps_sx[nch][:], ones_col[:],
                              x_bf[:, ct * TOK + nch * 512: ct * TOK + (nch + 1) * 512],
                              start=(ct == 0), stop=(ct == DT - 1))
        S.inc("ts", mm)
        sq_done = {}
        s.wait_ge(S["ld"], lds0)
        for ct in range(DT):
            if ct >= 2:
                s.wait_ge(S["ts"], sq_done[ct - 2])
            ins = s.activation(sqb[ct % 2][:], x_bf[:, ct * TOK:(ct + 1) * TOK], AF.Square)
            ssv = S.inc("ss", ins)
            t.wait_ge(S["ss"], ssv)
            for nch in range(NCH):
                mm = t.matmul(ps_sq[nch][:], ones_col[:],
                              sqb[ct % 2][:, nch * 512:(nch + 1) * 512],
                              start=(ct == 0), stop=(ct == DT - 1))
            sq_done[ct] = S.inc("ts", mm)
        ts_stats = S.val("ts")
        v.wait_ge(S["ts"], ts_stats)
        for nch in range(NCH):
            sl = slice(nch * 512, (nch + 1) * 512)
            v.tensor_scalar_mul(rows["m_row"][:, sl], ps_sx[nch][:], 1.0 / D)
            ins = v.tensor_scalar_mul(rows["e2_row"][:, sl], ps_sq[nch][:], 1.0 / D)
        vsync(ins)
        vsync(v.tensor_mul(rows["m2_row"][:], rows["m_row"][:], rows["m_row"][:]))
        ins = v.tensor_sub(rows["var_row"][:], rows["e2_row"][:], rows["m2_row"][:])
        S.inc("vs", ins)
        s.wait_ge(S["vs"], S.val("vs"))
        ins = s.activation(rows["std_row"][:], rows["var_row"][:], AF.Sqrt, bias=eps_sb[0:1, 0:1])
        S.inc("ss", ins)
        v.wait_ge(S["ss"], S.val("ss"))
        vsync(v.reciprocal(rows["r_row"][:], rows["std_row"][:]))
        ins = v.tensor_mul(rows["mr_row"][:], rows["m_row"][:], rows["r_row"][:])
        S.inc("vs", ins)
        t.wait_ge(S["vs"], S.val("vs"))
        for nch in range(NCH):
            sl = slice(nch * 512, (nch + 1) * 512)
            t.matmul(ps_r[:, sl], ones_row[:], rows["r_row"][:, sl], start=True, stop=True)
            mm = t.matmul(ps_mr[:, sl], ones_row[:], rows["mr_row"][:, sl], start=True, stop=True)
        S.inc("ts", mm)
        v.wait_ge(S["ts"], S.val("ts"))
        v.tensor_copy(r_f[:], ps_r[:])
        vsync(v.tensor_copy(mr_f[:], ps_mr[:]))
        for ct in range(DT):
            vsync(v.tensor_mul(tmpx[ct % 2][:], x_bf[:, ct * TOK:(ct + 1) * TOK], r_f[:]))
            ins = v.tensor_sub(xn[:, ct * TOK:(ct + 1) * TOK], tmpx[ct % 2][:], mr_f[:])
            S.inc("vs", ins)
        vs_xn = S.val("vs")
        sy.wait_ge(S["vs"], vs_xn)
        S.inc("st", sy.dma_start(out=xn1_o.rearrange("(t p) n -> p t n", p=128),
                                 in_=xn[:].rearrange("p (t n) -> p t n", t=DT)), 16)

        ctx_ln.close()
        ctx_ps.close()
        nat_sb = sb("nat_sb", [128, 8 * D], BF16)
        sqk = sb("sqk", [128, TOK], F32)
        bc_f = sb("bc_f", [128, TOK], F32)
        pb = [psum(f"pb{i}", [128, 512]) for i in range(4)]
        ps_t1 = psum("ps_t1", [128, TOK])
        ctx_pbt = ExitStack()
        ps_bt = ctx_pbt.enter_context(nc.psum_tensor("ps_bt", [128, 16], F32))

        t.wait_ge(S["vs"], vs_xn)
        bdone = {}
        for tt in range(8):
            if tt >= 1:
                t.wait_ge(S["vs"], bdone[tt - 1])
            for ct in range(DT):
                mm = t.matmul(ps_bt[:], xn[:, ct * TOK + tt * 128: ct * TOK + (tt + 1) * 128],
                              wb_sb[:, ct * 16:(ct + 1) * 16],
                              start=(ct == 0), stop=(ct == DT - 1))
            tv = S.inc("ts", mm)
            s.wait_ge(S["ts"], tv)
            ins = s.activation(beta_sb[:, tt * 16:(tt + 1) * 16], ps_bt[:], AF.Sigmoid)
            sv = S.inc("ss", ins)
            v.wait_ge(S["ss"], sv)
            ins = v.tensor_scalar_mul(beta_sb[:, tt * 16:(tt + 1) * 16],
                                      beta_sb[:, tt * 16:(tt + 1) * 16], 2.0)
            bdone[tt] = S.inc("vs", ins)
            v.wait_ge(S["vs"], bdone[tt])
        sy.wait_ge(S["vs"], S.val("vs"))
        S.inc("st", sy.dma_start(out=beta_o.rearrange("(t p) n -> p t n", p=128),
                                 in_=beta_sb[:].rearrange("p (t n) -> p t n", t=8)), 16)

        ctx_pbt.close()
        ctx_nrm = ExitStack()
        ps_nrm = [ctx_nrm.enter_context(nc.psum_tensor(f"ps_nrm{i}", [1, 512], F32)) for i in range(2)]

        gdone = {}
        for nch in range(NCH):
            for ct in range(DT):
                mm = t.matmul(ps_t1[:, nch * 512:(nch + 1) * 512],
                              wg1_sb[:, ct * 128:(ct + 1) * 128],
                              xn[:, ct * TOK + nch * 512: ct * TOK + (nch + 1) * 512],
                              start=(ct == 0), stop=(ct == DT - 1))
        tv = S.inc("ts", mm)
        v.wait_ge(S["ts"], tv)
        vsync(v.tensor_copy(t1_sb[:], ps_t1[:]))
        vs_t1 = S.val("vs")
        t.wait_ge(S["vs"], vs_t1)
        ngt = 0
        for mt in range(DT):
            for nch in range(NCH):
                bank = ngt % 4
                if ngt >= 4:
                    t.wait_ge(S["ss"], gdone[ngt - 4])
                mm = t.matmul(pb[bank][:], wg2_sb[:, mt * 128:(mt + 1) * 128],
                              t1_sb[:, nch * 512:(nch + 1) * 512], start=True, stop=True)
                tv = S.inc("ts", mm)
                s.wait_ge(S["ts"], tv)
                ins = s.activation(out_sb[:, mt * TOK + nch * 512: mt * TOK + (nch + 1) * 512],
                                   pb[bank][:], AF.Sigmoid)
                gdone[ngt] = S.inc("ss", ins)
                ngt += 1
        sy.wait_ge(S["ss"], gdone[ngt - 1])
        S.inc("st", sy.dma_start(out=gate_o.rearrange("(t p) n -> p t n", p=128),
                                 in_=out_sb[:].rearrange("p (t n) -> p t n", t=DT)), 16)
        st_gate = S.val("st")

        ctx_w = ExitStack()
        wbuf = [ctx_w.enter_context(nc.sbuf_tensor(f"wbuf{i}", [128, DT * 256], BF16)) for i in range(2)]
        wl_after = {}
        tile_mm = {}
        tile_ev = {}
        n_tile = 0
        g.wait_ge(S["vs"], vs_xn)
        for wi, wimg in enumerate((wq, wk)):
            dst = out_sb if wi == 0 else knT
            for mg in range(8):
                li = wi * 8 + mg
                if li >= 2:
                    g.wait_ge(S["ts"], tile_mm[(li - 2) * 4 + 3])
                wsem = "wl0" if li % 2 == 0 else "wl1"
                ins = g.dma_start(out=wbuf[li % 2][:].rearrange("p (t n) -> p t n", t=DT),
                                  in_=wimg[mg].rearrange("t p n -> p t n"))
                wl_after[li] = S.inc(wsem, ins, 16)
                t.wait_ge(S[wsem], wl_after[li])
                if li == 0:
                    t.wait_ge(S["ss"], S.val("ss"))
                    t.wait_ge(S["st"], st_gate)
                for m2 in range(2):
                    mt = mg * 2 + m2
                    for nch in range(NCH):
                        bank = n_tile % 4
                        if n_tile >= 4:
                            t.wait_ge(S["ss"], tile_ev[n_tile - 4])
                        for ct in range(DT):
                            mm = t.matmul(pb[bank][:],
                                          wbuf[li % 2][:, ct * 256 + m2 * 128: ct * 256 + (m2 + 1) * 128],
                                          xn[:, ct * TOK + nch * 512: ct * TOK + (nch + 1) * 512],
                                          start=(ct == 0), stop=(ct == DT - 1))
                        tile_mm[n_tile] = S.inc("ts", mm)
                        s.wait_ge(S["ts"], tile_mm[n_tile])
                        ins = s.activation(dst[:, mt * TOK + nch * 512: mt * TOK + (nch + 1) * 512],
                                           pb[bank][:], AF.Silu)
                        tile_ev[n_tile] = S.inc("ss", ins)
                        n_tile += 1
            if wi == 0:
                sy.wait_ge(S["ss"], tile_ev[n_tile - 1])
                S.inc("st", sy.dma_start(out=qT_o.rearrange("(t p) n -> p t n", p=128),
                                         in_=out_sb[:].rearrange("p (t n) -> p t n", t=DT)), 16)
        ss_k = tile_ev[n_tile - 1]

        kn_done = {}
        for hb in range(H16):
            s.wait_ge(S["ss"], ss_k)
            if hb > 0:
                s.wait_ge(S["vs"], kn_done[hb - 1])
            ins = s.activation(sqk[:], knT[:, hb * TOK:(hb + 1) * TOK], AF.Square)
            sv = S.inc("ss", ins)
            t.wait_ge(S["ss"], sv)
            for nch in range(NCH):
                mm = t.matmul(ps_nrm[nch][:], ones_colf[:],
                              sqk[:, nch * 512:(nch + 1) * 512], start=True, stop=True)
            tv = S.inc("ts", mm)
            s.wait_ge(S["ts"], tv)
            for nch in range(NCH):
                ins = s.activation(rows["m_row"][:, nch * 512:(nch + 1) * 512],
                                   ps_nrm[nch][:], AF.Sqrt, bias=eps_sb[0:1, 1:2])
            sv = S.inc("ss", ins)
            v.wait_ge(S["ss"], sv)
            vsync(v.reciprocal(rows["e2_row"][:], rows["m_row"][:]))
            vs_rn = S.val("vs")
            t.wait_ge(S["vs"], vs_rn)
            for nch in range(NCH):
                mm = t.matmul(ps_t1[:, nch * 512:(nch + 1) * 512], ones_row[:],
                              rows["e2_row"][:, nch * 512:(nch + 1) * 512], start=True, stop=True)
            tv = S.inc("ts", mm)
            v.wait_ge(S["ts"], tv)
            vsync(v.tensor_copy(bc_f[:], ps_t1[:]))
            ins = v.tensor_mul(knT[:, hb * TOK:(hb + 1) * TOK],
                               knT[:, hb * TOK:(hb + 1) * TOK], bc_f[:])
            kn_done[hb] = S.inc("vs", ins)
            v.wait_ge(S["vs"], kn_done[hb])
        sy.wait_ge(S["vs"], S.val("vs"))
        S.inc("st", sy.dma_start(out=kT_o.rearrange("(t p) n -> p t n", p=128),
                                 in_=knT[:].rearrange("p (t n) -> p t n", t=DT)), 16)

        ctx_nrm.close()
        ps_tr = [ctx.enter_context(nc.psum_tensor(f"ps_tr{i}", [128, 1024], BF16)) for i in range(2)]
        tr_ev = {}
        ntr = 0
        for tt in range(8):
            for hb in range(H16):
                bank = ntr % 2
                if ntr >= 2:
                    t.wait_ge(S["vs"], tr_ev[ntr - 2])
                mm = t.transpose(ps_tr[bank][:, 0:128],
                                 knT[:, hb * TOK + tt * 128: hb * TOK + (tt + 1) * 128],
                                 ident[:])
                tv = S.inc("ts", mm)
                v.wait_ge(S["ts"], tv)
                ins = v.tensor_copy(nat_sb[:, tt * D + hb * 128: tt * D + (hb + 1) * 128],
                                    ps_tr[bank][:, 0:128])
                tr_ev[ntr] = S.inc("vs", ins)
                ntr += 1
        sy.wait_ge(S["vs"], S.val("vs"))
        ins = sy.dma_start(out=kn_o.rearrange("(t p) n -> p t n", p=128),
                           in_=nat_sb[:].rearrange("p (t n) -> p t n", t=8))
        st_kn = S.inc("st", ins, 16)

        ctx_w.close()
        ctx_w2 = ExitStack()
        wvbuf = [ctx_w2.enter_context(nc.sbuf_tensor("wvbuf0", [128, DT * 512], BF16))] * 2
        v_mm = {}
        v_ev = {}
        n_vt = 0
        g.wait_ge(S["ts"], S.val("ts"))
        for dn in range(4):
            if dn >= 1:
                g.wait_ge(S["ts"], v_mm[(dn - 1) * 8 + 7])
            ins = g.dma_start(out=wvbuf[dn % 2][:].rearrange("p (t n) -> p t n", t=DT),
                              in_=wv[dn].rearrange("t p n -> p t n"))
            wlv = S.inc("wl0", ins, 16)
            t.wait_ge(S["wl0"], wlv)
            if dn == 0:
                t.wait_ge(S["vs"], S.val("vs"))
                t.wait_ge(S["st"], st_kn)
            for tt in range(8):
                bank = n_vt % 4
                if n_vt >= 4:
                    t.wait_ge(S["vs"], v_ev[n_vt - 4])
                for ct in range(DT):
                    mm = t.matmul(pb[bank][:],
                                  xn[:, ct * TOK + tt * 128: ct * TOK + (tt + 1) * 128],
                                  wvbuf[dn % 2][:, ct * 512:(ct + 1) * 512],
                                  start=(ct == 0), stop=(ct == DT - 1))
                v_mm[n_vt] = S.inc("ts", mm)
                v.wait_ge(S["ts"], v_mm[n_vt])
                ins = v.tensor_copy(nat_sb[:, tt * D + dn * 512: tt * D + (dn + 1) * 512],
                                    pb[bank][:])
                v_ev[n_vt] = S.inc("vs", ins)
                n_vt += 1
        sy.wait_ge(S["vs"], v_ev[n_vt - 1])
        S.inc("st", sy.dma_start(out=v_o.rearrange("(t p) n -> p t n", p=128),
                                 in_=nat_sb[:].rearrange("p (t n) -> p t n", t=8)), 16)
        sy.wait_ge(S["st"], S.val("st"))

    ctx_w2.close()
    ctx.close()
    return nc


def prep_C(inputs, b_results):
    wq_img = img_lhsT(_f32(inputs["L1_attn_Wq"]).T)
    wk_img = img_lhsT(_f32(inputs["L1_attn_Wk"]).T)
    wv_img = img_rhs(_f32(inputs["L1_attn_Wv"]).T)
    wb_img = _f32(inputs["L1_attn_Wbeta"]).T.reshape(DT, 128, 16)
    wg1_img = _f32(inputs["L1_attn_Wg1"]).T.reshape(DT, 128, 128)
    wg2_img = _f32(inputs["L1_attn_Wg2"]).T
    in_maps = []
    for c in range(NCORES):
        in_maps.append({
            "x1T": np.asarray(b_results[c]["x1T_o"]),
            "wq": wq_img, "wk": wk_img, "wv": wv_img,
            "wb": _f32(wb_img), "wg1": _f32(wg1_img), "wg2": _f32(wg2_img),
        })
    return in_maps





def build_launch_D():
    NBH = 4
    NCK = 32
    nc = bass.Bass("TRN2", target_bir_lowering=False)
    qT_i = nc.dram_tensor("qT_i", [NBH, 128, SEQ], BF16, kind="ExternalInput").ap()
    kT_i = nc.dram_tensor("kT_i", [NBH, 128, SEQ], BF16, kind="ExternalInput").ap()
    kn_i = nc.dram_tensor("kn_i", [NBH, NCK, 128, 128], BF16, kind="ExternalInput").ap()
    vn_i = nc.dram_tensor("vn_i", [NBH, NCK, 128, 128], BF16, kind="ExternalInput").ap()
    beta_i = nc.dram_tensor("beta_i", [NBH, 128, NCK], F32, kind="ExternalInput").ap()
    o_o = nc.dram_tensor("o_o", [NBH, NCK, 128, 128], BF16, kind="ExternalOutput").ap()

    lmask_c = nc.inline_tensor(np.tril(np.ones((128, 128), np.float32), -1).astype(BF), "lmask_c").ap()
    umaski_c = nc.inline_tensor(np.triu(np.ones((128, 128), np.float32), 0).astype(BF), "umaski_c").ap()
    _bd = np.zeros((128, 128), np.float32); _bd[:64, :64] = 1; _bd[64:, 64:] = 1
    bd64_c = nc.inline_tensor(_bd.astype(BF), "bd64_c").ap()
    identf_c = nc.inline_tensor(np.eye(128, dtype=np.float32), "identf_c").ap()

    ctx = ExitStack()
    sb = lambda name, shp, dt: ctx.enter_context(nc.sbuf_tensor(name, shp, dt))

    lmask = sb("lmask", [128, 128], BF16)
    umaski = sb("umaski", [128, 128], BF16)
    bd64 = sb("bd64", [128, 128], BF16)
    identf = sb("identf", [128, 128], F32)
    qT = sb("qT", [128, NBH * SEQ], BF16)
    kT = sb("kT", [128, NBH * SEQ], BF16)
    kn = sb("kn", [128, NBH * NCK * 128], BF16)
    vn = sb("vn", [128, NBH * NCK * 128], BF16)
    beta = sb("beta", [128, NBH * NCK], F32)
    S_sb = sb("S_sb", [128, NBH * 128], BF16)
    S_f32 = sb("S_f32", [128, NBH * 128], F32)
    t0f = sb("t0f", [128, 128], F32)
    rhs0 = sb("rhs0", [128, 128], F32)
    A_f = sb("A_f", [128, 128], F32)
    At_f = sb("At_f", [128, 128], F32)
    Db_f = sb("Db_f", [128, 128], F32)
    Pj = [sb(f"Pj{i}", [128, 128], F32) for i in range(2)]
    Ptld = [sb(f"Ptld{i}", [128, 128], F32) for i in range(6)]
    tmpu = [sb(f"tmpu{i}", [128, 128], F32) for i in range(2)]
    wfull = sb("wfull", [128, 128], F32)
    U_bf = sb("U_bf", [128, 128], BF16)
    lqk = sb("lqk", [128, 128], BF16)
    o_bf = [sb(f"o_bf{i}", [128, 128], BF16) for i in range(2)]

    psum = lambda name, shp, dt=F32: ctx.enter_context(nc.psum_tensor(name, shp, dt))
    ps_S = psum("ps_S", [128, 512])
    ps_w = [psum(f"ps_w{i}", [128, 512]) for i in range(2)]
    ps_ob = [psum(f"ps_ob{i}", [128, 512]) for i in range(2)]
    ps_trf = psum("ps_trf", [128, 512], F32)

    S = Sems(nc, ctx, ["ld", "ts", "vs", "ss", "st", "sx"])
    g, t, v, s, sy = nc.gpsimd, nc.tensor, nc.vector, nc.scalar, nc.sync

    def vsync(ins):
        v.wait_ge(S["vs"], S.inc("vs", ins))

    def tsy(mm):
        tv = S.inc("ts", mm)
        return tv

    with nc.Block():
        S.inc("ld", g.dma_start(out=lmask[:], in_=lmask_c), 16)
        S.inc("ld", g.dma_start(out=umaski[:], in_=umaski_c), 16)
        S.inc("ld", g.dma_start(out=bd64[:], in_=bd64_c), 16)
        S.inc("ld", g.dma_start(out=identf[:], in_=identf_c), 16)
        S.inc("ld", g.dma_start(out=qT[:].rearrange("p (b n) -> p b n", b=NBH),
                                in_=qT_i.rearrange("b p n -> p b n")), 16)
        S.inc("ld", g.dma_start(out=kT[:].rearrange("p (b n) -> p b n", b=NBH),
                                in_=kT_i.rearrange("b p n -> p b n")), 16)
        for bh_ in range(NBH):
            S.inc("ld", g.dma_start(
                out=kn[:, bh_ * NCK * 128:(bh_ + 1) * NCK * 128].rearrange("p (c n) -> p c n", c=NCK),
                in_=kn_i[bh_].rearrange("c p n -> p c n")), 16)
            S.inc("ld", g.dma_start(
                out=vn[:, bh_ * NCK * 128:(bh_ + 1) * NCK * 128].rearrange("p (c n) -> p c n", c=NCK),
                in_=vn_i[bh_].rearrange("c p n -> p c n")), 16)
        S.inc("ld", g.dma_start(out=beta[:].rearrange("p (b n) -> p b n", b=NBH),
                                in_=beta_i.rearrange("b p n -> p b n")), 16)
        lds0 = S.val("ld")
        t.wait_ge(S["ld"], lds0)
        v.wait_ge(S["ld"], lds0)
        s.wait_ge(S["ld"], lds0)
        vsync(v.memset(S_sb[:], 0.0))
        vsync(v.memset(S_f32[:], 0.0))

        o_ev = {}
        o_st = {}
        for bh in range(NBH):
            Ssl = S_sb[:, bh * 128:(bh + 1) * 128]
            pSsl = ps_S[:, bh * 128:(bh + 1) * 128]
            for ck in range(NCK):
                kTc = kT[:, bh * SEQ + ck * 128: bh * SEQ + (ck + 1) * 128]
                qTc = qT[:, bh * SEQ + ck * 128: bh * SEQ + (ck + 1) * 128]
                knc = kn[:, (bh * NCK + ck) * 128: (bh * NCK + ck + 1) * 128]
                vnc = vn[:, (bh * NCK + ck) * 128: (bh * NCK + ck + 1) * 128]
                bcol = beta[:, bh * NCK + ck: bh * NCK + ck + 1]
                t.wait_ge(S["vs"], S.val("vs"))
                tv = tsy(t.matmul(ps_w[0][:, 0:128], kTc, Ssl, start=True, stop=True))
                v.wait_ge(S["ts"], tv)
                v.wait_ge(S["ss"], S.val("ss"))
                ins = v.tensor_sub(t0f[:], vnc, ps_w[0][:, 0:128])
                sv = S.inc("vs", ins)
                s.wait_ge(S["vs"], sv)
                ins = s.mul(rhs0[:], t0f[:], bcol)
                rhs0_v = S.inc("ss", ins)
                tv = tsy(t.matmul(ps_w[1][:, 0:128], kTc, kTc, start=True, stop=True))
                v.wait_ge(S["ts"], tv)
                v.wait_ge(S["ss"], rhs0_v)
                ins = v.tensor_mul(t0f[:], ps_w[1][:, 0:128], lmask[:])
                sv = S.inc("vs", ins)
                s.wait_ge(S["vs"], sv)
                ins = s.mul(A_f[:], t0f[:], bcol)
                A_v = S.inc("ss", ins)
                t.wait_ge(S["ss"], A_v)
                tv = tsy(t.transpose(ps_trf[:, 0:128], A_f[:], identf[:]))
                v.wait_ge(S["ts"], tv)
                ins = v.tensor_copy(At_f[:], ps_trf[:, 0:128])
                vsync(ins)
                ins = v.tensor_mul(Db_f[:], A_f[:], bd64[:])
                vsync(ins)
                ins = v.tensor_mul(Ptld[0][:], At_f[:], bd64[:])
                vsync(ins)
                pj_cur = Db_f
                for j in range(5):
                    t.wait_ge(S["vs"], S.val("vs"))
                    tsy(t.matmul(ps_w[0][:, 128:256], Ptld[j][:], pj_cur[:], start=True, stop=True))
                    tv2 = tsy(t.matmul(ps_w[1][:, 128:256], pj_cur[:], Ptld[j][:], start=True, stop=True))
                    v.wait_ge(S["ts"], tv2)
                    ins = v.tensor_copy(Pj[(j + 1) % 2][:], ps_w[0][:, 128:256])
                    vsync(ins)
                    ins = v.tensor_copy(Ptld[j + 1][:], ps_w[1][:, 128:256])
                    vsync(ins)
                    pj_cur = Pj[(j + 1) % 2]

                def applyinv(dst_rows, src_ap, rows):
                    tcur = src_ap
                    for idx, j in enumerate([5, 4, 3, 2, 1, 0]):
                        t.wait_ge(S["vs"], S.val("vs"))
                        t.wait_ge(S["ss"], S.val("ss"))
                        pw = ps_w[idx % 2]
                        tv = tsy(t.matmul(pw[rows, 256:384], Ptld[j][rows, rows],
                                          tcur, start=True, stop=True))
                        v.wait_ge(S["ts"], tv)
                        tnew = tmpu[idx % 2][rows, :]
                        if j != 0:
                            ins = v.tensor_add(tnew, tcur, pw[rows, 256:384])
                        else:
                            ins = v.tensor_sub(tnew, tcur, pw[rows, 256:384])
                        vsync(ins)
                        tcur = tnew
                    return tcur

                full = slice(0, 128)
                s.wait_ge(S["ss"], rhs0_v)
                wend = applyinv(None, rhs0[:], full)
                v.wait_ge(S["vs"], S.val("vs"))
                ins = v.tensor_copy(wfull[:], wend)
                vsync(ins)
                t.wait_ge(S["vs"], S.val("vs"))
                tv = tsy(t.matmul(ps_w[1][64:128, 256:384], At_f[0:64, 64:128],
                                  wfull[0:64, :], start=True, stop=True))
                v.wait_ge(S["ts"], tv)
                bot = slice(64, 128)
                ins = v.tensor_sub(t0f[bot, :], rhs0[bot, :], ps_w[1][64:128, 256:384])
                vsync(ins)
                w2end = applyinv(None, t0f[bot, :], bot)
                v.wait_ge(S["vs"], S.val("vs"))
                ins = v.tensor_copy(U_bf[0:64, :], wfull[0:64, :])
                vsync(ins)
                ins = v.tensor_copy(U_bf[bot, :], w2end)
                vsync(ins)
                t.wait_ge(S["vs"], S.val("vs"))
                tv = tsy(t.matmul(ps_w[0][:, 384:512], kTc, qTc, start=True, stop=True))
                v.wait_ge(S["ts"], tv)
                ins = v.tensor_mul(lqk[:], ps_w[0][:, 384:512], umaski[:])
                lqk_v = S.inc("vs", ins)
                t.wait_ge(S["vs"], lqk_v)
                ob = ps_ob[ck % 2]
                if ck >= 2:
                    t.wait_ge(S["vs"], o_ev[(bh, ck - 2)])
                t.matmul(ob[:, 0:128], qTc, Ssl, start=True, stop=False)
                tv = tsy(t.matmul(ob[:, 0:128], lqk[:], U_bf[:], start=False, stop=True))
                v.wait_ge(S["ts"], tv)
                prev = (bh, ck - 2) if ck >= 2 else ((bh - 1, NCK - 2 + ck) if bh else None)
                if prev in o_st:
                    v.wait_ge(S["st" if prev[1] % 2 == 0 else "sx"], o_st[prev])
                ins = v.tensor_copy(o_bf[ck % 2][:], ob[:, 0:128])
                o_ev[(bh, ck)] = S.inc("vs", ins)
                sy.wait_ge(S["vs"], o_ev[(bh, ck)])
                ssem = "st" if ck % 2 == 0 else "sx"
                ins = sy.dma_start(out=o_o[bh, ck], in_=o_bf[ck % 2][:])
                o_st[(bh, ck)] = S.inc(ssem, ins, 16)
                if ck + 1 < NCK:
                    fsl = S_f32[:, bh * 128:(bh + 1) * 128]
                    tv = tsy(t.matmul(pSsl, knc, U_bf[:], start=True, stop=True))
                    v.wait_ge(S["ts"], tv)
                    vsync(v.tensor_add(fsl, fsl, pSsl))
                    vsync(v.tensor_copy(Ssl, fsl))
        sy.wait_ge(S["st"], S.val("st"))
        sy.wait_ge(S["sx"], S.val("sx"))

    ctx.close()
    return nc


def prep_D(inputs, c_results):
    qT_full, kT_full, kn_full, vn_full, beta_full = [], [], [], [], []
    for b in range(NB):
        qT_full.append(np.concatenate([np.asarray(c_results[b * 4 + i]["qT_o"]) for i in range(4)], axis=1))
        kT_full.append(np.concatenate([np.asarray(c_results[b * 4 + i]["kT_o"]) for i in range(4)], axis=1))
        kn_full.append(np.concatenate([np.asarray(c_results[b * 4 + i]["kn_o"]) for i in range(4)], axis=0))
        vn_full.append(np.concatenate([np.asarray(c_results[b * 4 + i]["v_o"]) for i in range(4)], axis=0))
        beta_full.append(np.concatenate([np.asarray(c_results[b * 4 + i]["beta_o"]) for i in range(4)], axis=0))
    in_maps = []
    for c in range(NCORES):
        bhs = [(b, h) for b in range(NB) for h in (2 * c, 2 * c + 1)]
        qT = np.stack([qT_full[b][h * 128:(h + 1) * 128, :] for (b, h) in bhs])
        kT = np.stack([kT_full[b][h * 128:(h + 1) * 128, :] for (b, h) in bhs])
        kn = np.stack([kn_full[b][:, h * 128:(h + 1) * 128].reshape(32, 128, 128) for (b, h) in bhs])
        vn = np.stack([vn_full[b][:, h * 128:(h + 1) * 128].reshape(32, 128, 128) for (b, h) in bhs])
        bt = np.stack([_f32(beta_full[b][:, h].reshape(32, 128).T) for (b, h) in bhs])
        in_maps.append({"qT_i": np.ascontiguousarray(qT), "kT_i": np.ascontiguousarray(kT),
                        "kn_i": np.ascontiguousarray(kn), "vn_i": np.ascontiguousarray(vn),
                        "beta_i": bt})
    return in_maps





def build_launch_W():
    CW = 256
    nc = bass.Bass("TRN2", target_bir_lowering=False)
    w2c = nc.dram_tensor("w2c", [64, 128, CW], F32, kind="ExternalInput").ap()
    w2f_o = nc.dram_tensor("w2f_o", [64, 128, CW], BF16, kind="ExternalOutput").ap()
    h128_c = nc.inline_tensor(_hadamard(128).astype(BF), "h128_c").ap()

    ctx = ExitStack()
    sb = lambda name, shp, dt: ctx.enter_context(nc.sbuf_tensor(name, shp, dt))
    h128 = sb("h128", [128, 128], BF16)
    win = sb("win", [128, 64 * CW], BF16)
    y0 = sb("y0", [128, 64 * CW], F32)
    y1 = sb("y1", [128, 64 * CW], F32)
    ps = [ctx.enter_context(nc.psum_tensor(f"psw{i}", [128, 512], F32)) for i in range(2)]
    S = Sems(nc, ctx, ["ld", "ts", "vs", "st"])
    g, t, v, sy = nc.gpsimd, nc.tensor, nc.vector, nc.sync

    with nc.Block():
        S.inc("ld", g.dma_start(out=h128[:], in_=h128_c), 16)
        S.inc("ld", g.dma_start(out=win[:].rearrange("p (t n) -> p t n", t=64),
                                in_=w2c.rearrange("t p n -> p t n")), 16)
        lds0 = S.val("ld")
        t.wait_ge(S["ld"], lds0)
        ev = {}
        for ct in range(64):
            bank = ct % 2
            if ct >= 2:
                t.wait_ge(S["vs"], ev[ct - 2])
            mm = t.matmul(ps[bank][:, 0:CW], h128[:], win[:, ct * CW:(ct + 1) * CW],
                          start=True, stop=True)
            tv = S.inc("ts", mm)
            v.wait_ge(S["ts"], tv)
            ins = v.tensor_copy(y0[:, ct * CW:(ct + 1) * CW], ps[bank][:, 0:CW])
            ev[ct] = S.inc("vs", ins)
        cur, nxt = y0, y1
        for dstep in (1, 2, 4, 8, 16, 32):
            v.wait_ge(S["vs"], S.val("vs"))
            for blk in range(0, 64, 2 * dstep):
                for a in range(blk, blk + dstep):
                    b = a + dstep
                    asl = slice(a * CW, (a + 1) * CW)
                    bsl = slice(b * CW, (b + 1) * CW)
                    v.tensor_add(nxt[:, asl], cur[:, asl], cur[:, bsl])
                    ins = v.tensor_sub(nxt[:, bsl], cur[:, asl], cur[:, bsl])
            S.inc("vs", ins)
            cur, nxt = nxt, cur
        vs_done = S.val("vs")
        sy.wait_ge(S["vs"], vs_done)
        g.wait_ge(S["vs"], vs_done)
        S.inc("st", g.dma_start(out=w2f_o.rearrange("t p n -> p t n"),
                                in_=cur[:].rearrange("p (t n) -> p t n", t=64)), 16)
        sy.wait_ge(S["st"], S.val("st"))
        g.wait_ge(S["st"], S.val("st"))
    ctx.close()
    return nc


def _hadamard(n):
    h = np.array([[1.0]], np.float32)
    while h.shape[0] < n:
        h = np.block([[h, h], [h, -h]])
    return h


def prep_W(inputs):
    w2T = _f32(inputs["L1_ffn_W2"]).T
    in_maps = []
    for c in range(NCORES):
        sl = w2T[:, c * 256:(c + 1) * 256].reshape(64, 128, 256)
        in_maps.append({"w2c": _f32(sl)})
    return in_maps



def build_launch_E():
    TOK = 1024
    NCH = TOK // 512
    nc = bass.Bass("TRN2", target_bir_lowering=False)
    x1T = nc.dram_tensor("x1T", [D, TOK], F32, kind="ExternalInput").ap()
    xn1 = nc.dram_tensor("xn1", [D, TOK], BF16, kind="ExternalInput").ap()
    oT_i = nc.dram_tensor("oT_i", [D, TOK], BF16, kind="ExternalInput").ap()
    gate_i = nc.dram_tensor("gate_i", [D, TOK], BF16, kind="ExternalInput").ap()
    gn = nc.dram_tensor("gn", [128, DT], F32, kind="ExternalInput").ap()
    wo = nc.dram_tensor("wo", [8, DT, 128, 256], F32, kind="ExternalInput").ap()
    w1 = nc.dram_tensor("w1", [32, DT, 128, 256], F32, kind="ExternalInput").ap()
    w3 = nc.dram_tensor("w3", [32, DT, 128, 256], F32, kind="ExternalInput").ap()
    w2f = nc.dram_tensor("w2f", [DT, 64, 128, 128], BF16, kind="ExternalInput").ap()
    x2T_o = nc.dram_tensor("x2T_o", [D, TOK], F32, kind="ExternalOutput").ap()

    ones_row_c = nc.inline_tensor(np.ones((1, 128), np.float32), "ones_row_c").ap()
    ctx = ExitStack()
    sb = lambda name, shp, dt: ctx.enter_context(nc.sbuf_tensor(name, shp, dt))
    psum = lambda name, shp, dt=F32: ctx.enter_context(nc.psum_tensor(name, shp, dt))

    ones_row = sb("ones_row", [1, 128], F32)
    ones_colf = sb("ones_colf", [128, 1], F32)
    eps_sb = sb("eps_sb", [128, 1], F32)
    xn_sb = sb("xn_sb", [128, DT * TOK], BF16)
    s1 = sb("s1", [128, DT * TOK], BF16)
    gn_sb = sb("gn_sb", [128, DT], F32)
    rowb = sb("rowb", [1, 2 * TOK], F32)
    bc_f = sb("bc_f", [128, TOK], F32)

    S = Sems(nc, ctx, ["ld", "wl0", "wl1", "ts", "vs", "ss", "st", "sx"])
    g, t, v, s, sy = nc.gpsimd, nc.vector and nc.tensor, nc.vector, nc.scalar, nc.sync
    t = nc.tensor

    def vsync(ins):
        v.wait_ge(S["vs"], S.inc("vs", ins))

    with nc.Block():
        ctx1 = ExitStack()
        oT = ctx1.enter_context(nc.sbuf_tensor("oT", [128, DT * TOK], BF16))
        gate = ctx1.enter_context(nc.sbuf_tensor("gate", [128, DT * TOK], BF16))
        ogn = ctx1.enter_context(nc.sbuf_tensor("ogn", [128, DT * TOK], BF16))
        tmb = ctx1.enter_context(nc.sbuf_tensor("tmb", [128, TOK], BF16))
        sqe = ctx1.enter_context(nc.sbuf_tensor("sqe", [128, TOK], F32))
        t2f = ctx1.enter_context(nc.sbuf_tensor("t2f", [128, TOK], F32))
        wbuf = [ctx1.enter_context(nc.sbuf_tensor(f"wbuf{i}", [128, DT * 256], BF16)) for i in range(2)]
        ctxp1 = ExitStack()
        ps_n = [ctxp1.enter_context(nc.psum_tensor(f"ps_n{i}", [1, 512], F32)) for i in range(2)]
        ps_bc = ctxp1.enter_context(nc.psum_tensor("ps_bc", [128, TOK], F32))
        pb = [ctxp1.enter_context(nc.psum_tensor(f"pb{i}", [128, 512], F32)) for i in range(4)]

        g.memset(eps_sb[:], EPS_L1)
        g.memset(ones_colf[:], 1.0)
        S.inc("ld", g.dma_start(out=ones_row[:], in_=ones_row_c), 16)
        S.inc("ld", g.dma_start(out=xn_sb[:].rearrange("p (t n) -> p t n", t=DT),
                                in_=xn1.rearrange("(t p) n -> p t n", p=128)), 16)
        S.inc("ld", g.dma_start(out=oT[:].rearrange("p (t n) -> p t n", t=DT),
                                in_=oT_i.rearrange("(t p) n -> p t n", p=128)), 16)
        S.inc("ld", g.dma_start(out=gate[:].rearrange("p (t n) -> p t n", t=DT),
                                in_=gate_i.rearrange("(t p) n -> p t n", p=128)), 16)
        S.inc("ld", g.dma_start(out=gn_sb[:], in_=gn), 16)
        lds0 = S.val("ld")
        v.wait_ge(S["ld"], lds0)
        s.wait_ge(S["ld"], lds0)
        t.wait_ge(S["ld"], lds0)
        ogn_done = 0
        for hb in range(H16):
            hs = slice(hb * TOK, (hb + 1) * TOK)
            if hb > 0:
                v.wait_ge(S["ss"], ogn_done)
            vsync(v.tensor_mul(tmb[:], oT[:, hs], gate[:, hs]))
            s.wait_ge(S["vs"], S.val("vs"))
            ins = s.activation(sqe[:], tmb[:], AF.Square)
            sv = S.inc("ss", ins)
            t.wait_ge(S["ss"], sv)
            for nch in range(NCH):
                mm = t.matmul(ps_n[nch][:], ones_colf[:],
                              sqe[:, nch * 512:(nch + 1) * 512], start=True, stop=True)
            tv = S.inc("ts", mm)
            s.wait_ge(S["ts"], tv)
            for nch in range(NCH):
                ins = s.activation(rowb[:, nch * 512:(nch + 1) * 512], ps_n[nch][:],
                                   AF.Sqrt, bias=eps_sb[0:1, 0:1], scale=1.0 / 128.0)
            sv = S.inc("ss", ins)
            v.wait_ge(S["ss"], sv)
            vsync(v.reciprocal(rowb[:, TOK:TOK + TOK], rowb[:, 0:TOK]))
            t.wait_ge(S["vs"], S.val("vs"))
            for nch in range(NCH):
                mm = t.matmul(ps_bc[:, nch * 512:(nch + 1) * 512], ones_row[:],
                              rowb[:, TOK + nch * 512: TOK + (nch + 1) * 512],
                              start=True, stop=True)
            tv = S.inc("ts", mm)
            v.wait_ge(S["ts"], tv)
            vsync(v.tensor_copy(bc_f[:], ps_bc[:]))
            ins = v.tensor_mul(t2f[:], tmb[:], bc_f[:])
            vsync(ins)
            s.wait_ge(S["vs"], S.val("vs"))
            ins = s.mul(ogn[:, hs], t2f[:], gn_sb[:, hb:hb + 1])
            ogn_done = S.inc("ss", ins)
        ss_ogn = S.val("ss")
        n_tile = 0
        womm = {}
        woev = {}
        t.wait_ge(S["ss"], ss_ogn)
        for mg in range(8):
            if mg >= 2:
                g.wait_ge(S["ts"], womm[(mg - 2) * 4 + 3])
            else:
                g.wait_ge(S["ld"], lds0)
            wsem = "wl0" if mg % 2 == 0 else "wl1"
            ins = g.dma_start(out=wbuf[mg % 2][:].rearrange("p (t n) -> p t n", t=DT),
                              in_=wo[mg].rearrange("t p n -> p t n"))
            wlv = S.inc(wsem, ins, 16)
            t.wait_ge(S[wsem], wlv)
            for m2 in range(2):
                mt = mg * 2 + m2
                for nch in range(NCH):
                    bank = n_tile % 4
                    if n_tile >= 4:
                        t.wait_ge(S["ss"], woev[n_tile - 4])
                    for ct in range(DT):
                        mm = t.matmul(pb[bank][:],
                                      wbuf[mg % 2][:, ct * 256 + m2 * 128: ct * 256 + (m2 + 1) * 128],
                                      ogn[:, ct * TOK + nch * 512: ct * TOK + (nch + 1) * 512],
                                      start=(ct == 0), stop=(ct == DT - 1))
                    womm[n_tile] = S.inc("ts", mm)
                    s.wait_ge(S["ts"], womm[n_tile])
                    ins = s.activation(s1[:, mt * TOK + nch * 512: mt * TOK + (nch + 1) * 512],
                                       pb[bank][:], AF.Identity)
                    woev[n_tile] = S.inc("ss", ins)
                    n_tile += 1
        ss_s1 = S.val("ss")
        g.wait_ge(S["ss"], ss_s1)
        ctxp1.close()
        ctx1.close()

        ctx2 = ExitStack()
        hT = ctx2.enter_context(nc.sbuf_tensor("hT", [128, 64 * 512], BF16))
        w1b = [ctx2.enter_context(nc.sbuf_tensor(f"w1b{i}", [128, DT * 256], BF16)) for i in range(2)]
        w3b = [ctx2.enter_context(nc.sbuf_tensor(f"w3b{i}", [128, DT * 256], BF16)) for i in range(2)]
        w2b = [ctx2.enter_context(nc.sbuf_tensor("w2b0", [128, 64 * 128], BF16))] * 2
        sgt = ctx2.enter_context(nc.sbuf_tensor("sgt", [128, 512], BF16))
        g1t = ctx2.enter_context(nc.sbuf_tensor("g1t", [128, 512], BF16))
        x1b = [ctx2.enter_context(nc.sbuf_tensor(f"x1b{i}", [128, 512], F32)) for i in range(2)]
        x2b = [ctx2.enter_context(nc.sbuf_tensor(f"x2b{i}", [128, 512], F32)) for i in range(2)]
        ctxp2 = ExitStack()
        pb2 = [ctxp2.enter_context(nc.psum_tensor(f"pb2{i}", [128, 512], F32)) for i in range(4)]

        SC = float(1.0 / np.sqrt(8192.0))
        st_mm = {}
        x2_ev = {}
        g1_used = {}
        for th in range(2):
            tsl = slice(th * 512, (th + 1) * 512)
            hmm = {}
            hev = {}
            for mg in range(32):
                li = th * 32 + mg
                wsem = "wl0" if mg % 2 == 0 else "wl1"
                if mg >= 2:
                    g.wait_ge(S["ts"], hmm[(mg - 2, 1, "u")])
                elif th == 1:
                    g.wait_ge(S["ts"], S.val("ts"))
                ins = g.dma_start(out=w1b[mg % 2][:].rearrange("p (t n) -> p t n", t=DT),
                                  in_=w1[mg].rearrange("t p n -> p t n"))
                S.inc(wsem, ins, 16)
                ins = g.dma_start(out=w3b[mg % 2][:].rearrange("p (t n) -> p t n", t=DT),
                                  in_=w3[mg].rearrange("t p n -> p t n"))
                wlv = S.inc(wsem, ins, 16)
                t.wait_ge(S[wsem], wlv)
                for m2 in range(2):
                    mt = mg * 2 + m2
                    bg, bu = (mt % 2) * 2, (mt % 2) * 2 + 1
                    if th == 1 and mg == 0:
                        t.wait_ge(S["vs"], S.val("vs"))
                    if (mg, m2) >= (1, 0):
                        prevmt = mt - 2
                        if prevmt >= 0:
                            t.wait_ge(S["ss"], hev[(prevmt, "sg")])
                            t.wait_ge(S["vs"], hev[(prevmt, "h")])
                    for ct in range(DT):
                        mm = t.matmul(pb2[bg][:], w1b[mg % 2][:, ct * 256 + m2 * 128: ct * 256 + (m2 + 1) * 128],
                                      xn_sb[:, ct * TOK + th * 512: ct * TOK + th * 512 + 512],
                                      start=(ct == 0), stop=(ct == DT - 1))
                    hmm[(mg, m2, "g")] = S.inc("ts", mm)
                    for ct in range(DT):
                        mm = t.matmul(pb2[bu][:], w3b[mg % 2][:, ct * 256 + m2 * 128: ct * 256 + (m2 + 1) * 128],
                                      xn_sb[:, ct * TOK + th * 512: ct * TOK + th * 512 + 512],
                                      start=(ct == 0), stop=(ct == DT - 1))
                    hmm[(mg, m2, "u")] = S.inc("ts", mm)
                    s.wait_ge(S["ts"], hmm[(mg, m2, "u")])
                    if (mt - 1, "h") in hev:
                        s.wait_ge(S["vs"], hev[(mt - 1, "h")])
                    ins = s.activation(sgt[:], pb2[bg][:], AF.Silu)
                    hev[(mt, "sg")] = S.inc("ss", ins)
                    v.wait_ge(S["ss"], hev[(mt, "sg")])
                    ins = v.tensor_mul(hT[:, mt * 512:(mt + 1) * 512], sgt[:], pb2[bu][:])
                    hev[(mt, "h")] = S.inc("vs", ins)
            vs_h = S.val("vs")
            x1_ld = {}
            x1_ld[0] = S.inc("ld", g.dma_start(out=x1b[0][:], in_=x1T[0:128, tsl]), 16)
            x1_ld[1] = S.inc("ld", g.dma_start(out=x1b[1][:], in_=x1T[128:256, tsl]), 16)
            st_x2 = {}
            for mt in range(DT):
                wsem = "wl0" if mt % 2 == 0 else "wl1"
                if mt >= 1:
                    g.wait_ge(S["ts"], st_mm[mt - 1])
                elif th == 1:
                    g.wait_ge(S["ts"], S.val("ts"))
                ins = g.dma_start(out=w2b[mt % 2][:].rearrange("p (t n) -> p t n", t=64),
                                  in_=w2f[mt].rearrange("t p n -> p t n"))
                wlv = S.inc(wsem, ins, 16)
                t.wait_ge(S[wsem], wlv)
                if mt == 0:
                    t.wait_ge(S["vs"], vs_h)
                bank = mt % 4
                if mt >= 4:
                    t.wait_ge(S["vs"], x2_ev[mt - 4])
                for ct in range(64):
                    mm = t.matmul(pb2[bank][:], w2b[mt % 2][:, ct * 128:(ct + 1) * 128],
                                  hT[:, ct * 512:(ct + 1) * 512],
                                  start=(ct == 0), stop=(ct == 63))
                st_mm[mt] = S.inc("ts", mm)
                s.wait_ge(S["ts"], st_mm[mt])
                if mt >= 1:
                    s.wait_ge(S["vs"], g1_used[mt - 1])
                ins = s.activation(g1t[:], pb2[bank][:], AF.Silu, scale=SC)
                gv = S.inc("ss", ins)
                v.wait_ge(S["ss"], gv)
                v.wait_ge(S["ld"], x1_ld[mt])
                ins = v.tensor_mul(g1t[:], s1[:, mt * TOK + tsl.start: mt * TOK + tsl.start + 512], g1t[:])
                g1_used[mt] = S.inc("vs", ins)
                v.wait_ge(S["vs"], g1_used[mt])
                if mt >= 2:
                    v.wait_ge(S["st" if (mt - 2) % 2 == 0 else "sx"], st_x2[mt - 2])
                ins = v.tensor_add(x2b[mt % 2][:], x1b[mt % 2][:], g1t[:])
                vv = S.inc("vs", ins)
                x2_ev[mt] = vv
                sy.wait_ge(S["vs"], vv)
                ins = sy.dma_start(out=x2T_o[mt * 128:(mt + 1) * 128, tsl], in_=x2b[mt % 2][:])
                st_x2[mt] = S.inc("st" if mt % 2 == 0 else "sx", ins, 16)
                if mt + 2 < DT:
                    g.wait_ge(S["vs"], vv)
                    x1_ld[mt + 2] = S.inc("ld", g.dma_start(
                        out=x1b[mt % 2][:], in_=x1T[(mt + 2) * 128:(mt + 3) * 128, tsl]), 16)
        sy.wait_ge(S["st"], S.val("st"))
        sy.wait_ge(S["sx"], S.val("sx"))
        ctxp2.close()
        ctx2.close()
    ctx.close()
    return nc


def prep_E(inputs, b_results, c_results, d_results, w_results):
    gn_img = img_col(inputs["L1_attn_g_norm"])
    wo_img = img_lhsT(_f32(inputs["L1_attn_Wo"]).T)
    w1_img = img_lhsT(_f32(inputs["L1_ffn_W1"]).T)
    w3_img = img_lhsT(_f32(inputs["L1_ffn_W3"]).T)
    w2f_full = np.concatenate([np.asarray(w_results[c]["w2f_o"]) for c in range(NCORES)],
                              axis=2)
    w2f_img = np.ascontiguousarray(
        w2f_full.reshape(64, 128, DT, 128).transpose(2, 0, 1, 3))
    oT_b = []
    for b in range(NB):
        ob = np.zeros((D, SEQ), dtype=np.asarray(d_results[0]["o_o"]).dtype)
        for h in range(H16):
            core = h // 2
            bi = (b * 2) + (h % 2)
            o_nat = np.asarray(d_results[core]["o_o"][bi]).reshape(SEQ, 128)
            ob[h * 128:(h + 1) * 128, :] = o_nat.T
        oT_b.append(ob)
    in_maps = []
    for c in range(NCORES):
        b = c // 4
        tok0 = (c % 4) * 1024
        in_maps.append({
            "x1T": np.asarray(b_results[c]["x1T_o"]),
            "xn1": np.asarray(c_results[c]["xn1_o"]),
            "oT_i": np.ascontiguousarray(oT_b[b][:, tok0:tok0 + 1024]),
            "gate_i": np.asarray(c_results[c]["gate_o"]),
            "gn": gn_img, "wo": wo_img, "w1": w1_img, "w3": w3_img,
            "w2f": w2f_img,
        })
    return in_maps


def kernel(**inputs):
    resW = run_bass_kernel_spmd(build_launch_W(), prep_W(inputs), core_ids=list(range(NCORES)))
    resA = run_bass_kernel_spmd(build_launch_A(), prep_A(inputs), core_ids=list(range(NCORES)))
    resB = run_bass_kernel_spmd(build_launch_B(), prep_B(inputs, resA.results), core_ids=list(range(NCORES)))
    resC = run_bass_kernel_spmd(build_launch_C(), prep_C(inputs, resB.results), core_ids=list(range(NCORES)))
    resD = run_bass_kernel_spmd(build_launch_D(), prep_D(inputs, resC.results), core_ids=list(range(NCORES)))
    resE = run_bass_kernel_spmd(build_launch_E(),
                                prep_E(inputs, resB.results, resC.results, resD.results, resW.results),
                                core_ids=list(range(NCORES)))
    x2 = np.concatenate([np.asarray(resE.results[c]["x2T_o"], np.float32).T for c in range(NCORES)], axis=0)
    return np.ascontiguousarray(x2.reshape(NB, SEQ, D))



# version 02
# speedup vs baseline: 1.0651x; 1.0651x over previous
import numpy as np
import ml_dtypes
import concourse.bass as bass
import concourse.mybir as mybir
from concourse.bass_utils import run_bass_kernel_spmd
from contextlib import ExitStack

F32 = mybir.dt.float32
BF16 = mybir.dt.bfloat16
F32R = mybir.dt.float32r
I32 = mybir.dt.int32
AF = mybir.ActivationFunctionType
ALU = mybir.AluOpType
AX = mybir.AxisListType
BF = ml_dtypes.bfloat16

D = 2048
DT = D // 128
NCORES = 8
SEQ = 4096
NB = 2
H16 = 16
HD = 128
EPS_LN = 1e-5
EPS_L1 = 1e-6


def _f32(x):
    return np.ascontiguousarray(x, dtype=np.float32)


def img_lhsT(w_in_out, mg_cols=256):
    din, dout = w_in_out.shape
    assert din == D
    n_mg = dout // mg_cols
    v = w_in_out.reshape(DT, 128, n_mg, mg_cols)
    return _f32(v.transpose(2, 0, 1, 3))


def img_rhs(w_in_out, dn_cols=512):
    din, dout = w_in_out.shape
    n_dn = dout // dn_cols
    v = w_in_out.reshape(DT, 128, n_dn, dn_cols)
    return _f32(v.transpose(2, 0, 1, 3))


def img_col(vec):
    return _f32(np.asarray(vec, np.float32).reshape(-1, 128).T)


class Sems:

    def __init__(self, nc, ctx, names):
        self.h = {}
        self.v = {}
        for n in names:
            self.h[n] = ctx.enter_context(nc.semaphore(n))
            self.v[n] = 0

    def inc(self, name, instr, k=1):
        instr.then_inc(self.h[name], k)
        self.v[name] += k
        return self.v[name]

    def val(self, name):
        return self.v[name]

    def __getitem__(self, name):
        return self.h[name]



def build_launch_A():
    TOK = 1024
    NCH = TOK // 512
    nc = bass.Bass("TRN2", target_bir_lowering=False)
    x0T = nc.dram_tensor("x0T", [D, TOK], F32, kind="ExternalInput").ap()
    wq = nc.dram_tensor("wq", [8, DT, 128, 256], F32, kind="ExternalInput").ap()
    wk = nc.dram_tensor("wk", [8, DT, 128, 256], F32, kind="ExternalInput").ap()
    wv = nc.dram_tensor("wv", [4, DT, 128, 512], F32, kind="ExternalInput").ap()
    bq = nc.dram_tensor("bq", [128, DT], F32, kind="ExternalInput").ap()
    bk = nc.dram_tensor("bk", [128, DT], F32, kind="ExternalInput").ap()
    bvb = nc.dram_tensor("bvb", [128, D], F32, kind="ExternalInput").ap()
    qT_o = nc.dram_tensor("qT_o", [D, TOK], BF16, kind="ExternalOutput").ap()
    kT_o = nc.dram_tensor("kT_o", [D, TOK], BF16, kind="ExternalOutput").ap()
    v_o = nc.dram_tensor("v_o", [TOK, D], BF16, kind="ExternalOutput").ap()
    xn_o = nc.dram_tensor("xn_o", [D, TOK], BF16, kind="ExternalOutput").ap()

    ones_col_c = nc.inline_tensor(np.ones((128, 1), BF), "ones_col_c").ap()
    ones_row_c = nc.inline_tensor(np.ones((1, 128), np.float32), "ones_row_c").ap()

    ctx = ExitStack()
    sb = lambda name, shp, dt: ctx.enter_context(nc.sbuf_tensor(name, shp, dt))
    psum = lambda name, shp, dt=F32: ctx.enter_context(nc.psum_tensor(name, shp, dt))

    ones_col = sb("ones_col", [128, 1], BF16)
    ones_row = sb("ones_row", [1, 128], F32)
    xn = sb("xn", [128, DT * TOK], BF16)
    r_f = sb("r_f", [128, TOK], F32)
    mr_f = sb("mr_f", [128, TOK], F32)
    rowbuf = sb("rowbuf", [1, 6 * TOK], F32)
    rows = {n: rowbuf[:, i * TOK:(i + 1) * TOK]
            for i, n in enumerate(("m_row", "e2_row", "m2_row", "var_row", "r_row", "mr_row"))}
    rows["std_row"] = rows["m2_row"]
    eps_sb = sb("eps_sb", [128, 2], F32)
    bq_sb = sb("bq_sb", [128, DT], F32)
    bk_sb = sb("bk_sb", [128, DT], F32)
    bvb_sb = sb("bvb_sb", [128, D], F32)
    ctx_ln = ExitStack()
    sbl = lambda name, shp, dt: ctx_ln.enter_context(nc.sbuf_tensor(name, shp, dt))
    x_bf = sbl("x_bf", [128, DT * TOK], BF16)
    sqb = [sbl(f"sq{i}", [128, TOK], BF16) for i in range(2)]
    tmpx = [sbl(f"tmpx{i}", [128, TOK], F32) for i in range(2)]

    ctx_ps = ExitStack()
    psl = lambda name, shp: ctx_ps.enter_context(nc.psum_tensor(name, shp, F32))
    ps_sx = [psl(f"ps_sx{i}", [1, 512]) for i in range(2)]
    ps_sq = [psl(f"ps_sq{i}", [1, 512]) for i in range(2)]
    ps_r = psl("ps_r", [128, TOK])
    ps_mr = psl("ps_mr", [128, TOK])

    S = Sems(nc, ctx, ["ld", "wl0", "wl1", "ts", "vs", "ss", "st"])
    g, t, v, s, sy = nc.gpsimd, nc.tensor, nc.vector, nc.scalar, nc.sync

    with nc.Block():
        g.memset(eps_sb[:, 0:1], EPS_LN)
        g.memset(eps_sb[:, 1:2], EPS_L1)
        S.inc("ld", g.dma_start(out=ones_col[:], in_=ones_col_c), 16)
        S.inc("ld", g.dma_start(out=ones_row[:], in_=ones_row_c), 16)
        S.inc("ld", g.dma_start(out=x_bf[:].rearrange("p (t n) -> p t n", t=DT),
                                in_=x0T.rearrange("(t p) n -> p t n", p=128)), 16)
        S.inc("ld", g.dma_start(out=bq_sb[:], in_=bq), 16)
        S.inc("ld", g.dma_start(out=bk_sb[:], in_=bk), 16)
        S.inc("ld", g.dma_start(out=bvb_sb[:], in_=bvb), 16)
        lds0 = S.val("ld")

        t.wait_ge(S["ld"], lds0)
        for nch in range(NCH):
            for ct in range(DT):
                mm = t.matmul(ps_sx[nch][:], ones_col[:],
                              x_bf[:, ct * TOK + nch * 512: ct * TOK + (nch + 1) * 512],
                              start=(ct == 0), stop=(ct == DT - 1))
        S.inc("ts", mm)
        sq_done = {}
        s.wait_ge(S["ld"], lds0)
        for ct in range(DT):
            if ct >= 2:
                s.wait_ge(S["ts"], sq_done[ct - 2])
            ins = s.activation(sqb[ct % 2][:], x_bf[:, ct * TOK:(ct + 1) * TOK], AF.Square)
            ssv = S.inc("ss", ins)
            t.wait_ge(S["ss"], ssv)
            for nch in range(NCH):
                mm = t.matmul(ps_sq[nch][:], ones_col[:],
                              sqb[ct % 2][:, nch * 512:(nch + 1) * 512],
                              start=(ct == 0), stop=(ct == DT - 1))
            sq_done[ct] = S.inc("ts", mm)
        ts_stats = S.val("ts")

        v.wait_ge(S["ts"], ts_stats)
        for nch in range(NCH):
            sl = slice(nch * 512, (nch + 1) * 512)
            v.tensor_scalar_mul(rows["m_row"][:, sl], ps_sx[nch][:], 1.0 / D)
            ins = v.tensor_scalar_mul(rows["e2_row"][:, sl], ps_sq[nch][:], 1.0 / D)
        v.wait_ge(S["vs"], S.inc("vs", ins))
        ins = v.tensor_mul(rows["m2_row"][:], rows["m_row"][:], rows["m_row"][:])
        v.wait_ge(S["vs"], S.inc("vs", ins))
        ins = v.tensor_sub(rows["var_row"][:], rows["e2_row"][:], rows["m2_row"][:])
        S.inc("vs", ins)
        s.wait_ge(S["vs"], S.val("vs"))
        ins = s.activation(rows["std_row"][:], rows["var_row"][:], AF.Sqrt, bias=eps_sb[0:1, 0:1])
        S.inc("ss", ins)
        v.wait_ge(S["ss"], S.val("ss"))
        ins = v.reciprocal(rows["r_row"][:], rows["std_row"][:])
        v.wait_ge(S["vs"], S.inc("vs", ins))
        ins = v.tensor_mul(rows["mr_row"][:], rows["m_row"][:], rows["r_row"][:])
        S.inc("vs", ins)

        t.wait_ge(S["vs"], S.val("vs"))
        for nch in range(NCH):
            sl = slice(nch * 512, (nch + 1) * 512)
            t.matmul(ps_r[:, sl], ones_row[:], rows["r_row"][:, sl], start=True, stop=True)
            mm = t.matmul(ps_mr[:, sl], ones_row[:], rows["mr_row"][:, sl], start=True, stop=True)
        S.inc("ts", mm)

        v.wait_ge(S["ts"], S.val("ts"))
        v.tensor_copy(r_f[:], ps_r[:])
        ins = v.tensor_copy(mr_f[:], ps_mr[:])
        v.wait_ge(S["vs"], S.inc("vs", ins))
        for ct in range(DT):
            ins = v.tensor_mul(tmpx[ct % 2][:], x_bf[:, ct * TOK:(ct + 1) * TOK], r_f[:])
            v.wait_ge(S["vs"], S.inc("vs", ins))
            ins = v.tensor_sub(xn[:, ct * TOK:(ct + 1) * TOK], tmpx[ct % 2][:], mr_f[:])
            S.inc("vs", ins)
        vs_xn = S.val("vs")

        sy.wait_ge(S["vs"], vs_xn)
        ins = sy.dma_start(out=xn_o.rearrange("(t p) n -> p t n", p=128),
                           in_=xn[:].rearrange("p (t n) -> p t n", t=DT))
        S.inc("st", ins, 16)

        ctx_ln.close()
        ctx_ps.close()
        pb = [psum(f"pb{i}", [128, 512]) for i in range(4)]
        wbuf = [sb(f"wbuf{i}", [128, DT * 256], BF16) for i in range(2)]
        wvbuf = [sb(f"wvbuf{i}", [128, DT * 512], BF16) for i in range(2)]
        out_sb = sb("out_sb", [128, DT * TOK], BF16)
        v_sb = sb("v_sb", [128, 8 * D], BF16)

        qk_tile_mm = {}
        qk_tile_ev = {}
        wl_after = {}
        g.wait_ge(S["vs"], vs_xn)
        n_tile = 0
        for wi, wimg in enumerate((wq, wk)):
            for mg in range(8):
                li = wi * 8 + mg
                if li >= 2:
                    g.wait_ge(S["ts"], qk_tile_mm[(li - 2) * 4 + 3])
                wsem = "wl0" if li % 2 == 0 else "wl1"
                ins = g.dma_start(out=wbuf[li % 2][:].rearrange("p (t n) -> p t n", t=DT),
                                  in_=wimg[mg].rearrange("t p n -> p t n"))
                wl_after[li] = S.inc(wsem, ins, 16)
                t.wait_ge(S[wsem], wl_after[li])
                for m2 in range(2):
                    mt = mg * 2 + m2
                    for nch in range(NCH):
                        bank = n_tile % 4
                        if n_tile >= 4:
                            t.wait_ge(S["ss"], qk_tile_ev[n_tile - 4])
                        for ct in range(DT):
                            mm = t.matmul(pb[bank][:],
                                          wbuf[li % 2][:, ct * 256 + m2 * 128: ct * 256 + (m2 + 1) * 128],
                                          xn[:, ct * TOK + nch * 512: ct * TOK + (nch + 1) * 512],
                                          start=(ct == 0), stop=(ct == DT - 1))
                        qk_tile_mm[n_tile] = S.inc("ts", mm)
                        s.wait_ge(S["ts"], qk_tile_mm[n_tile])
                        if wi == 1 and mg == 0 and m2 == 0 and nch == 0:
                            s.wait_ge(S["st"], 32)
                        bsb = bq_sb if wi == 0 else bk_sb
                        ins = s.activation(out_sb[:, mt * TOK + nch * 512: mt * TOK + (nch + 1) * 512],
                                           pb[bank][:], AF.Identity, bias=bsb[:, mt:mt + 1])
                        qk_tile_ev[n_tile] = S.inc("ss", ins)
                        n_tile += 1
            sy.wait_ge(S["ss"], qk_tile_ev[n_tile - 1])
            ins = sy.dma_start(out=(qT_o if wi == 0 else kT_o).rearrange("(t p) n -> p t n", p=128),
                               in_=out_sb[:].rearrange("p (t n) -> p t n", t=DT))
            S.inc("st", ins, 16)

        v_tile_mm = {}
        v_tile_ev = {}
        n_vt = 0
        for dn in range(4):
            if dn >= 2:
                g.wait_ge(S["ts"], v_tile_mm[(dn - 2) * 8 + 7])
            wsem = "wl0" if dn % 2 == 0 else "wl1"
            ins = g.dma_start(out=wvbuf[dn % 2][:].rearrange("p (t n) -> p t n", t=DT),
                              in_=wv[dn].rearrange("t p n -> p t n"))
            wlv = S.inc(wsem, ins, 16)
            t.wait_ge(S[wsem], wlv)
            for tt in range(8):
                bank = n_vt % 4
                if n_vt == 0:
                    t.wait_ge(S["ss"], qk_tile_ev[63])
                elif n_vt >= 4:
                    t.wait_ge(S["vs"], v_tile_ev[n_vt - 4])
                for ct in range(DT):
                    mm = t.matmul(pb[bank][:],
                                  xn[:, ct * TOK + tt * 128: ct * TOK + (tt + 1) * 128],
                                  wvbuf[dn % 2][:, ct * 512:(ct + 1) * 512],
                                  start=(ct == 0), stop=(ct == DT - 1))
                v_tile_mm[n_vt] = S.inc("ts", mm)
                v.wait_ge(S["ts"], v_tile_mm[n_vt])
                ins = v.tensor_add(v_sb[:, tt * D + dn * 512: tt * D + (dn + 1) * 512],
                                   pb[bank][:], bvb_sb[:, dn * 512:(dn + 1) * 512])
                v_tile_ev[n_vt] = S.inc("vs", ins)
                n_vt += 1
        sy.wait_ge(S["vs"], v_tile_ev[n_vt - 1])
        ins = sy.dma_start(out=v_o.rearrange("(t p) n -> p t n", p=128),
                           in_=v_sb[:].rearrange("p (t n) -> p t n", t=8))
        S.inc("st", ins, 16)
        sy.wait_ge(S["st"], S.val("st"))

    ctx.close()
    return nc


def prep_A(inputs):
    X = _f32(inputs["L0_attn_X"])
    xT = _f32(X.reshape(NCORES * 1024, D).T)
    wq_img = img_lhsT(_f32(inputs["L0_attn_Wq"]).T)
    wk_img = img_lhsT(_f32(inputs["L0_attn_Wk"]).T)
    wv_img = img_rhs(_f32(inputs["L0_attn_Wv"]).T)
    bq_img = img_col(inputs["L0_attn_bq"])
    bk_img = img_col(inputs["L0_attn_bk"])
    bvb = _f32(np.broadcast_to(np.asarray(inputs["L0_attn_bv"], np.float32), (128, D)))
    return [{"x0T": _f32(xT[:, c * 1024:(c + 1) * 1024]),
             "wq": wq_img, "wk": wk_img, "wv": wv_img,
             "bq": bq_img, "bk": bk_img, "bvb": bvb} for c in range(NCORES)]





def build_launch_B(debug=False):
    TOK = 1024
    NCH = TOK // 512
    nc = bass.Bass("TRN2", target_bir_lowering=False)
    qT_i = nc.dram_tensor("qT_i", [D, TOK], BF16, kind="ExternalInput").ap()
    kTb = nc.dram_tensor("kTb", [H16, 128, SEQ], BF16, kind="ExternalInput").ap()
    vb = nc.dram_tensor("vb", [H16, 32, 128, 128], BF16, kind="ExternalInput").ap()
    wo = nc.dram_tensor("wo", [8, DT, 128, 256], F32, kind="ExternalInput").ap()
    bo = nc.dram_tensor("bo", [128, DT], F32, kind="ExternalInput").ap()
    xn_i = nc.dram_tensor("xn_i", [D, TOK], BF16, kind="ExternalInput").ap()
    x0T = nc.dram_tensor("x0T", [D, TOK], F32, kind="ExternalInput").ap()
    gw = nc.dram_tensor("gw", [DT, 128, 8], F32, kind="ExternalInput").ap()
    wg = nc.dram_tensor("wg", [8, 11, DT, 128, 128], F32, kind="ExternalInput").ap()
    wu = nc.dram_tensor("wu", [8, 11, DT, 128, 128], F32, kind="ExternalInput").ap()
    wd = nc.dram_tensor("wd", [8, 8, 11, 128, 256], F32, kind="ExternalInput").ap()
    shg = nc.dram_tensor("shg", [128, DT], F32, kind="ExternalInput").ap()
    shwg = nc.dram_tensor("shwg", [22, DT, 128, 128], F32, kind="ExternalInput").ap()
    shwu = nc.dram_tensor("shwu", [22, DT, 128, 128], F32, kind="ExternalInput").ap()
    shwd = nc.dram_tensor("shwd", [8, 22, 128, 256], F32, kind="ExternalInput").ap()
    sT_o = nc.dram_tensor("sT_o", [D, TOK], BF16, kind="ExternalOutput").ap()
    x1T_o = nc.dram_tensor("x1T_o", [D, TOK], F32, kind="ExternalOutput").ap()
    if debug:
        dbg_logit = nc.dram_tensor("dbg_logit", [128, 64], F32, kind="ExternalOutput").ap()
        dbg_comb = nc.dram_tensor("dbg_comb", [128, 64], F32, kind="ExternalOutput").ap()
        dbg_ce = nc.dram_tensor("dbg_ce", [128, TOK], F32, kind="ExternalOutput").ap()
        dbg_macc = nc.dram_tensor("dbg_macc", [D, TOK], F32, kind="ExternalOutput").ap()

    ones_col_c = nc.inline_tensor(np.ones((128, 1), BF), "ones_col_c").ap()
    ones_row_c = nc.inline_tensor(np.ones((1, 128), np.float32), "ones_row_c").ap()

    ctx = ExitStack()
    sb = lambda name, shp, dt: ctx.enter_context(nc.sbuf_tensor(name, shp, dt))
    psum = lambda name, shp, dt=F32: ctx.enter_context(nc.psum_tensor(name, shp, dt))

    ones_col = sb("ones_col", [128, 1], BF16)
    ones_row = sb("ones_row", [1, 128], F32)
    xn = sb("xn", [128, DT * TOK], BF16)
    out_sb = sb("out_sb", [128, DT * TOK], BF16)
    ctx_att = ExitStack()
    sba = lambda name, shp, dt: ctx_att.enter_context(nc.sbuf_tensor(name, shp, dt))
    psa = lambda name, shp: ctx_att.enter_context(nc.psum_tensor(name, shp, F32))
    qT = sba("qT", [128, DT * TOK], BF16)
    oT = sba("oT", [128, DT * TOK], BF16)
    kbuf = [sba(f"kbuf{i}", [128, SEQ], BF16) for i in range(2)]
    vbuf = [sba(f"vbuf{i}", [128, SEQ], BF16) for i in range(2)]
    att = sba("att", [128, TOK], BF16)
    den_row = sba("den_row", [1, TOK], F32)
    rden_row = sba("rden_row", [1, TOK], F32)
    bc_f = sba("bc_f", [128, TOK], F32)
    bo_sb = sba("bo_sb", [128, DT], F32)
    wbuf = [sba(f"wbuf{i}", [128, DT * 256], BF16) for i in range(2)]

    ps_s = psa("ps_s", [128, TOK])
    ps_o = psa("ps_o", [128, TOK])
    ps_den = [psa(f"ps_den{i}", [1, 512]) for i in range(2)]
    ps_bc = psa("ps_bc", [128, TOK])

    S = Sems(nc, ctx, ["ld", "kl0", "kl1", "wl0", "wl1", "ts", "vs", "ss", "st", "sx"])
    g, t, v, s, sy = nc.gpsimd, nc.tensor, nc.vector, nc.scalar, nc.sync

    with nc.Block():
        S.inc("ld", g.dma_start(out=ones_col[:], in_=ones_col_c), 16)
        S.inc("ld", g.dma_start(out=ones_row[:], in_=ones_row_c), 16)
        S.inc("ld", g.dma_start(out=qT[:].rearrange("p (t n) -> p t n", t=DT),
                                in_=qT_i.rearrange("(t p) n -> p t n", p=128)), 16)
        S.inc("ld", g.dma_start(out=bo_sb[:], in_=bo), 16)
        S.inc("ld", g.dma_start(out=xn[:].rearrange("p (t n) -> p t n", t=DT),
                                in_=xn_i.rearrange("(t p) n -> p t n", p=128)), 16)
        lds0 = S.val("ld")

        kv_after = {}
        S.inc("kl0", sy.dma_start(out=kbuf[0][:], in_=kTb[0]), 16)
        ins = sy.dma_start(out=vbuf[0][:].rearrange("p (k n) -> p k n", k=32),
                           in_=vb[0].rearrange("k p n -> p k n"))
        kv_after[0] = S.inc("kl0", ins, 16)

        exp_done = {}
        avden_done = {}
        norm_done = {}
        for h in range(H16):
            if h + 1 < H16:
                if h >= 1:
                    sy.wait_ge(S["ts"], avden_done[(h - 1, 31)])
                ksem = "kl0" if (h + 1) % 2 == 0 else "kl1"
                S.inc(ksem, sy.dma_start(out=kbuf[(h + 1) % 2][:], in_=kTb[h + 1]), 16)
                ins = sy.dma_start(out=vbuf[(h + 1) % 2][:].rearrange("p (k n) -> p k n", k=32),
                                   in_=vb[h + 1].rearrange("k p n -> p k n"))
                kv_after[h + 1] = S.inc(ksem, ins, 16)
            t.wait_ge(S["kl0" if h % 2 == 0 else "kl1"], kv_after[h])
            if h == 0:
                t.wait_ge(S["ld"], lds0)
            for kjt in range(32):
                if (h, kjt) != (0, 0):
                    prev = (h, kjt - 1) if kjt else (h - 1, 31)
                    t.wait_ge(S["ss"], exp_done[prev])
                for nch in range(NCH):
                    mm = t.matmul(ps_s[:, nch * 512:(nch + 1) * 512],
                                  kbuf[h % 2][:, kjt * 128:(kjt + 1) * 128],
                                  qT[:, h * TOK + nch * 512: h * TOK + (nch + 1) * 512],
                                  start=True, stop=True)
                sc_v = S.inc("ts", mm)
                s.wait_ge(S["ts"], sc_v)
                if (h, kjt) != (0, 0):
                    prev = (h, kjt - 1) if kjt else (h - 1, 31)
                    s.wait_ge(S["ts"], avden_done[prev])
                ins = s.activation(att[:], ps_s[:], AF.Exp, scale=float(1.0 / np.sqrt(HD)))
                exp_done[(h, kjt)] = S.inc("ss", ins)
                t.wait_ge(S["ss"], exp_done[(h, kjt)])
                if kjt == 0 and h > 0:
                    t.wait_ge(S["vs"], norm_done[h - 1])
                for nch in range(NCH):
                    t.matmul(ps_o[:, nch * 512:(nch + 1) * 512],
                             vbuf[h % 2][:, kjt * 128:(kjt + 1) * 128],
                             att[:, nch * 512:(nch + 1) * 512],
                             start=(kjt == 0), stop=(kjt == 31))
                    mm = t.matmul(ps_den[nch][:], ones_col[:],
                                  att[:, nch * 512:(nch + 1) * 512],
                                  start=(kjt == 0), stop=(kjt == 31))
                avden_done[(h, kjt)] = S.inc("ts", mm)
            v.wait_ge(S["ts"], avden_done[(h, 31)])
            for nch in range(NCH):
                ins = v.tensor_copy(den_row[:, nch * 512:(nch + 1) * 512], ps_den[nch][:])
            v.wait_ge(S["vs"], S.inc("vs", ins))
            ins = v.reciprocal(rden_row[:], den_row[:])
            S.inc("vs", ins)
            t.wait_ge(S["vs"], S.val("vs"))
            for nch in range(NCH):
                mm = t.matmul(ps_bc[:, nch * 512:(nch + 1) * 512], ones_row[:],
                              rden_row[:, nch * 512:(nch + 1) * 512], start=True, stop=True)
            bc_v = S.inc("ts", mm)
            v.wait_ge(S["ts"], bc_v)
            ins = v.tensor_copy(bc_f[:], ps_bc[:])
            v.wait_ge(S["vs"], S.inc("vs", ins))
            ins = v.tensor_mul(oT[:, h * TOK:(h + 1) * TOK], ps_o[:], bc_f[:])
            norm_done[h] = S.inc("vs", ins)

        n_tile = 0
        wo_mm = {}
        wo_ev = {}
        wl_after = {}
        pb = [ps_s[:, 0:512], ps_s[:, 512:1024], ps_o[:, 0:512], ps_o[:, 512:1024]]
        g.wait_ge(S["vs"], norm_done[H16 - 1])
        for mg in range(8):
            if mg >= 2:
                g.wait_ge(S["ts"], wo_mm[(mg - 2) * 4 + 3])
            wsem = "wl0" if mg % 2 == 0 else "wl1"
            ins = g.dma_start(out=wbuf[mg % 2][:].rearrange("p (t n) -> p t n", t=DT),
                              in_=wo[mg].rearrange("t p n -> p t n"))
            wl_after[mg] = S.inc(wsem, ins, 16)
            t.wait_ge(S[wsem], wl_after[mg])
            if mg == 0:
                t.wait_ge(S["vs"], norm_done[H16 - 1])
            for m2 in range(2):
                mt = mg * 2 + m2
                for nch in range(NCH):
                    bank = n_tile % 4
                    if n_tile >= 4:
                        t.wait_ge(S["ss"], wo_ev[n_tile - 4])
                    for ct in range(DT):
                        mm = t.matmul(pb[bank],
                                      wbuf[mg % 2][:, ct * 256 + m2 * 128: ct * 256 + (m2 + 1) * 128],
                                      oT[:, ct * TOK + nch * 512: ct * TOK + (nch + 1) * 512],
                                      start=(ct == 0), stop=(ct == DT - 1))
                    wo_mm[n_tile] = S.inc("ts", mm)
                    s.wait_ge(S["ts"], wo_mm[n_tile])
                    ins = s.activation(out_sb[:, mt * TOK + nch * 512: mt * TOK + (nch + 1) * 512],
                                       pb[bank], AF.Identity, bias=bo_sb[:, mt:mt + 1])
                    wo_ev[n_tile] = S.inc("ss", ins)
                    n_tile += 1
        sy.wait_ge(S["ss"], wo_ev[n_tile - 1])
        ins = sy.dma_start(out=sT_o.rearrange("(t p) n -> p t n", p=128),
                           in_=out_sb[:].rearrange("p (t n) -> p t n", t=DT))
        S.inc("st", ins, 16)

    ctx_att.close()
    I128_c = nc.inline_tensor(np.eye(128, dtype=np.float32), "I128_c").ap()
    ctx_moe = ExitStack()
    sbm = lambda name, shp, dt: ctx_moe.enter_context(nc.sbuf_tensor(name, shp, dt))
    psm = lambda name, shp: ctx_moe.enter_context(nc.psum_tensor(name, shp, F32))
    I128 = sbm("I128", [128, 128], F32)
    gw_sb = sbm("gw_sb", [128, DT * 8], BF16)
    logit = sbm("logit", [128, 8 * 8], F32)
    comb = sbm("comb", [128, 8 * 8], F32)
    msc = sbm("msc", [128, 6], F32)
    mask = sbm("mask", [128, 8], F32)
    zex = sbm("zex", [128, 8], F32)
    macc = sbm("macc", [128, DT * TOK], F32)
    ce_row = sbm("ce_row", [1, TOK], F32)
    ce_bc = sbm("ce_bc", [128, TOK], F32)
    shg_sb = sbm("shg_sb", [128, DT], BF16)
    sg_row = sbm("sg_row", [1, TOK], F32)
    ctx_ex = ExitStack()
    sbe = lambda name, shp, dt: ctx_ex.enter_context(nc.sbuf_tensor(name, shp, dt))
    sg_t = sbe("sg_t", [128, 512], BF16)
    u_t = sbe("u_t", [128, 512], BF16)
    wga = [sbe(f"wga{i}", [128, DT * 128], BF16) for i in range(2)]
    wua = [sbe(f"wua{i}", [128, DT * 128], BF16) for i in range(2)]
    wda = [sbe(f"wda{i}", [128, 11 * 256], BF16) for i in range(2)]
    pb4 = [psm(f"pbm{i}", [128, 512]) for i in range(4)]
    ps_row = [psm(f"ps_row{i}", [1, 512]) for i in range(2)]
    ps_bc2 = psm("ps_bc2", [128, TOK])

    with nc.Block():
        S.inc("ld", g.dma_start(out=I128[:], in_=I128_c), 16)
        S.inc("ld", g.dma_start(out=gw_sb[:].rearrange("p (t n) -> p t n", t=DT),
                                in_=gw.rearrange("t p n -> p t n")), 16)
        S.inc("ld", g.dma_start(out=shg_sb[:], in_=shg), 16)
        lds1 = S.val("ld")

        t.wait_ge(S["ld"], lds1)
        lcopy = {}
        for tt in range(8):
            if tt >= 2:
                t.wait_ge(S["vs"], lcopy[tt - 2])
            for ct in range(DT):
                mm = t.matmul(pb4[tt % 2][:, 0:8],
                              xn[:, ct * TOK + tt * 128: ct * TOK + (tt + 1) * 128],
                              gw_sb[:, ct * 8:(ct + 1) * 8],
                              start=(ct == 0), stop=(ct == DT - 1))
            lv = S.inc("ts", mm)
            v.wait_ge(S["ts"], lv)
            ins = v.tensor_copy(logit[:, tt * 8:(tt + 1) * 8], pb4[tt % 2][:, 0:8])
            lcopy[tt] = S.inc("vs", ins)
        vs_logit = S.val("vs")
        def vsync(ins):
            v.wait_ge(S["vs"], S.inc("vs", ins))
        for tt in range(8):
            lsl = logit[:, tt * 8:(tt + 1) * 8]
            vsync(v.reduce_max(msc[:, 0:1], lsl, axis=AX.X))
            ins = v.tensor_scalar_mul(msc[:, 1:2], msc[:, 0:1], -1.0)
            vv = S.inc("vs", ins)
            v.wait_ge(S["vs"], vv)
            s.wait_ge(S["vs"], vv)
            ins = s.activation(mask[:], lsl, AF.Identity, bias=msc[:, 1:2])
            sv = S.inc("ss", ins)
            v.wait_ge(S["ss"], sv)
            vsync(v.tensor_scalar(mask[:], mask[:], 0.0, None, ALU.is_ge))
            vsync(v.tensor_scalar_mul(mask[:], mask[:], -1e30))
            vsync(v.tensor_add(mask[:], mask[:], lsl))
            vsync(v.reduce_max(msc[:, 2:3], mask[:], axis=AX.X))
            ins = v.tensor_scalar_mul(msc[:, 3:4], msc[:, 2:3], -1.0)
            vv = S.inc("vs", ins)
            v.wait_ge(S["vs"], vv)
            s.wait_ge(S["vs"], vv)
            ins = s.activation(zex[:], lsl, AF.Exp, bias=msc[:, 1:2])
            v.wait_ge(S["ss"], S.inc("ss", ins))
            ins = s.activation(mask[:], lsl, AF.Identity, bias=msc[:, 3:4])
            sv = S.inc("ss", ins)
            v.wait_ge(S["ss"], sv)
            vsync(v.tensor_scalar(mask[:], mask[:], 0.0, None, ALU.is_ge))
            vsync(v.tensor_mul(zex[:], zex[:], mask[:]))
            vsync(v.reduce_sum(msc[:, 4:5], zex[:], axis=AX.X))
            ins = v.reciprocal(msc[:, 5:6], msc[:, 4:5])
            vv = S.inc("vs", ins)
            v.wait_ge(S["vs"], vv)
            s.wait_ge(S["vs"], vv)
            ins = s.mul(comb[:, tt * 8:(tt + 1) * 8], zex[:], msc[:, 5:6])
            S.inc("ss", ins)
        ss_comb = S.val("ss")

        if debug:
            sy.wait_ge(S["ss"], ss_comb)
            S.inc("st", sy.dma_start(out=dbg_logit, in_=logit[:]), 16)
            S.inc("st", sy.dma_start(out=dbg_comb, in_=comb[:]), 16)

        t.wait_ge(S["ss"], ss_comb)
        for nch in range(NCH):
            for ct in range(DT):
                mm = t.matmul(ps_row[nch][:], shg_sb[:, ct:ct + 1],
                              xn[:, ct * TOK + nch * 512: ct * TOK + (nch + 1) * 512],
                              start=(ct == 0), stop=(ct == DT - 1))
        tv = S.inc("ts", mm)
        s.wait_ge(S["ts"], tv)
        for nch in range(NCH):
            ins = s.activation(sg_row[:, nch * 512:(nch + 1) * 512], ps_row[nch][:], AF.Sigmoid)
        sv = S.inc("ss", ins)
        t.wait_ge(S["ss"], sv)
        for nch in range(NCH):
            mm = t.matmul(ps_bc2[:, nch * 512:(nch + 1) * 512], ones_row[:],
                          sg_row[:, nch * 512:(nch + 1) * 512], start=True, stop=True)
        tv = S.inc("ts", mm)
        v.wait_ge(S["ts"], tv)
        ins = v.tensor_copy(ce_bc[:], ps_bc2[:])
        vs_sg = S.inc("vs", ins)
        v.wait_ge(S["vs"], vs_sg)

        he = ctx_ex.enter_context(nc.sbuf_tensor("he", [128, 11 * TOK], BF16))
        ev_prev = None
        prev_unit_mm = None
        for ui in range(10):
            is_sh = ui < 2
            e = ui - 2
            if ui == 0:
                pass
            elif ui == 1:
                pass
            else:
                t.wait_ge(S["ss"], ss_comb)
                t.wait_ge(S["vs"], S.val("vs"))
                for tt in range(8):
                    t.matmul(ps_row[0][0:1, tt * 64: tt * 64 + 64],
                             comb[:, tt * 8 + e: tt * 8 + e + 1], I128[:, 0:64],
                             start=True, stop=True)
                    mm = t.matmul(ps_row[1][0:1, tt * 64: tt * 64 + 64],
                                  comb[:, tt * 8 + e: tt * 8 + e + 1], I128[:, 64:128],
                                  start=True, stop=True)
                tv = S.inc("ts", mm)
                v.wait_ge(S["ts"], tv)
                for tt in range(8):
                    v.tensor_copy(ce_row[:, tt * 128: tt * 128 + 64],
                                  ps_row[0][0:1, tt * 64: tt * 64 + 64])
                    ins = v.tensor_copy(ce_row[:, tt * 128 + 64: (tt + 1) * 128],
                                        ps_row[1][0:1, tt * 64: tt * 64 + 64])
                vv = S.inc("vs", ins)
                t.wait_ge(S["vs"], vv)
                for nch in range(NCH):
                    mm = t.matmul(ps_bc2[:, nch * 512:(nch + 1) * 512], ones_row[:],
                                  ce_row[:, nch * 512:(nch + 1) * 512], start=True, stop=True)
                tv = S.inc("ts", mm)
                v.wait_ge(S["ts"], tv)
                ins = v.tensor_copy(ce_bc[:], ps_bc2[:])
                v.wait_ge(S["vs"], S.inc("vs", ins))
            if debug and ui == 2:
                sy.wait_ge(S["vs"], S.val("vs"))
                S.inc("st", sy.dma_start(out=dbg_ce, in_=ce_bc[:]), 16)
            gu_done = {}
            for hc in range(11):
                if hc >= 2:
                    g.wait_ge(S["ts"], gu_done[("mmu", hc - 2, NCH - 1)])
                elif prev_unit_mm is not None:
                    g.wait_ge(S["ts"], prev_unit_mm)
                if is_sh:
                    gsrc = shwg[ui * 11 + hc]
                    usrc = shwu[ui * 11 + hc]
                else:
                    gsrc = wg[e, hc]
                    usrc = wu[e, hc]
                wsem = "wl0" if hc % 2 == 0 else "wl1"
                S.inc(wsem, g.dma_start(out=wga[hc % 2][:].rearrange("p (t n) -> p t n", t=DT),
                                        in_=gsrc.rearrange("t p n -> p t n")), 16)
                S.inc(wsem, g.dma_start(out=wua[hc % 2][:].rearrange("p (t n) -> p t n", t=DT),
                                        in_=usrc.rearrange("t p n -> p t n")), 16)
                wlv = S.val(wsem)
                t.wait_ge(S[wsem], wlv)
                if ui == 0 and hc == 0:
                    t.wait_ge(S["vs"], vs_sg)
                for nch in range(NCH):
                    bg, bu = nch, 2 + nch
                    if ("evg", hc - 1, nch) in gu_done:
                        t.wait_ge(S["ss"], gu_done[("evg", hc - 1, nch)])
                        t.wait_ge(S["vs"], gu_done[("evu", hc - 1, nch)])
                    elif hc == 0 and ui > 0:
                        t.wait_ge(S["vs"], S.val("vs"))
                    for ct in range(DT):
                        mm = t.matmul(pb4[bg][:], wga[hc % 2][:, ct * 128:(ct + 1) * 128],
                                      xn[:, ct * TOK + nch * 512: ct * TOK + (nch + 1) * 512],
                                      start=(ct == 0), stop=(ct == DT - 1))
                    S.inc("ts", mm)
                    for ct in range(DT):
                        mm = t.matmul(pb4[bu][:], wua[hc % 2][:, ct * 128:(ct + 1) * 128],
                                      xn[:, ct * TOK + nch * 512: ct * TOK + (nch + 1) * 512],
                                      start=(ct == 0), stop=(ct == DT - 1))
                    mmv = S.inc("ts", mm)
                    gu_done[("mmu", hc, nch)] = mmv
                    s.wait_ge(S["ts"], mmv)
                    if ev_prev is not None:
                        s.wait_ge(S["vs"], ev_prev)
                    ins = s.activation(sg_t[:], pb4[bg][:], AF.Silu)
                    gu_done[("evg", hc, nch)] = S.inc("ss", ins)
                    v.wait_ge(S["ss"], gu_done[("evg", hc, nch)])
                    ins = v.tensor_mul(u_t[:], pb4[bu][:], ce_bc[:, nch * 512:(nch + 1) * 512])
                    v.wait_ge(S["vs"], S.inc("vs", ins))
                    ins = v.tensor_mul(he[:, hc * TOK + nch * 512: hc * TOK + (nch + 1) * 512],
                                       sg_t[:], u_t[:])
                    gu_done[("evu", hc, nch)] = S.inc("vs", ins)
                    ev_prev = gu_done[("evu", hc, nch)]
            vs_he = S.val("vs")
            nmt2 = 0
            dn_mm = {}
            dn_ev = {}
            for mg in range(8):
                if mg >= 2:
                    g.wait_ge(S["ts"], dn_mm[(mg - 2) * 4 + 3])
                if is_sh:
                    dsrc = shwd[mg, ui * 11:(ui + 1) * 11]
                else:
                    dsrc = wd[e, mg]
                wsem = "wl0" if mg % 2 == 0 else "wl1"
                S.inc(wsem, g.dma_start(out=wda[mg % 2][:].rearrange("p (t n) -> p t n", t=11),
                                        in_=dsrc.rearrange("t p n -> p t n")), 16)
                wlv = S.val(wsem)
                t.wait_ge(S[wsem], wlv)
                if mg == 0:
                    t.wait_ge(S["vs"], vs_he)
                for m2 in range(2):
                    mt = mg * 2 + m2
                    for nch in range(NCH):
                        bank = nmt2 % 4
                        if nmt2 >= 4:
                            t.wait_ge(S["vs"], dn_ev[nmt2 - 4])
                        for hc in range(11):
                            mm = t.matmul(pb4[bank][:],
                                          wda[mg % 2][:, hc * 256 + m2 * 128: hc * 256 + (m2 + 1) * 128],
                                          he[:, hc * TOK + nch * 512: hc * TOK + (nch + 1) * 512],
                                          start=(hc == 0), stop=(hc == 10))
                        dn_mm[nmt2] = S.inc("ts", mm)
                        v.wait_ge(S["ts"], dn_mm[nmt2])
                        sl = slice(mt * TOK + nch * 512, mt * TOK + (nch + 1) * 512)
                        if ui == 0:
                            ins = v.tensor_copy(macc[:, sl], pb4[bank][:])
                        else:
                            ins = v.tensor_add(macc[:, sl], macc[:, sl], pb4[bank][:])
                        dn_ev[nmt2] = S.inc("vs", ins)
                        v.wait_ge(S["vs"], dn_ev[nmt2])
                        nmt2 += 1
            prev_unit_mm = S.val("ts")

        if debug:
            sy.wait_ge(S["vs"], S.val("vs"))
            S.inc("st", sy.dma_start(out=dbg_macc.rearrange("(t p) n -> p t n", p=128),
                                     in_=macc[:].rearrange("p (t n) -> p t n", t=DT)), 16)
        ctx_ex.close()
        x0b = [ctx_moe.enter_context(nc.sbuf_tensor(f"x0b{i}", [128, TOK], F32)) for i in range(2)]
        gt = ctx_moe.enter_context(nc.sbuf_tensor("gt", [128, TOK], BF16))
        x1t = [ctx_moe.enter_context(nc.sbuf_tensor(f"x1t{i}", [128, TOK], F32)) for i in range(2)]
        vs_all = S.val("vs")
        gt2 = ctx_moe.enter_context(nc.sbuf_tensor("gt2", [128, TOK], F32))
        x0_ld = {}
        add_done = {}
        st_x1 = {}
        g.wait_ge(S["vs"], vs_all)
        x0_ld[0] = S.inc("kl0", g.dma_start(out=x0b[0][:], in_=x0T[0:128, :]), 16)
        x0_ld[1] = S.inc("kl1", g.dma_start(out=x0b[1][:], in_=x0T[128:256, :]), 16)
        s.wait_ge(S["vs"], vs_all)
        for mt in range(DT):
            if mt - 1 in add_done:
                s.wait_ge(S["vs"], add_done[mt - 1])
            ins = s.activation(gt[:], macc[:, mt * TOK:(mt + 1) * TOK], AF.Silu)
            gv = S.inc("ss", ins)
            v.wait_ge(S["ss"], gv)
            v.wait_ge(S["kl0" if mt % 2 == 0 else "kl1"], x0_ld[mt])
            ins = v.tensor_mul(gt2[:], out_sb[:, mt * TOK:(mt + 1) * TOK], gt[:])
            v.wait_ge(S["vs"], S.inc("vs", ins))
            if mt >= 2:
                v.wait_ge(S["st" if (mt - 2) % 2 == 0 else "sx"], st_x1[mt - 2])
            ins = v.tensor_add(x1t[mt % 2][:], x0b[mt % 2][:], gt2[:])
            vv = S.inc("vs", ins)
            add_done[mt] = vv
            v.wait_ge(S["vs"], vv)
            sy.wait_ge(S["vs"], vv)
            ins = sy.dma_start(out=x1T_o[mt * 128:(mt + 1) * 128, :], in_=x1t[mt % 2][:])
            st_x1[mt] = S.inc("st" if mt % 2 == 0 else "sx", ins, 16)
            if mt + 2 < DT:
                g.wait_ge(S["vs"], vv)
                x0_ld[mt + 2] = S.inc("kl0" if mt % 2 == 0 else "kl1", g.dma_start(
                    out=x0b[mt % 2][:], in_=x0T[(mt + 2) * 128:(mt + 3) * 128, :]), 16)
        sy.wait_ge(S["st"], S.val("st"))
        sy.wait_ge(S["sx"], S.val("sx"))

    ctx_moe.close()
    ctx.close()
    return nc


def prep_B(inputs, a_results):
    wo_img = img_lhsT(_f32(inputs["L0_attn_Wo"]).T)
    bo_img = img_col(inputs["L0_attn_bo"])
    kTb_list, vb_list = [], []
    for b in range(NB):
        ks = np.concatenate([np.asarray(a_results[b * 4 + i]["kT_o"]) for i in range(4)], axis=1)
        kTb_list.append(np.ascontiguousarray(ks.reshape(H16, 128, SEQ)))
        vs_nat = np.concatenate([np.asarray(a_results[b * 4 + i]["v_o"]) for i in range(4)], axis=0)
        vimg = vs_nat.reshape(32, 128, H16, 128).transpose(2, 0, 1, 3)
        vb_list.append(np.ascontiguousarray(vimg))
    X = _f32(inputs["L0_attn_X"])
    xT = _f32(X.reshape(NCORES * 1024, D).T)
    gw_img = _f32(inputs["L0_ffn_gate_w"]).T.reshape(DT, 128, 8)
    wg_img = np.stack([_f32(inputs["L0_ffn_w_gate"][e]).reshape(DT, 128, 11, 128).transpose(2, 0, 1, 3)
                       for e in range(8)])
    wu_img = np.stack([_f32(inputs["L0_ffn_w_up"][e]).reshape(DT, 128, 11, 128).transpose(2, 0, 1, 3)
                       for e in range(8)])
    wd_img = np.stack([_f32(inputs["L0_ffn_w_down"][e]).reshape(11, 128, 8, 256).transpose(2, 0, 1, 3)
                       for e in range(8)])
    shg_img = img_col(inputs["L0_ffn_shgate_w"])
    shwg_img = _f32(inputs["L0_ffn_shw_gate"]).reshape(DT, 128, 22, 128).transpose(2, 0, 1, 3)
    shwu_img = _f32(inputs["L0_ffn_shw_up"]).reshape(DT, 128, 22, 128).transpose(2, 0, 1, 3)
    shwd_img = _f32(inputs["L0_ffn_shw_down"]).reshape(22, 128, 8, 256).transpose(2, 0, 1, 3)
    in_maps = []
    for c in range(NCORES):
        in_maps.append({
            "qT_i": np.asarray(a_results[c]["qT_o"]),
            "kTb": kTb_list[c // 4], "vb": vb_list[c // 4],
            "wo": wo_img, "bo": bo_img,
            "xn_i": np.asarray(a_results[c]["xn_o"]),
            "x0T": _f32(xT[:, c * 1024:(c + 1) * 1024]),
            "gw": _f32(gw_img), "wg": _f32(wg_img), "wu": _f32(wu_img), "wd": _f32(wd_img),
            "shg": shg_img, "shwg": _f32(shwg_img), "shwu": _f32(shwu_img), "shwd": _f32(shwd_img),
        })
    return in_maps





def build_launch_C():
    TOK = 1024
    NCH = TOK // 512
    nc = bass.Bass("TRN2", target_bir_lowering=False)
    x1T = nc.dram_tensor("x1T", [D, TOK], F32, kind="ExternalInput").ap()
    wq = nc.dram_tensor("wq", [8, DT, 128, 256], F32, kind="ExternalInput").ap()
    wk = nc.dram_tensor("wk", [8, DT, 128, 256], F32, kind="ExternalInput").ap()
    wv = nc.dram_tensor("wv", [4, DT, 128, 512], F32, kind="ExternalInput").ap()
    wb = nc.dram_tensor("wb", [DT, 128, 16], F32, kind="ExternalInput").ap()
    wg1 = nc.dram_tensor("wg1", [DT, 128, 128], F32, kind="ExternalInput").ap()
    wg2 = nc.dram_tensor("wg2", [128, D], F32, kind="ExternalInput").ap()
    qT_o = nc.dram_tensor("qT_o", [D, TOK], BF16, kind="ExternalOutput").ap()
    kT_o = nc.dram_tensor("kT_o", [D, TOK], BF16, kind="ExternalOutput").ap()
    kn_o = nc.dram_tensor("kn_o", [TOK, D], BF16, kind="ExternalOutput").ap()
    v_o = nc.dram_tensor("v_o", [TOK, D], BF16, kind="ExternalOutput").ap()
    beta_o = nc.dram_tensor("beta_o", [TOK, 16], F32, kind="ExternalOutput").ap()
    gate_o = nc.dram_tensor("gate_o", [D, TOK], BF16, kind="ExternalOutput").ap()
    xn1_o = nc.dram_tensor("xn1_o", [D, TOK], BF16, kind="ExternalOutput").ap()

    ones_col_c = nc.inline_tensor(np.ones((128, 1), BF), "ones_col_c").ap()
    ones_row_c = nc.inline_tensor(np.ones((1, 128), np.float32), "ones_row_c").ap()
    ident_c = nc.inline_tensor(np.eye(128, dtype=BF), "ident_c").ap()

    ctx = ExitStack()
    sb = lambda name, shp, dt: ctx.enter_context(nc.sbuf_tensor(name, shp, dt))
    psum = lambda name, shp, dt=F32: ctx.enter_context(nc.psum_tensor(name, shp, dt))

    ones_col = sb("ones_col", [128, 1], BF16)
    ones_colf = sb("ones_colf", [128, 1], F32)
    ones_row = sb("ones_row", [1, 128], F32)
    ident = sb("ident", [128, 128], BF16)
    eps_sb = sb("eps_sb", [128, 2], F32)
    xn = sb("xn", [128, DT * TOK], BF16)
    r_f = sb("r_f", [128, TOK], F32)
    mr_f = sb("mr_f", [128, TOK], F32)
    rowbuf = sb("rowbuf", [1, 6 * TOK], F32)
    rows = {n: rowbuf[:, i * TOK:(i + 1) * TOK]
            for i, n in enumerate(("m_row", "e2_row", "m2_row", "var_row", "r_row", "mr_row"))}
    rows["std_row"] = rows["m2_row"]
    out_sb = sb("out_sb", [128, DT * TOK], BF16)
    knT = sb("knT", [128, DT * TOK], BF16)
    wb_sb = sb("wb_sb", [128, DT * 16], BF16)
    wg1_sb = sb("wg1_sb", [128, DT * 128], BF16)
    wg2_sb = sb("wg2_sb", [128, D], BF16)
    t1_sb = sb("t1_sb", [128, TOK], BF16)
    beta_sb = sb("beta_sb", [128, 8 * 16], F32)

    ctx_ln = ExitStack()
    x_bf = ctx_ln.enter_context(nc.sbuf_tensor("x_bf", [128, DT * TOK], BF16))
    sqb = [ctx_ln.enter_context(nc.sbuf_tensor(f"sq{i}", [128, TOK], BF16)) for i in range(2)]
    tmpx = [ctx_ln.enter_context(nc.sbuf_tensor(f"tmpx{i}", [128, TOK], F32)) for i in range(2)]

    ctx_ps = ExitStack()
    psl = lambda name, shp: ctx_ps.enter_context(nc.psum_tensor(name, shp, F32))
    ps_sx = [psl(f"ps_sx{i}", [1, 512]) for i in range(2)]
    ps_sq = [psl(f"ps_sq{i}", [1, 512]) for i in range(2)]
    ps_r = psl("ps_r", [128, TOK])
    ps_mr = psl("ps_mr", [128, TOK])

    S = Sems(nc, ctx, ["ld", "wl0", "wl1", "ts", "vs", "ss", "st"])
    g, t, v, s, sy = nc.gpsimd, nc.tensor, nc.vector, nc.scalar, nc.sync

    def vsync(ins):
        v.wait_ge(S["vs"], S.inc("vs", ins))

    with nc.Block():
        g.memset(eps_sb[:, 0:1], EPS_LN)
        g.memset(eps_sb[:, 1:2], EPS_L1)
        g.memset(ones_colf[:], 1.0)
        S.inc("ld", g.dma_start(out=ones_col[:], in_=ones_col_c), 16)
        S.inc("ld", g.dma_start(out=ones_row[:], in_=ones_row_c), 16)
        S.inc("ld", g.dma_start(out=ident[:], in_=ident_c), 16)
        S.inc("ld", g.dma_start(out=x_bf[:].rearrange("p (t n) -> p t n", t=DT),
                                in_=x1T.rearrange("(t p) n -> p t n", p=128)), 16)
        S.inc("ld", g.dma_start(out=wb_sb[:].rearrange("p (t n) -> p t n", t=DT),
                                in_=wb.rearrange("t p n -> p t n")), 16)
        S.inc("ld", g.dma_start(out=wg1_sb[:].rearrange("p (t n) -> p t n", t=DT),
                                in_=wg1.rearrange("t p n -> p t n")), 16)
        S.inc("ld", g.dma_start(out=wg2_sb[:], in_=wg2), 16)
        lds0 = S.val("ld")

        t.wait_ge(S["ld"], lds0)
        for nch in range(NCH):
            for ct in range(DT):
                mm = t.matmul(ps_sx[nch][:], ones_col[:],
                              x_bf[:, ct * TOK + nch * 512: ct * TOK + (nch + 1) * 512],
                              start=(ct == 0), stop=(ct == DT - 1))
        S.inc("ts", mm)
        sq_done = {}
        s.wait_ge(S["ld"], lds0)
        for ct in range(DT):
            if ct >= 2:
                s.wait_ge(S["ts"], sq_done[ct - 2])
            ins = s.activation(sqb[ct % 2][:], x_bf[:, ct * TOK:(ct + 1) * TOK], AF.Square)
            ssv = S.inc("ss", ins)
            t.wait_ge(S["ss"], ssv)
            for nch in range(NCH):
                mm = t.matmul(ps_sq[nch][:], ones_col[:],
                              sqb[ct % 2][:, nch * 512:(nch + 1) * 512],
                              start=(ct == 0), stop=(ct == DT - 1))
            sq_done[ct] = S.inc("ts", mm)
        ts_stats = S.val("ts")
        v.wait_ge(S["ts"], ts_stats)
        for nch in range(NCH):
            sl = slice(nch * 512, (nch + 1) * 512)
            v.tensor_scalar_mul(rows["m_row"][:, sl], ps_sx[nch][:], 1.0 / D)
            ins = v.tensor_scalar_mul(rows["e2_row"][:, sl], ps_sq[nch][:], 1.0 / D)
        vsync(ins)
        vsync(v.tensor_mul(rows["m2_row"][:], rows["m_row"][:], rows["m_row"][:]))
        ins = v.tensor_sub(rows["var_row"][:], rows["e2_row"][:], rows["m2_row"][:])
        S.inc("vs", ins)
        s.wait_ge(S["vs"], S.val("vs"))
        ins = s.activation(rows["std_row"][:], rows["var_row"][:], AF.Sqrt, bias=eps_sb[0:1, 0:1])
        S.inc("ss", ins)
        v.wait_ge(S["ss"], S.val("ss"))
        vsync(v.reciprocal(rows["r_row"][:], rows["std_row"][:]))
        ins = v.tensor_mul(rows["mr_row"][:], rows["m_row"][:], rows["r_row"][:])
        S.inc("vs", ins)
        t.wait_ge(S["vs"], S.val("vs"))
        for nch in range(NCH):
            sl = slice(nch * 512, (nch + 1) * 512)
            t.matmul(ps_r[:, sl], ones_row[:], rows["r_row"][:, sl], start=True, stop=True)
            mm = t.matmul(ps_mr[:, sl], ones_row[:], rows["mr_row"][:, sl], start=True, stop=True)
        S.inc("ts", mm)
        v.wait_ge(S["ts"], S.val("ts"))
        v.tensor_copy(r_f[:], ps_r[:])
        vsync(v.tensor_copy(mr_f[:], ps_mr[:]))
        for ct in range(DT):
            vsync(v.tensor_mul(tmpx[ct % 2][:], x_bf[:, ct * TOK:(ct + 1) * TOK], r_f[:]))
            ins = v.tensor_sub(xn[:, ct * TOK:(ct + 1) * TOK], tmpx[ct % 2][:], mr_f[:])
            S.inc("vs", ins)
        vs_xn = S.val("vs")
        sy.wait_ge(S["vs"], vs_xn)
        S.inc("st", sy.dma_start(out=xn1_o.rearrange("(t p) n -> p t n", p=128),
                                 in_=xn[:].rearrange("p (t n) -> p t n", t=DT)), 16)

        ctx_ln.close()
        ctx_ps.close()
        nat_sb = sb("nat_sb", [128, 8 * D], BF16)
        sqk = sb("sqk", [128, TOK], F32)
        bc_f = sb("bc_f", [128, TOK], F32)
        pb = [psum(f"pb{i}", [128, 512]) for i in range(4)]
        ps_t1 = psum("ps_t1", [128, TOK])
        ctx_pbt = ExitStack()
        ps_bt = ctx_pbt.enter_context(nc.psum_tensor("ps_bt", [128, 16], F32))

        t.wait_ge(S["vs"], vs_xn)
        bdone = {}
        for tt in range(8):
            if tt >= 1:
                t.wait_ge(S["vs"], bdone[tt - 1])
            for ct in range(DT):
                mm = t.matmul(ps_bt[:], xn[:, ct * TOK + tt * 128: ct * TOK + (tt + 1) * 128],
                              wb_sb[:, ct * 16:(ct + 1) * 16],
                              start=(ct == 0), stop=(ct == DT - 1))
            tv = S.inc("ts", mm)
            s.wait_ge(S["ts"], tv)
            ins = s.activation(beta_sb[:, tt * 16:(tt + 1) * 16], ps_bt[:], AF.Sigmoid)
            sv = S.inc("ss", ins)
            v.wait_ge(S["ss"], sv)
            ins = v.tensor_scalar_mul(beta_sb[:, tt * 16:(tt + 1) * 16],
                                      beta_sb[:, tt * 16:(tt + 1) * 16], 2.0)
            bdone[tt] = S.inc("vs", ins)
            v.wait_ge(S["vs"], bdone[tt])
        sy.wait_ge(S["vs"], S.val("vs"))
        S.inc("st", sy.dma_start(out=beta_o.rearrange("(t p) n -> p t n", p=128),
                                 in_=beta_sb[:].rearrange("p (t n) -> p t n", t=8)), 16)

        ctx_pbt.close()
        ctx_nrm = ExitStack()
        ps_nrm = [ctx_nrm.enter_context(nc.psum_tensor(f"ps_nrm{i}", [1, 512], F32)) for i in range(2)]

        gdone = {}
        for nch in range(NCH):
            for ct in range(DT):
                mm = t.matmul(ps_t1[:, nch * 512:(nch + 1) * 512],
                              wg1_sb[:, ct * 128:(ct + 1) * 128],
                              xn[:, ct * TOK + nch * 512: ct * TOK + (nch + 1) * 512],
                              start=(ct == 0), stop=(ct == DT - 1))
        tv = S.inc("ts", mm)
        v.wait_ge(S["ts"], tv)
        vsync(v.tensor_copy(t1_sb[:], ps_t1[:]))
        vs_t1 = S.val("vs")
        t.wait_ge(S["vs"], vs_t1)
        ngt = 0
        for mt in range(DT):
            for nch in range(NCH):
                bank = ngt % 4
                if ngt >= 4:
                    t.wait_ge(S["ss"], gdone[ngt - 4])
                mm = t.matmul(pb[bank][:], wg2_sb[:, mt * 128:(mt + 1) * 128],
                              t1_sb[:, nch * 512:(nch + 1) * 512], start=True, stop=True)
                tv = S.inc("ts", mm)
                s.wait_ge(S["ts"], tv)
                ins = s.activation(out_sb[:, mt * TOK + nch * 512: mt * TOK + (nch + 1) * 512],
                                   pb[bank][:], AF.Sigmoid)
                gdone[ngt] = S.inc("ss", ins)
                ngt += 1
        sy.wait_ge(S["ss"], gdone[ngt - 1])
        S.inc("st", sy.dma_start(out=gate_o.rearrange("(t p) n -> p t n", p=128),
                                 in_=out_sb[:].rearrange("p (t n) -> p t n", t=DT)), 16)
        st_gate = S.val("st")

        ctx_w = ExitStack()
        wbuf = [ctx_w.enter_context(nc.sbuf_tensor(f"wbuf{i}", [128, DT * 256], BF16)) for i in range(2)]
        wl_after = {}
        tile_mm = {}
        tile_ev = {}
        n_tile = 0
        g.wait_ge(S["vs"], vs_xn)
        for wi, wimg in enumerate((wq, wk)):
            dst = out_sb if wi == 0 else knT
            for mg in range(8):
                li = wi * 8 + mg
                if li >= 2:
                    g.wait_ge(S["ts"], tile_mm[(li - 2) * 4 + 3])
                wsem = "wl0" if li % 2 == 0 else "wl1"
                ins = g.dma_start(out=wbuf[li % 2][:].rearrange("p (t n) -> p t n", t=DT),
                                  in_=wimg[mg].rearrange("t p n -> p t n"))
                wl_after[li] = S.inc(wsem, ins, 16)
                t.wait_ge(S[wsem], wl_after[li])
                if li == 0:
                    t.wait_ge(S["ss"], S.val("ss"))
                    t.wait_ge(S["st"], st_gate)
                for m2 in range(2):
                    mt = mg * 2 + m2
                    for nch in range(NCH):
                        bank = n_tile % 4
                        if n_tile >= 4:
                            t.wait_ge(S["ss"], tile_ev[n_tile - 4])
                        for ct in range(DT):
                            mm = t.matmul(pb[bank][:],
                                          wbuf[li % 2][:, ct * 256 + m2 * 128: ct * 256 + (m2 + 1) * 128],
                                          xn[:, ct * TOK + nch * 512: ct * TOK + (nch + 1) * 512],
                                          start=(ct == 0), stop=(ct == DT - 1))
                        tile_mm[n_tile] = S.inc("ts", mm)
                        s.wait_ge(S["ts"], tile_mm[n_tile])
                        ins = s.activation(dst[:, mt * TOK + nch * 512: mt * TOK + (nch + 1) * 512],
                                           pb[bank][:], AF.Silu)
                        tile_ev[n_tile] = S.inc("ss", ins)
                        n_tile += 1
            if wi == 0:
                sy.wait_ge(S["ss"], tile_ev[n_tile - 1])
                S.inc("st", sy.dma_start(out=qT_o.rearrange("(t p) n -> p t n", p=128),
                                         in_=out_sb[:].rearrange("p (t n) -> p t n", t=DT)), 16)
        ss_k = tile_ev[n_tile - 1]

        kn_done = {}
        for hb in range(H16):
            s.wait_ge(S["ss"], ss_k)
            if hb > 0:
                s.wait_ge(S["vs"], kn_done[hb - 1])
            ins = s.activation(sqk[:], knT[:, hb * TOK:(hb + 1) * TOK], AF.Square)
            sv = S.inc("ss", ins)
            t.wait_ge(S["ss"], sv)
            for nch in range(NCH):
                mm = t.matmul(ps_nrm[nch][:], ones_colf[:],
                              sqk[:, nch * 512:(nch + 1) * 512], start=True, stop=True)
            tv = S.inc("ts", mm)
            s.wait_ge(S["ts"], tv)
            for nch in range(NCH):
                ins = s.activation(rows["m_row"][:, nch * 512:(nch + 1) * 512],
                                   ps_nrm[nch][:], AF.Sqrt, bias=eps_sb[0:1, 1:2])
            sv = S.inc("ss", ins)
            v.wait_ge(S["ss"], sv)
            vsync(v.reciprocal(rows["e2_row"][:], rows["m_row"][:]))
            vs_rn = S.val("vs")
            t.wait_ge(S["vs"], vs_rn)
            for nch in range(NCH):
                mm = t.matmul(ps_t1[:, nch * 512:(nch + 1) * 512], ones_row[:],
                              rows["e2_row"][:, nch * 512:(nch + 1) * 512], start=True, stop=True)
            tv = S.inc("ts", mm)
            v.wait_ge(S["ts"], tv)
            vsync(v.tensor_copy(bc_f[:], ps_t1[:]))
            ins = v.tensor_mul(knT[:, hb * TOK:(hb + 1) * TOK],
                               knT[:, hb * TOK:(hb + 1) * TOK], bc_f[:])
            kn_done[hb] = S.inc("vs", ins)
            v.wait_ge(S["vs"], kn_done[hb])
        sy.wait_ge(S["vs"], S.val("vs"))
        S.inc("st", sy.dma_start(out=kT_o.rearrange("(t p) n -> p t n", p=128),
                                 in_=knT[:].rearrange("p (t n) -> p t n", t=DT)), 16)

        ctx_nrm.close()
        ps_tr = [ctx.enter_context(nc.psum_tensor(f"ps_tr{i}", [128, 1024], BF16)) for i in range(2)]
        tr_ev = {}
        ntr = 0
        for tt in range(8):
            for hb in range(H16):
                bank = ntr % 2
                if ntr >= 2:
                    t.wait_ge(S["vs"], tr_ev[ntr - 2])
                mm = t.transpose(ps_tr[bank][:, 0:128],
                                 knT[:, hb * TOK + tt * 128: hb * TOK + (tt + 1) * 128],
                                 ident[:])
                tv = S.inc("ts", mm)
                v.wait_ge(S["ts"], tv)
                ins = v.tensor_copy(nat_sb[:, tt * D + hb * 128: tt * D + (hb + 1) * 128],
                                    ps_tr[bank][:, 0:128])
                tr_ev[ntr] = S.inc("vs", ins)
                ntr += 1
        sy.wait_ge(S["vs"], S.val("vs"))
        ins = sy.dma_start(out=kn_o.rearrange("(t p) n -> p t n", p=128),
                           in_=nat_sb[:].rearrange("p (t n) -> p t n", t=8))
        st_kn = S.inc("st", ins, 16)

        ctx_w.close()
        ctx_w2 = ExitStack()
        wvbuf = [ctx_w2.enter_context(nc.sbuf_tensor("wvbuf0", [128, DT * 512], BF16))] * 2
        v_mm = {}
        v_ev = {}
        n_vt = 0
        g.wait_ge(S["ts"], S.val("ts"))
        for dn in range(4):
            if dn >= 1:
                g.wait_ge(S["ts"], v_mm[(dn - 1) * 8 + 7])
            ins = g.dma_start(out=wvbuf[dn % 2][:].rearrange("p (t n) -> p t n", t=DT),
                              in_=wv[dn].rearrange("t p n -> p t n"))
            wlv = S.inc("wl0", ins, 16)
            t.wait_ge(S["wl0"], wlv)
            if dn == 0:
                t.wait_ge(S["vs"], S.val("vs"))
                t.wait_ge(S["st"], st_kn)
            for tt in range(8):
                bank = n_vt % 4
                if n_vt >= 4:
                    t.wait_ge(S["vs"], v_ev[n_vt - 4])
                for ct in range(DT):
                    mm = t.matmul(pb[bank][:],
                                  xn[:, ct * TOK + tt * 128: ct * TOK + (tt + 1) * 128],
                                  wvbuf[dn % 2][:, ct * 512:(ct + 1) * 512],
                                  start=(ct == 0), stop=(ct == DT - 1))
                v_mm[n_vt] = S.inc("ts", mm)
                v.wait_ge(S["ts"], v_mm[n_vt])
                ins = v.tensor_copy(nat_sb[:, tt * D + dn * 512: tt * D + (dn + 1) * 512],
                                    pb[bank][:])
                v_ev[n_vt] = S.inc("vs", ins)
                n_vt += 1
        sy.wait_ge(S["vs"], v_ev[n_vt - 1])
        S.inc("st", sy.dma_start(out=v_o.rearrange("(t p) n -> p t n", p=128),
                                 in_=nat_sb[:].rearrange("p (t n) -> p t n", t=8)), 16)
        sy.wait_ge(S["st"], S.val("st"))

    ctx_w2.close()
    ctx.close()
    return nc


def prep_C(inputs, b_results):
    wq_img = img_lhsT(_f32(inputs["L1_attn_Wq"]).T)
    wk_img = img_lhsT(_f32(inputs["L1_attn_Wk"]).T)
    wv_img = img_rhs(_f32(inputs["L1_attn_Wv"]).T)
    wb_img = _f32(inputs["L1_attn_Wbeta"]).T.reshape(DT, 128, 16)
    wg1_img = _f32(inputs["L1_attn_Wg1"]).T.reshape(DT, 128, 128)
    wg2_img = _f32(inputs["L1_attn_Wg2"]).T
    in_maps = []
    for c in range(NCORES):
        in_maps.append({
            "x1T": np.asarray(b_results[c]["x1T_o"]),
            "wq": wq_img, "wk": wk_img, "wv": wv_img,
            "wb": _f32(wb_img), "wg1": _f32(wg1_img), "wg2": _f32(wg2_img),
        })
    return in_maps





def build_launch_D():
    NBH = 4
    NCK = 32
    nc = bass.Bass("TRN2", target_bir_lowering=False)
    qT_i = nc.dram_tensor("qT_i", [NBH, 128, SEQ], BF16, kind="ExternalInput").ap()
    kT_i = nc.dram_tensor("kT_i", [NBH, 128, SEQ], BF16, kind="ExternalInput").ap()
    kn_i = nc.dram_tensor("kn_i", [NBH, NCK, 128, 128], BF16, kind="ExternalInput").ap()
    vn_i = nc.dram_tensor("vn_i", [NBH, NCK, 128, 128], BF16, kind="ExternalInput").ap()
    beta_i = nc.dram_tensor("beta_i", [NBH, 128, NCK], F32, kind="ExternalInput").ap()
    o_o = nc.dram_tensor("o_o", [NBH, NCK, 128, 128], BF16, kind="ExternalOutput").ap()

    lmask_c = nc.inline_tensor(np.tril(np.ones((128, 128), np.float32), -1).astype(BF), "lmask_c").ap()
    umaski_c = nc.inline_tensor(np.triu(np.ones((128, 128), np.float32), 0).astype(BF), "umaski_c").ap()
    _bd = np.zeros((128, 128), np.float32); _bd[:64, :64] = 1; _bd[64:, 64:] = 1
    bd64_c = nc.inline_tensor(_bd.astype(BF), "bd64_c").ap()
    identf_c = nc.inline_tensor(np.eye(128, dtype=np.float32), "identf_c").ap()

    ctx = ExitStack()
    sb = lambda name, shp, dt: ctx.enter_context(nc.sbuf_tensor(name, shp, dt))

    lmask = sb("lmask", [128, 128], BF16)
    umaski = sb("umaski", [128, 128], BF16)
    bd64 = sb("bd64", [128, 128], BF16)
    identf = sb("identf", [128, 128], F32)
    qT = sb("qT", [128, NBH * SEQ], BF16)
    kT = sb("kT", [128, NBH * SEQ], BF16)
    kn = sb("kn", [128, NBH * NCK * 128], BF16)
    vn = sb("vn", [128, NBH * NCK * 128], BF16)
    beta = sb("beta", [128, NBH * NCK], F32)
    S_sb = sb("S_sb", [128, NBH * 128], BF16)
    S_f32 = sb("S_f32", [128, NBH * 128], F32)
    t0f = sb("t0f", [128, 128], F32)
    rhs0 = sb("rhs0", [128, 128], F32)
    A_f = sb("A_f", [128, 128], F32)
    At_f = sb("At_f", [128, 128], F32)
    Db_f = sb("Db_f", [128, 128], F32)
    Pj = [sb(f"Pj{i}", [128, 128], F32) for i in range(2)]
    Ptld = [sb(f"Ptld{i}", [128, 128], F32) for i in range(6)]
    tmpu = [sb(f"tmpu{i}", [128, 128], F32) for i in range(2)]
    wfull = sb("wfull", [128, 128], F32)
    U_bf = sb("U_bf", [128, 128], BF16)
    lqk = sb("lqk", [128, 128], BF16)
    o_bf = [sb(f"o_bf{i}", [128, 128], BF16) for i in range(2)]

    psum = lambda name, shp, dt=F32: ctx.enter_context(nc.psum_tensor(name, shp, dt))
    ps_S = psum("ps_S", [128, 512])
    ps_w = [psum(f"ps_w{i}", [128, 512]) for i in range(2)]
    ps_ob = [psum(f"ps_ob{i}", [128, 512]) for i in range(2)]
    ps_trf = psum("ps_trf", [128, 512], F32)

    S = Sems(nc, ctx, ["ld", "ts", "vs", "ss", "st", "sx"])
    g, t, v, s, sy = nc.gpsimd, nc.tensor, nc.vector, nc.scalar, nc.sync

    def vsync(ins):
        v.wait_ge(S["vs"], S.inc("vs", ins))

    def tsy(mm):
        tv = S.inc("ts", mm)
        return tv

    with nc.Block():
        S.inc("ld", g.dma_start(out=lmask[:], in_=lmask_c), 16)
        S.inc("ld", g.dma_start(out=umaski[:], in_=umaski_c), 16)
        S.inc("ld", g.dma_start(out=bd64[:], in_=bd64_c), 16)
        S.inc("ld", g.dma_start(out=identf[:], in_=identf_c), 16)
        S.inc("ld", g.dma_start(out=qT[:].rearrange("p (b n) -> p b n", b=NBH),
                                in_=qT_i.rearrange("b p n -> p b n")), 16)
        S.inc("ld", g.dma_start(out=kT[:].rearrange("p (b n) -> p b n", b=NBH),
                                in_=kT_i.rearrange("b p n -> p b n")), 16)
        for bh_ in range(NBH):
            S.inc("ld", g.dma_start(
                out=kn[:, bh_ * NCK * 128:(bh_ + 1) * NCK * 128].rearrange("p (c n) -> p c n", c=NCK),
                in_=kn_i[bh_].rearrange("c p n -> p c n")), 16)
            S.inc("ld", g.dma_start(
                out=vn[:, bh_ * NCK * 128:(bh_ + 1) * NCK * 128].rearrange("p (c n) -> p c n", c=NCK),
                in_=vn_i[bh_].rearrange("c p n -> p c n")), 16)
        S.inc("ld", g.dma_start(out=beta[:].rearrange("p (b n) -> p b n", b=NBH),
                                in_=beta_i.rearrange("b p n -> p b n")), 16)
        lds0 = S.val("ld")
        t.wait_ge(S["ld"], lds0)
        v.wait_ge(S["ld"], lds0)
        s.wait_ge(S["ld"], lds0)
        vsync(v.memset(S_sb[:], 0.0))
        vsync(v.memset(S_f32[:], 0.0))

        o_ev = {}
        o_st = {}
        for bh in range(NBH):
            Ssl = S_sb[:, bh * 128:(bh + 1) * 128]
            pSsl = ps_S[:, bh * 128:(bh + 1) * 128]
            for ck in range(NCK):
                kTc = kT[:, bh * SEQ + ck * 128: bh * SEQ + (ck + 1) * 128]
                qTc = qT[:, bh * SEQ + ck * 128: bh * SEQ + (ck + 1) * 128]
                knc = kn[:, (bh * NCK + ck) * 128: (bh * NCK + ck + 1) * 128]
                vnc = vn[:, (bh * NCK + ck) * 128: (bh * NCK + ck + 1) * 128]
                bcol = beta[:, bh * NCK + ck: bh * NCK + ck + 1]
                t.wait_ge(S["vs"], S.val("vs"))
                tv = tsy(t.matmul(ps_w[0][:, 0:128], kTc, Ssl, start=True, stop=True))
                v.wait_ge(S["ts"], tv)
                v.wait_ge(S["ss"], S.val("ss"))
                ins = v.tensor_sub(t0f[:], vnc, ps_w[0][:, 0:128])
                sv = S.inc("vs", ins)
                s.wait_ge(S["vs"], sv)
                ins = s.mul(rhs0[:], t0f[:], bcol)
                rhs0_v = S.inc("ss", ins)
                tv = tsy(t.matmul(ps_w[1][:, 0:128], kTc, kTc, start=True, stop=True))
                v.wait_ge(S["ts"], tv)
                v.wait_ge(S["ss"], rhs0_v)
                ins = v.tensor_mul(t0f[:], ps_w[1][:, 0:128], lmask[:])
                sv = S.inc("vs", ins)
                s.wait_ge(S["vs"], sv)
                ins = s.mul(A_f[:], t0f[:], bcol)
                A_v = S.inc("ss", ins)
                t.wait_ge(S["ss"], A_v)
                tv = tsy(t.transpose(ps_trf[:, 0:128], A_f[:], identf[:]))
                v.wait_ge(S["ts"], tv)
                ins = v.tensor_copy(At_f[:], ps_trf[:, 0:128])
                vsync(ins)
                ins = v.tensor_mul(Db_f[:], A_f[:], bd64[:])
                vsync(ins)
                ins = v.tensor_mul(Ptld[0][:], At_f[:], bd64[:])
                vsync(ins)
                pj_cur = Db_f
                for j in range(5):
                    t.wait_ge(S["vs"], S.val("vs"))
                    tsy(t.matmul(ps_w[0][:, 128:256], Ptld[j][:], pj_cur[:], start=True, stop=True))
                    tv2 = tsy(t.matmul(ps_w[1][:, 128:256], pj_cur[:], Ptld[j][:], start=True, stop=True))
                    v.wait_ge(S["ts"], tv2)
                    ins = v.tensor_copy(Pj[(j + 1) % 2][:], ps_w[0][:, 128:256])
                    vsync(ins)
                    ins = v.tensor_copy(Ptld[j + 1][:], ps_w[1][:, 128:256])
                    vsync(ins)
                    pj_cur = Pj[(j + 1) % 2]

                def applyinv(dst_rows, src_ap, rows):
                    tcur = src_ap
                    for idx, j in enumerate([5, 4, 3, 2, 1, 0]):
                        t.wait_ge(S["vs"], S.val("vs"))
                        t.wait_ge(S["ss"], S.val("ss"))
                        pw = ps_w[idx % 2]
                        tv = tsy(t.matmul(pw[rows, 256:384], Ptld[j][rows, rows],
                                          tcur, start=True, stop=True))
                        v.wait_ge(S["ts"], tv)
                        tnew = tmpu[idx % 2][rows, :]
                        if j != 0:
                            ins = v.tensor_add(tnew, tcur, pw[rows, 256:384])
                        else:
                            ins = v.tensor_sub(tnew, tcur, pw[rows, 256:384])
                        vsync(ins)
                        tcur = tnew
                    return tcur

                full = slice(0, 128)
                s.wait_ge(S["ss"], rhs0_v)
                wend = applyinv(None, rhs0[:], full)
                v.wait_ge(S["vs"], S.val("vs"))
                ins = v.tensor_copy(wfull[:], wend)
                vsync(ins)
                t.wait_ge(S["vs"], S.val("vs"))
                tv = tsy(t.matmul(ps_w[1][64:128, 256:384], At_f[0:64, 64:128],
                                  wfull[0:64, :], start=True, stop=True))
                v.wait_ge(S["ts"], tv)
                bot = slice(64, 128)
                ins = v.tensor_sub(t0f[bot, :], rhs0[bot, :], ps_w[1][64:128, 256:384])
                vsync(ins)
                w2end = applyinv(None, t0f[bot, :], bot)
                v.wait_ge(S["vs"], S.val("vs"))
                ins = v.tensor_copy(U_bf[0:64, :], wfull[0:64, :])
                vsync(ins)
                ins = v.tensor_copy(U_bf[bot, :], w2end)
                vsync(ins)
                t.wait_ge(S["vs"], S.val("vs"))
                tv = tsy(t.matmul(ps_w[0][:, 384:512], kTc, qTc, start=True, stop=True))
                v.wait_ge(S["ts"], tv)
                ins = v.tensor_mul(lqk[:], ps_w[0][:, 384:512], umaski[:])
                lqk_v = S.inc("vs", ins)
                t.wait_ge(S["vs"], lqk_v)
                ob = ps_ob[ck % 2]
                if ck >= 2:
                    t.wait_ge(S["vs"], o_ev[(bh, ck - 2)])
                t.matmul(ob[:, 0:128], qTc, Ssl, start=True, stop=False)
                tv = tsy(t.matmul(ob[:, 0:128], lqk[:], U_bf[:], start=False, stop=True))
                v.wait_ge(S["ts"], tv)
                prev = (bh, ck - 2) if ck >= 2 else ((bh - 1, NCK - 2 + ck) if bh else None)
                if prev in o_st:
                    v.wait_ge(S["st" if prev[1] % 2 == 0 else "sx"], o_st[prev])
                ins = v.tensor_copy(o_bf[ck % 2][:], ob[:, 0:128])
                o_ev[(bh, ck)] = S.inc("vs", ins)
                sy.wait_ge(S["vs"], o_ev[(bh, ck)])
                ssem = "st" if ck % 2 == 0 else "sx"
                ins = sy.dma_start(out=o_o[bh, ck], in_=o_bf[ck % 2][:])
                o_st[(bh, ck)] = S.inc(ssem, ins, 16)
                if ck + 1 < NCK:
                    fsl = S_f32[:, bh * 128:(bh + 1) * 128]
                    tv = tsy(t.matmul(pSsl, knc, U_bf[:], start=True, stop=True))
                    v.wait_ge(S["ts"], tv)
                    vsync(v.tensor_add(fsl, fsl, pSsl))
                    vsync(v.tensor_copy(Ssl, fsl))
        sy.wait_ge(S["st"], S.val("st"))
        sy.wait_ge(S["sx"], S.val("sx"))

    ctx.close()
    return nc


def prep_D(inputs, c_results):
    qT_full, kT_full, kn_full, vn_full, beta_full = [], [], [], [], []
    for b in range(NB):
        qT_full.append(np.concatenate([np.asarray(c_results[b * 4 + i]["qT_o"]) for i in range(4)], axis=1))
        kT_full.append(np.concatenate([np.asarray(c_results[b * 4 + i]["kT_o"]) for i in range(4)], axis=1))
        kn_full.append(np.concatenate([np.asarray(c_results[b * 4 + i]["kn_o"]) for i in range(4)], axis=0))
        vn_full.append(np.concatenate([np.asarray(c_results[b * 4 + i]["v_o"]) for i in range(4)], axis=0))
        beta_full.append(np.concatenate([np.asarray(c_results[b * 4 + i]["beta_o"]) for i in range(4)], axis=0))
    in_maps = []
    for c in range(NCORES):
        bhs = [(b, h) for b in range(NB) for h in (2 * c, 2 * c + 1)]
        qT = np.stack([qT_full[b][h * 128:(h + 1) * 128, :] for (b, h) in bhs])
        kT = np.stack([kT_full[b][h * 128:(h + 1) * 128, :] for (b, h) in bhs])
        kn = np.stack([kn_full[b][:, h * 128:(h + 1) * 128].reshape(32, 128, 128) for (b, h) in bhs])
        vn = np.stack([vn_full[b][:, h * 128:(h + 1) * 128].reshape(32, 128, 128) for (b, h) in bhs])
        bt = np.stack([_f32(beta_full[b][:, h].reshape(32, 128).T) for (b, h) in bhs])
        in_maps.append({"qT_i": np.ascontiguousarray(qT), "kT_i": np.ascontiguousarray(kT),
                        "kn_i": np.ascontiguousarray(kn), "vn_i": np.ascontiguousarray(vn),
                        "beta_i": bt})
    return in_maps





def build_launch_W():
    CW = 256
    nc = bass.Bass("TRN2", target_bir_lowering=False)
    w2c = nc.dram_tensor("w2c", [64, 128, CW], F32, kind="ExternalInput").ap()
    w2f_o = nc.dram_tensor("w2f_o", [64, 128, CW], BF16, kind="ExternalOutput").ap()
    h128_c = nc.inline_tensor(_hadamard(128).astype(BF), "h128_c").ap()

    ctx = ExitStack()
    sb = lambda name, shp, dt: ctx.enter_context(nc.sbuf_tensor(name, shp, dt))
    h128 = sb("h128", [128, 128], BF16)
    win = sb("win", [128, 64 * CW], BF16)
    y0 = sb("y0", [128, 64 * CW], F32)
    y1 = sb("y1", [128, 64 * CW], F32)
    ps = [ctx.enter_context(nc.psum_tensor(f"psw{i}", [128, 512], F32)) for i in range(2)]
    S = Sems(nc, ctx, ["ld", "ts", "vs", "st"])
    g, t, v, sy = nc.gpsimd, nc.tensor, nc.vector, nc.sync

    with nc.Block():
        S.inc("ld", g.dma_start(out=h128[:], in_=h128_c), 16)
        S.inc("ld", g.dma_start(out=win[:].rearrange("p (t n) -> p t n", t=64),
                                in_=w2c.rearrange("t p n -> p t n")), 16)
        lds0 = S.val("ld")
        t.wait_ge(S["ld"], lds0)
        ev = {}
        for ct in range(64):
            bank = ct % 2
            if ct >= 2:
                t.wait_ge(S["vs"], ev[ct - 2])
            mm = t.matmul(ps[bank][:, 0:CW], h128[:], win[:, ct * CW:(ct + 1) * CW],
                          start=True, stop=True)
            tv = S.inc("ts", mm)
            v.wait_ge(S["ts"], tv)
            ins = v.tensor_copy(y0[:, ct * CW:(ct + 1) * CW], ps[bank][:, 0:CW])
            ev[ct] = S.inc("vs", ins)
        cur, nxt = y0, y1
        for dstep in (1, 2, 4, 8, 16, 32):
            v.wait_ge(S["vs"], S.val("vs"))
            for blk in range(0, 64, 2 * dstep):
                for a in range(blk, blk + dstep):
                    b = a + dstep
                    asl = slice(a * CW, (a + 1) * CW)
                    bsl = slice(b * CW, (b + 1) * CW)
                    v.tensor_add(nxt[:, asl], cur[:, asl], cur[:, bsl])
                    ins = v.tensor_sub(nxt[:, bsl], cur[:, asl], cur[:, bsl])
            S.inc("vs", ins)
            cur, nxt = nxt, cur
        vs_done = S.val("vs")
        sy.wait_ge(S["vs"], vs_done)
        g.wait_ge(S["vs"], vs_done)
        S.inc("st", g.dma_start(out=w2f_o.rearrange("t p n -> p t n"),
                                in_=cur[:].rearrange("p (t n) -> p t n", t=64)), 16)
        sy.wait_ge(S["st"], S.val("st"))
        g.wait_ge(S["st"], S.val("st"))
    ctx.close()
    return nc


def _hadamard(n):
    h = np.array([[1.0]], np.float32)
    while h.shape[0] < n:
        h = np.block([[h, h], [h, -h]])
    return h


def prep_W(inputs):
    w2T = _f32(inputs["L1_ffn_W2"]).T
    in_maps = []
    for c in range(NCORES):
        sl = w2T[:, c * 256:(c + 1) * 256].reshape(64, 128, 256)
        in_maps.append({"w2c": _f32(sl)})
    return in_maps



def build_launch_E():
    TOK = 1024
    NCH = TOK // 512
    nc = bass.Bass("TRN2", target_bir_lowering=False)
    x1T = nc.dram_tensor("x1T", [D, TOK], F32, kind="ExternalInput").ap()
    xn1 = nc.dram_tensor("xn1", [D, TOK], BF16, kind="ExternalInput").ap()
    oT_i = nc.dram_tensor("oT_i", [D, TOK], BF16, kind="ExternalInput").ap()
    gate_i = nc.dram_tensor("gate_i", [D, TOK], BF16, kind="ExternalInput").ap()
    gn = nc.dram_tensor("gn", [128, DT], F32, kind="ExternalInput").ap()
    wo = nc.dram_tensor("wo", [8, DT, 128, 256], F32, kind="ExternalInput").ap()
    w1 = nc.dram_tensor("w1", [32, DT, 128, 256], F32, kind="ExternalInput").ap()
    w3 = nc.dram_tensor("w3", [32, DT, 128, 256], F32, kind="ExternalInput").ap()
    w2f = nc.dram_tensor("w2f", [DT, 64, 128, 128], BF16, kind="ExternalInput").ap()
    x2T_o = nc.dram_tensor("x2T_o", [D, TOK], F32, kind="ExternalOutput").ap()

    ones_row_c = nc.inline_tensor(np.ones((1, 128), np.float32), "ones_row_c").ap()
    ctx = ExitStack()
    sb = lambda name, shp, dt: ctx.enter_context(nc.sbuf_tensor(name, shp, dt))
    psum = lambda name, shp, dt=F32: ctx.enter_context(nc.psum_tensor(name, shp, dt))

    ones_row = sb("ones_row", [1, 128], F32)
    ones_colf = sb("ones_colf", [128, 1], F32)
    eps_sb = sb("eps_sb", [128, 1], F32)
    xn_sb = sb("xn_sb", [128, DT * TOK], BF16)
    s1 = sb("s1", [128, DT * TOK], BF16)
    gn_sb = sb("gn_sb", [128, DT], F32)
    rowb = sb("rowb", [1, 2 * TOK], F32)
    bc_f = sb("bc_f", [128, TOK], F32)

    S = Sems(nc, ctx, ["ld", "wl0", "wl1", "ts", "vs", "ss", "st", "sx"])
    g, t, v, s, sy = nc.gpsimd, nc.vector and nc.tensor, nc.vector, nc.scalar, nc.sync
    t = nc.tensor

    def vsync(ins):
        v.wait_ge(S["vs"], S.inc("vs", ins))

    with nc.Block():
        ctx1 = ExitStack()
        oT = ctx1.enter_context(nc.sbuf_tensor("oT", [128, DT * TOK], BF16))
        gate = ctx1.enter_context(nc.sbuf_tensor("gate", [128, DT * TOK], BF16))
        ogn = ctx1.enter_context(nc.sbuf_tensor("ogn", [128, DT * TOK], BF16))
        tmb = ctx1.enter_context(nc.sbuf_tensor("tmb", [128, TOK], BF16))
        sqe = ctx1.enter_context(nc.sbuf_tensor("sqe", [128, TOK], F32))
        t2f = ctx1.enter_context(nc.sbuf_tensor("t2f", [128, TOK], F32))
        wbuf = [ctx1.enter_context(nc.sbuf_tensor(f"wbuf{i}", [128, DT * 256], BF16)) for i in range(2)]
        ctxp1 = ExitStack()
        ps_n = [ctxp1.enter_context(nc.psum_tensor(f"ps_n{i}", [1, 512], F32)) for i in range(2)]
        ps_bc = ctxp1.enter_context(nc.psum_tensor("ps_bc", [128, TOK], F32))
        pb = [ctxp1.enter_context(nc.psum_tensor(f"pb{i}", [128, 512], F32)) for i in range(4)]

        g.memset(eps_sb[:], EPS_L1)
        g.memset(ones_colf[:], 1.0)
        S.inc("ld", g.dma_start(out=ones_row[:], in_=ones_row_c), 16)
        S.inc("ld", g.dma_start(out=xn_sb[:].rearrange("p (t n) -> p t n", t=DT),
                                in_=xn1.rearrange("(t p) n -> p t n", p=128)), 16)
        S.inc("ld", g.dma_start(out=oT[:].rearrange("p (t n) -> p t n", t=DT),
                                in_=oT_i.rearrange("(t p) n -> p t n", p=128)), 16)
        S.inc("ld", g.dma_start(out=gate[:].rearrange("p (t n) -> p t n", t=DT),
                                in_=gate_i.rearrange("(t p) n -> p t n", p=128)), 16)
        S.inc("ld", g.dma_start(out=gn_sb[:], in_=gn), 16)
        lds0 = S.val("ld")
        v.wait_ge(S["ld"], lds0)
        s.wait_ge(S["ld"], lds0)
        t.wait_ge(S["ld"], lds0)
        ogn_done = 0
        for hb in range(H16):
            hs = slice(hb * TOK, (hb + 1) * TOK)
            if hb > 0:
                v.wait_ge(S["ss"], ogn_done)
            vsync(v.tensor_mul(tmb[:], oT[:, hs], gate[:, hs]))
            s.wait_ge(S["vs"], S.val("vs"))
            ins = s.activation(sqe[:], tmb[:], AF.Square)
            sv = S.inc("ss", ins)
            t.wait_ge(S["ss"], sv)
            for nch in range(NCH):
                mm = t.matmul(ps_n[nch][:], ones_colf[:],
                              sqe[:, nch * 512:(nch + 1) * 512], start=True, stop=True)
            tv = S.inc("ts", mm)
            s.wait_ge(S["ts"], tv)
            for nch in range(NCH):
                ins = s.activation(rowb[:, nch * 512:(nch + 1) * 512], ps_n[nch][:],
                                   AF.Sqrt, bias=eps_sb[0:1, 0:1], scale=1.0 / 128.0)
            sv = S.inc("ss", ins)
            v.wait_ge(S["ss"], sv)
            vsync(v.reciprocal(rowb[:, TOK:TOK + TOK], rowb[:, 0:TOK]))
            t.wait_ge(S["vs"], S.val("vs"))
            for nch in range(NCH):
                mm = t.matmul(ps_bc[:, nch * 512:(nch + 1) * 512], ones_row[:],
                              rowb[:, TOK + nch * 512: TOK + (nch + 1) * 512],
                              start=True, stop=True)
            tv = S.inc("ts", mm)
            v.wait_ge(S["ts"], tv)
            vsync(v.tensor_copy(bc_f[:], ps_bc[:]))
            ins = v.tensor_mul(t2f[:], tmb[:], bc_f[:])
            vsync(ins)
            s.wait_ge(S["vs"], S.val("vs"))
            ins = s.mul(ogn[:, hs], t2f[:], gn_sb[:, hb:hb + 1])
            ogn_done = S.inc("ss", ins)
        ss_ogn = S.val("ss")
        n_tile = 0
        womm = {}
        woev = {}
        t.wait_ge(S["ss"], ss_ogn)
        for mg in range(8):
            if mg >= 2:
                g.wait_ge(S["ts"], womm[(mg - 2) * 4 + 3])
            else:
                g.wait_ge(S["ld"], lds0)
            wsem = "wl0" if mg % 2 == 0 else "wl1"
            ins = g.dma_start(out=wbuf[mg % 2][:].rearrange("p (t n) -> p t n", t=DT),
                              in_=wo[mg].rearrange("t p n -> p t n"))
            wlv = S.inc(wsem, ins, 16)
            t.wait_ge(S[wsem], wlv)
            for m2 in range(2):
                mt = mg * 2 + m2
                for nch in range(NCH):
                    bank = n_tile % 4
                    if n_tile >= 4:
                        t.wait_ge(S["ss"], woev[n_tile - 4])
                    for ct in range(DT):
                        mm = t.matmul(pb[bank][:],
                                      wbuf[mg % 2][:, ct * 256 + m2 * 128: ct * 256 + (m2 + 1) * 128],
                                      ogn[:, ct * TOK + nch * 512: ct * TOK + (nch + 1) * 512],
                                      start=(ct == 0), stop=(ct == DT - 1))
                    womm[n_tile] = S.inc("ts", mm)
                    s.wait_ge(S["ts"], womm[n_tile])
                    ins = s.activation(s1[:, mt * TOK + nch * 512: mt * TOK + (nch + 1) * 512],
                                       pb[bank][:], AF.Identity)
                    woev[n_tile] = S.inc("ss", ins)
                    n_tile += 1
        ss_s1 = S.val("ss")
        g.wait_ge(S["ss"], ss_s1)
        ctxp1.close()
        ctx1.close()

        ctx2 = ExitStack()
        hT = ctx2.enter_context(nc.sbuf_tensor("hT", [128, 64 * 512], BF16))
        w1b = [ctx2.enter_context(nc.sbuf_tensor(f"w1b{i}", [128, DT * 256], BF16)) for i in range(2)]
        w3b = [ctx2.enter_context(nc.sbuf_tensor(f"w3b{i}", [128, DT * 256], BF16)) for i in range(2)]
        w2b = [ctx2.enter_context(nc.sbuf_tensor("w2b0", [128, 64 * 128], BF16))] * 2
        sgt = ctx2.enter_context(nc.sbuf_tensor("sgt", [128, 512], BF16))
        g1t = ctx2.enter_context(nc.sbuf_tensor("g1t", [128, 512], BF16))
        x1b = [ctx2.enter_context(nc.sbuf_tensor(f"x1b{i}", [128, 512], F32)) for i in range(2)]
        x2b = [ctx2.enter_context(nc.sbuf_tensor(f"x2b{i}", [128, 512], F32)) for i in range(2)]
        ctxp2 = ExitStack()
        pb2 = [ctxp2.enter_context(nc.psum_tensor(f"pb2{i}", [128, 512], F32)) for i in range(4)]

        SC = float(1.0 / np.sqrt(8192.0))
        st_mm = {}
        x2_ev = {}
        g1_used = {}
        for th in range(2):
            tsl = slice(th * 512, (th + 1) * 512)
            hmm = {}
            hev = {}
            for mg in range(32):
                li = th * 32 + mg
                wsem = "wl0" if mg % 2 == 0 else "wl1"
                if mg >= 2:
                    g.wait_ge(S["ts"], hmm[(mg - 2, 1, "u")])
                elif th == 1:
                    g.wait_ge(S["ts"], S.val("ts"))
                ins = g.dma_start(out=w1b[mg % 2][:].rearrange("p (t n) -> p t n", t=DT),
                                  in_=w1[mg].rearrange("t p n -> p t n"))
                S.inc(wsem, ins, 16)
                ins = g.dma_start(out=w3b[mg % 2][:].rearrange("p (t n) -> p t n", t=DT),
                                  in_=w3[mg].rearrange("t p n -> p t n"))
                wlv = S.inc(wsem, ins, 16)
                t.wait_ge(S[wsem], wlv)
                for m2 in range(2):
                    mt = mg * 2 + m2
                    bg, bu = (mt % 2) * 2, (mt % 2) * 2 + 1
                    if th == 1 and mg == 0:
                        t.wait_ge(S["vs"], S.val("vs"))
                    if (mg, m2) >= (1, 0):
                        prevmt = mt - 2
                        if prevmt >= 0:
                            t.wait_ge(S["ss"], hev[(prevmt, "sg")])
                            t.wait_ge(S["vs"], hev[(prevmt, "h")])
                    for ct in range(DT):
                        mm = t.matmul(pb2[bg][:], w1b[mg % 2][:, ct * 256 + m2 * 128: ct * 256 + (m2 + 1) * 128],
                                      xn_sb[:, ct * TOK + th * 512: ct * TOK + th * 512 + 512],
                                      start=(ct == 0), stop=(ct == DT - 1))
                    hmm[(mg, m2, "g")] = S.inc("ts", mm)
                    for ct in range(DT):
                        mm = t.matmul(pb2[bu][:], w3b[mg % 2][:, ct * 256 + m2 * 128: ct * 256 + (m2 + 1) * 128],
                                      xn_sb[:, ct * TOK + th * 512: ct * TOK + th * 512 + 512],
                                      start=(ct == 0), stop=(ct == DT - 1))
                    hmm[(mg, m2, "u")] = S.inc("ts", mm)
                    s.wait_ge(S["ts"], hmm[(mg, m2, "u")])
                    if (mt - 1, "h") in hev:
                        s.wait_ge(S["vs"], hev[(mt - 1, "h")])
                    ins = s.activation(sgt[:], pb2[bg][:], AF.Silu)
                    hev[(mt, "sg")] = S.inc("ss", ins)
                    v.wait_ge(S["ss"], hev[(mt, "sg")])
                    ins = v.tensor_mul(hT[:, mt * 512:(mt + 1) * 512], sgt[:], pb2[bu][:])
                    hev[(mt, "h")] = S.inc("vs", ins)
            vs_h = S.val("vs")
            x1_ld = {}
            x1_ld[0] = S.inc("ld", g.dma_start(out=x1b[0][:], in_=x1T[0:128, tsl]), 16)
            x1_ld[1] = S.inc("ld", g.dma_start(out=x1b[1][:], in_=x1T[128:256, tsl]), 16)
            st_x2 = {}
            for mt in range(DT):
                wsem = "wl0" if mt % 2 == 0 else "wl1"
                if mt >= 1:
                    g.wait_ge(S["ts"], st_mm[mt - 1])
                elif th == 1:
                    g.wait_ge(S["ts"], S.val("ts"))
                ins = g.dma_start(out=w2b[mt % 2][:].rearrange("p (t n) -> p t n", t=64),
                                  in_=w2f[mt].rearrange("t p n -> p t n"))
                wlv = S.inc(wsem, ins, 16)
                t.wait_ge(S[wsem], wlv)
                if mt == 0:
                    t.wait_ge(S["vs"], vs_h)
                bank = mt % 4
                if mt >= 4:
                    t.wait_ge(S["vs"], x2_ev[mt - 4])
                for ct in range(64):
                    mm = t.matmul(pb2[bank][:], w2b[mt % 2][:, ct * 128:(ct + 1) * 128],
                                  hT[:, ct * 512:(ct + 1) * 512],
                                  start=(ct == 0), stop=(ct == 63))
                st_mm[mt] = S.inc("ts", mm)
                s.wait_ge(S["ts"], st_mm[mt])
                if mt >= 1:
                    s.wait_ge(S["vs"], g1_used[mt - 1])
                ins = s.activation(g1t[:], pb2[bank][:], AF.Silu, scale=SC)
                gv = S.inc("ss", ins)
                v.wait_ge(S["ss"], gv)
                v.wait_ge(S["ld"], x1_ld[mt])
                ins = v.tensor_mul(g1t[:], s1[:, mt * TOK + tsl.start: mt * TOK + tsl.start + 512], g1t[:])
                g1_used[mt] = S.inc("vs", ins)
                v.wait_ge(S["vs"], g1_used[mt])
                if mt >= 2:
                    v.wait_ge(S["st" if (mt - 2) % 2 == 0 else "sx"], st_x2[mt - 2])
                ins = v.tensor_add(x2b[mt % 2][:], x1b[mt % 2][:], g1t[:])
                vv = S.inc("vs", ins)
                x2_ev[mt] = vv
                sy.wait_ge(S["vs"], vv)
                ins = sy.dma_start(out=x2T_o[mt * 128:(mt + 1) * 128, tsl], in_=x2b[mt % 2][:])
                st_x2[mt] = S.inc("st" if mt % 2 == 0 else "sx", ins, 16)
                if mt + 2 < DT:
                    g.wait_ge(S["vs"], vv)
                    x1_ld[mt + 2] = S.inc("ld", g.dma_start(
                        out=x1b[mt % 2][:], in_=x1T[(mt + 2) * 128:(mt + 3) * 128, tsl]), 16)
        sy.wait_ge(S["st"], S.val("st"))
        sy.wait_ge(S["sx"], S.val("sx"))
        ctxp2.close()
        ctx2.close()
    ctx.close()
    return nc


def prep_E(inputs, b_results, c_results, d_results, w_results):
    gn_img = img_col(inputs["L1_attn_g_norm"])
    wo_img = img_lhsT(_f32(inputs["L1_attn_Wo"]).T)
    w1_img = img_lhsT(_f32(inputs["L1_ffn_W1"]).T)
    w3_img = img_lhsT(_f32(inputs["L1_ffn_W3"]).T)
    w2f_full = np.concatenate([np.asarray(w_results[c]["w2f_o"]) for c in range(NCORES)],
                              axis=2)
    w2f_img = np.ascontiguousarray(
        w2f_full.reshape(64, 128, DT, 128).transpose(2, 0, 1, 3))
    oT_b = []
    for b in range(NB):
        ob = np.zeros((D, SEQ), dtype=np.asarray(d_results[0]["o_o"]).dtype)
        for h in range(H16):
            core = h // 2
            bi = (b * 2) + (h % 2)
            o_nat = np.asarray(d_results[core]["o_o"][bi]).reshape(SEQ, 128)
            ob[h * 128:(h + 1) * 128, :] = o_nat.T
        oT_b.append(ob)
    in_maps = []
    for c in range(NCORES):
        b = c // 4
        tok0 = (c % 4) * 1024
        in_maps.append({
            "x1T": np.asarray(b_results[c]["x1T_o"]),
            "xn1": np.asarray(c_results[c]["xn1_o"]),
            "oT_i": np.ascontiguousarray(oT_b[b][:, tok0:tok0 + 1024]),
            "gate_i": np.asarray(c_results[c]["gate_o"]),
            "gn": gn_img, "wo": wo_img, "w1": w1_img, "w3": w3_img,
            "w2f": w2f_img,
        })
    return in_maps


def kernel(**inputs):
    resW = run_bass_kernel_spmd(build_launch_W(), prep_W(inputs), core_ids=list(range(NCORES)))
    resA = run_bass_kernel_spmd(build_launch_A(), prep_A(inputs), core_ids=list(range(NCORES)))
    resB = run_bass_kernel_spmd(build_launch_B(), prep_B(inputs, resA.results), core_ids=list(range(NCORES)))
    resC = run_bass_kernel_spmd(build_launch_C(), prep_C(inputs, resB.results), core_ids=list(range(NCORES)))
    resD = run_bass_kernel_spmd(build_launch_D(), prep_D(inputs, resC.results), core_ids=list(range(NCORES)))
    resE = run_bass_kernel_spmd(build_launch_E(),
                                prep_E(inputs, resB.results, resC.results, resD.results, resW.results),
                                core_ids=list(range(NCORES)))
    x2 = np.concatenate([np.asarray(resE.results[c]["x2T_o"], np.float32).T for c in range(NCORES)], axis=0)
    return np.ascontiguousarray(x2.reshape(NB, SEQ, D))

